# version 02
# baseline (speedup 1.0000x reference)
import contextlib
import numpy as np
import concourse.bass as bass
import concourse.mybir as mybir
from concourse.bass_utils import run_bass_kernel_spmd

F32 = mybir.dt.float32
F16 = mybir.dt.float16
I32 = mybir.dt.int32
I8 = mybir.dt.int8
AF = mybir.ActivationFunctionType
ALU = mybir.AluOpType
AX = mybir.AxisListType

D = 1024
DI = 3072
H = 16
KC = D // 128
NB = DI // 128
NCORES = 8
QMAX = 32767.0

ENGS = ("sync", "scalar", "vector", "tensor", "gpsimd")


class Prog:

    def __init__(self, nc, stk, paranoid=False):
        self.nc = nc
        self.ops = {e: [] for e in ENGS}
        self.sems = {}
        self.cnt = {}
        self.stk = stk
        self.paranoid = paranoid
        self.gcnt = 0

    def sem(self, *names):
        for n in names:
            self.sems[n] = self.stk.enter_context(self.nc.semaphore(n))
            self.cnt[n] = 0

    def _ser(self, eng):
        if not self.paranoid:
            return
        g = self.sems["gseq"]
        v = self.gcnt
        self.ops[eng].append(lambda e, g=g, v=v: e.wait_ge(g, v))

    def op(self, eng, fn, inc=None):
        if eng == "vector" and not self.paranoid:
            self.wait("vector", "mV", self.cnt["mV"])
            if inc is None:
                inc = ("mV", 1)
            elif inc[0] != "mV":
                raise ValueError("vector ops must inc mV")
        self._ser(eng)
        if self.paranoid:
            g = self.sems["gseq"]
            self.gcnt += 1
            if inc is None:
                self.ops[eng].append(lambda e, fn=fn, g=g: fn(e).then_inc(g, 1))
                return None
            s, by = inc
            self.cnt[s] += by
            sh = self.sems[s]

            def run2(e, fn=fn, sh=sh, by=by, g=g):
                ins = fn(e)
                ins.then_inc(sh, by)
                e.sem_inc(g, 1)
            self.ops[eng].append(run2)
            return self.cnt[s]
        if inc is None:
            self.ops[eng].append(lambda e, fn=fn: fn(e))
            return None
        s, by = inc
        self.cnt[s] += by
        sh = self.sems[s]

        def run(e, fn=fn, sh=sh, by=by):
            fn(e).then_inc(sh, by)

        self.ops[eng].append(run)
        return self.cnt[s]

    def dma(self, eng, out, in_, sem, transpose=False, **kw):
        self._ser(eng)
        self.cnt[sem] += 16
        sh = self.sems[sem]
        if transpose:
            def run(e, out=out, in_=in_, sh=sh):
                e.dma_start_transpose(out=out, in_=in_).then_inc(sh, 16)
        else:
            def run(e, out=out, in_=in_, sh=sh, kw=kw):
                e.dma_start(out=out, in_=in_, **kw).then_inc(sh, 16)
        self.ops[eng].append(run)
        if self.paranoid:
            g = self.sems["gseq"]
            v = self.cnt[sem]
            self.gcnt += 1
            self.ops[eng].append(lambda e, sh=sh, v=v, g=g: (e.wait_ge(sh, v),
                                                            e.sem_inc(g, 1)))
        return self.cnt[sem]


    def opw(self, eng, fn):
        sname = {"vector": "mV", "scalar": "mS", "gpsimd": "cc"}[eng]
        v = self.op(eng, fn, inc=(sname, 1))
        self.wait(eng, sname, v)
        return v

    def wait(self, eng, sem, val):
        if val is None or val <= 0:
            return
        sh = self.sems[sem]
        self.ops[eng].append(lambda e, sh=sh, v=val: e.wait_ge(sh, v))

    def wait_all(self, eng, sem):
        self.wait(eng, sem, self.cnt[sem])

    def snap(self, sem):
        return self.cnt[sem]

    def build(self):
        with self.nc.Block() as block:
            @block.sync
            def _(sync):
                for fn in self.ops["sync"]:
                    fn(sync)

            @block.scalar
            def _(scalar):
                for fn in self.ops["scalar"]:
                    fn(scalar)

            @block.vector
            def _(vector):
                for fn in self.ops["vector"]:
                    fn(vector)

            @block.tensor
            def _(tensor):
                for fn in self.ops["tensor"]:
                    fn(tensor)

            @block.gpsimd
            def _(gpsimd):
                for fn in self.ops["gpsimd"]:
                    fn(gpsimd)


def build_nc(T, dbg=False, phases=3, stop_after=99, p2mode="", dbg_big=False, paranoid=False, parmask="", dump_src=None):
    NTT = T // 128
    NG = T // 512
    TPG = 4
    TH = T // 2

    nc = bass.Bass("TRN2", target_bir_lowering=False, detect_race_conditions=False)
    kdbg = "ExternalOutput" if dbg else "Internal"
    kbig = "ExternalOutput" if (dbg and dbg_big) else "Internal"

    def dram(name, shape, dt, kind="Internal"):
        return nc.dram_tensor(name, shape, dt, kind=kind).ap()

    x_in = dram("x_in", [T, D], F32, "ExternalInput")
    w_q = dram("w_q", [D, D], F32, "ExternalInput")
    w_k = dram("w_k", [D, D], F32, "ExternalInput")
    w_v = dram("w_v", [D, D], F32, "ExternalInput")
    w_o = dram("w_o", [D, D], F32, "ExternalInput")
    b_all = dram("b_all", [4, D], F32, "ExternalInput")
    w_up = dram("w_up", [DI, D], F32, "ExternalInput")
    w_gate = dram("w_gate", [DI, D], F32, "ExternalInput")
    w_down = dram("w_down", [D, DI], F32, "ExternalInput")
    costok = dram("costok", [128, NTT * 5], F32, "ExternalInput")
    sintok = dram("sintok", [128, NTT * 5], F32, "ExternalInput")
    x_out = dram("x_out", [T, D], F32, "ExternalOutput")

    res_a = dram("res_a", [T, D], F16)
    res_b = dram("res_b", [T, D], F16)
    ln_h = dram("ln_h", [T, D], F16)
    spU = dram("spU", [NB, 128, T], F16)
    spG = dram("spG", [NB, 128, T], F16)
    spT = dram("spT", [NB, 128, T], F16)
    spZ = dram("spZ", [NB, 128, T], F16)
    wnat = dram("wnat", [DI * D], F16)
    wupT = dram("wupT", [NB, KC, 128, 128], F16)
    wgT = dram("wgT", [NB, KC, 128, 128], F16)
    wdnT = dram("wdnT", [NB, 128, D], F16)
    wat16 = dram("wat16", [4, D, D], F16)
    q_h = dram("q_h", [T, D], F16)
    dbg_arA = dram("dbg_arA", [128, KC * T], F16, "ExternalOutput") if (dbg and dbg_big) else None
    dbg_ps = dram("dbg_ps", [128, 512], F32, "ExternalOutput") if dbg else None
    dbg_m1 = dram("dbg_m1", [128, 512], F16, "ExternalOutput") if dbg else None
    dbg_acc = dram("dbg_acc", [128, 18], F32, "ExternalOutput") if dbg else None
    dbg_stA = dram("dbg_stA", [128, T], F16, "ExternalOutput") if dbg else None
    bar_i = [dram(f"bar_i{i}", [256], F32) for i in range(8)]
    bar_o = [dram(f"bar_o{i}", [256], F32) for i in range(8)]
    dbg_scl = dram("dbg_scl", [128, 8], F32, "ExternalOutput") if dbg else None
    dbg_ra = dram("dbg_ra", [T, D], F16, "ExternalOutput") if dbg else None
    dbg_rb = dram("dbg_rb", [T, D], F16, "ExternalOutput") if dbg else None
    dbg_lh = dram("dbg_lh", [T, D], F16, "ExternalOutput") if (dbg and dbg_big) else None
    dbg_qh = dram("dbg_qh", [T, D], F16, "ExternalOutput") if (dbg and dbg_big) else None
    dbg_spU = dram("dbg_spU", [NB * 128 * T], F16, "ExternalOutput") if (dbg and dbg_big) else None
    dbg_spG = dram("dbg_spG", [NB * 128 * T], F16, "ExternalOutput") if (dbg and dbg_big) else None
    dbg_spT = dram("dbg_spT", [NB * 128 * T], F16, "ExternalOutput") if (dbg and dbg_big) else None
    dbg_wu = dram("dbg_wu", [NB * KC * 128 * 128], F16, "ExternalOutput") if (dbg and dbg_big) else None

    stk = contextlib.ExitStack()

    def sb(name, shape, dt):
        return stk.enter_context(nc.sbuf_tensor(name, shape, dt))

    AR = 32768
    arA = sb("arA", [128, AR], F16)
    arB = sb("arB", [128, AR], F16)
    stA = sb("stA", [128, 4096], F16)
    stB = sb("stB", [128, 4096], F16)
    rpb = sb("rpb", [128, 4096], F16)
    wbf = sb("wbf", [128, 8192], F16)
    xf = sb("xf", [128, 2 * D], F32)
    acc = sb("acc", [128, 4], F32)
    tmp1 = sb("tmp1", [128, 8], F32)
    scl = sb("scl", [128, 8], F32)
    ctok = sb("ctok", [128, NTT * 5], F32)
    stok = sb("stok", [128, NTT * 5], F32)
    brow = sb("brow", [1, 256], F32)
    gmax = sb("gmax", [1, 8], F32)
    grow = sb("grow", [1, 8], F32)
    ones16 = sb("ones16", [1, 128], F16)
    ones32 = sb("ones32", [1, 128], F32)
    brows = sb("brows", [1, 4 * D], F16)
    m1sb = sb("m1sb", [128, 512], F16)
    sv1 = sb("sv1", [1, 4], F32)
    wsc = sb("wsc", [128, 96], F32)
    wrec = sb("wrec", [128, 96], F32)
    tI1 = sb("tI1", [128, 1024], I32)
    tI2 = sb("tI2", [128, 1024], I32)
    tF1 = sb("tF1", [128, 1024], F32)
    tH1 = sb("tH1", [128, 1024], F16)
    outstg = sb("outstg", [128, 1024], F32)

    pA = stk.enter_context(nc.psum_tensor("pA", [128, 1024], F32))
    pB = stk.enter_context(nc.psum_tensor("pB", [128, 1024], F32))
    pC = stk.enter_context(nc.psum_tensor("pC", [128, 512], F32))
    pD = stk.enter_context(nc.psum_tensor("pD", [128, 512], F32))
    pE = stk.enter_context(nc.psum_tensor("pE", [128, 512], F32))
    pF = stk.enter_context(nc.psum_tensor("pF", [128, 512], F32))

    P = Prog(nc, stk, paranoid=paranoid)
    PM = set(parmask.split(",")) if parmask else set()

    def par(tag):
        if not paranoid:
            P.paranoid = (tag in PM) or ("all" in PM)
    P.sem("gseq","qW", "qLN", "qTR", "qSP", "qSL", "qR", "qO", "qB", "qX",
          "mS", "mV", "mT", "cc")

    xf16 = xf.ap().bitcast(F16)
    arA32 = arA.ap().bitcast(F32)
    wdtmp32 = arA32[:, 0:DI]
    wtmpq = arA32[:, DI:2 * DI]
    wq8 = arA.ap().bitcast(I8)[:, 24576:24576 + DI]

    par("prep1")
    P.opw("vector", lambda e: e.memset(ones16[:], 1.0))
    P.opw("vector", lambda e: e.memset(ones32[:], 1.0))
    P.opw("vector", lambda e: e.memset(scl[:], 0.0))
    P.opw("vector", lambda e: e.memset(tmp1[:], 0.0))
    P.opw("vector", lambda e: e.memset(gmax[:], 0.0))
    P.opw("vector", lambda e: e.memset(grow[:], 0.0))
    P.opw("vector", lambda e: e.memset(brow[:], 0.0))

    for wi, wsrc in enumerate((w_q, w_k, w_v, w_o)):
        for c in range(KC):
            ld = P.dma("sync", wdtmp32[:, 0:D], wsrc[c * 128:(c + 1) * 128, :], "qW")
            P.wait("scalar", "qW", ld)
            ev = P.op("scalar", lambda e: e.copy(out=wbf[:, 0:D], in_=wdtmp32[:, 0:D]),
                      inc=("mS", 1))
            P.wait("gpsimd", "mS", ev)
            st = P.dma("gpsimd", wat16[wi, c * 128:(c + 1) * 128, :], wbf[:, 0:D], "qSP")
            P.wait("sync", "mS", ev)
            P.wait("scalar", "qSP", st)
    for j in range(4):
        ld = P.dma("sync", wdtmp32[0:1, 0:D], b_all[j:j + 1, :], "qW")
        P.wait("scalar", "qW", ld)
        bev = P.op("scalar", lambda e, j=j: e.copy(
            out=brows[:, j * D:(j + 1) * D], in_=wdtmp32[0:1, 0:D]), inc=("mS", 1))
        P.wait("sync", "mS", bev)
    P.dma("sync", ctok[:], costok, "qW")
    P.dma("sync", stok[:], sintok, "qW")
    consts_ld = P.snap("qW")

    def prep_ffn_weight(wsrc, rows, cols, store_T, tag="prep2"):
        nt = rows // 128
        nbl = cols // 32
        wnat_v = wnat[0:rows * cols].rearrange("(r c) -> r c", c=cols)
        par(tag + "a")
        for t in range(nt):
            ld = P.dma("sync", wdtmp32[:, 0:cols], wsrc[t * 128:(t + 1) * 128, :], "qW")
            P.wait("vector", "qW", ld)
            v3 = wdtmp32[:, 0:cols].rearrange("p (b i) -> p b i", i=32)
            P.opw("vector", lambda e, v3=v3, nbl=nbl: e.tensor_reduce(
                out=wsc[:, 0:nbl], in_=v3, axis=AX.X, op=ALU.max,
                apply_absolute_value=True))
            P.op("vector", lambda e, nbl=nbl: e.tensor_scalar(
                out=wsc[:, 0:nbl], in0=wsc[:, 0:nbl], scalar1=1.0 / 7.0,
                scalar2=1e-12, op0=ALU.mult, op1=ALU.max))
            P.opw("vector", lambda e, nbl=nbl: e.reciprocal(
                out=wrec[:, 0:nbl], in_=wsc[:, 0:nbl]))
            rb = wrec.ap()[:, 0:nbl, None].broadcast_to([128, nbl, 32])
            mv = P.op("vector", lambda e, v3=v3, rb=rb, cols=cols: e.tensor_tensor(
                out=wtmpq[:, 0:cols], in0=v3, in1=rb, op=ALU.mult), inc=("mV", 1))
            P.wait("scalar", "mV", mv)
            cv = P.op("scalar", lambda e, cols=cols: e.copy(
                out=wq8[:, 0:cols], in_=wtmpq[:, 0:cols]), inc=("mS", 1))
            P.wait("vector", "mS", cv)
            sbb = wsc.ap()[:, 0:nbl, None].broadcast_to([128, nbl, 32])
            q3 = wq8[:, 0:cols].rearrange("p (b i) -> p b i", i=32)
            dq = P.op("vector", lambda e, q3=q3, sbb=sbb, cols=cols: e.tensor_tensor(
                out=wbf[:, 0:cols], in0=q3, in1=sbb, op=ALU.mult), inc=("mV", 1))
            P.wait("gpsimd", "mV", dq)
            st = P.dma("gpsimd", wnat_v[t * 128:(t + 1) * 128, :], wbf[:, 0:cols], "qSP")
            P.wait("sync", "mV", dq)
            P.wait("vector", "qSP", st)
        P.wait("sync", "qSP", P.snap("qSP"))
        par(tag + "b")
        store_T(wnat_v)

    def store_T_upgate(dst):
        def go(wnat_v):
            for c in range(KC):
                tr = P.dma("sync", wbf[:, 0:DI], wnat_v[:, c * 128:(c + 1) * 128],
                           "qTR", transpose=True)
                P.wait("gpsimd", "qTR", tr)
                src = wbf.ap()[:, 0:DI].rearrange("p (nb x) -> p nb x", x=128)
                st = P.dma("gpsimd", dst[:, c, :, :].rearrange("nb p x -> p nb x"),
                           src, "qSP")
                P.wait("sync", "qSP", st)
        return go

    def store_T_down(wnat_v):
        for nb2 in range(NB):
            tr = P.dma("sync", wbf[:, 0:D], wnat_v[:, nb2 * 128:(nb2 + 1) * 128],
                       "qTR", transpose=True)
            P.wait("gpsimd", "qTR", tr)
            st = P.dma("gpsimd", wdnT[nb2], wbf[:, 0:D], "qSP")
            P.wait("sync", "qSP", st)

    prep_ffn_weight(w_up, DI, D, store_T_upgate(wupT), "prep2")
    prep_ffn_weight(w_gate, DI, D, store_T_upgate(wgT), "prep3")
    prep_ffn_weight(w_down, D, DI, store_T_down, "prep4")

    prep_done = P.snap("qSP")
    for e in ENGS:
        P.wait(e, "qSP", prep_done)

    LNO = 2048
    QSZ = min(1024, T)
    NQ4 = T // QSZ

    def ln_pass(x_src, x_dtype, need_max):
        if need_max:
            P.opw("vector", lambda e: e.memset(acc[:, 0:1], 0.0))
        prev_free = None
        prev_ve = None
        prev_st = None
        for i in range(NTT):
            slot = i % 2
            if x_dtype == F32:
                xt = xf[:, slot * D:(slot + 1) * D]
            else:
                xt = rpb[:, slot * D:(slot + 1) * D]
            if prev_free is not None:
                P.wait("sync", prev_free[0], prev_free[1])
            ld = P.dma("sync", xt, x_src[i * 128:(i + 1) * 128, :], "qX")
            P.wait("vector", "qX", ld)
            P.wait("scalar", "qX", ld)
            P.opw("vector", lambda e, xt=xt: e.reduce_sum(
                out=tmp1[:, 0:1], in_=xt, axis=AX.X))
            P.wait("scalar", "mV", prev_ve)
            sqv = P.op("scalar", lambda e, xt=xt: e.activation(
                out=tF1[:, 0:D], in_=xt, func=AF.Square), inc=("mS", 1))
            P.wait("vector", "mS", sqv)
            P.opw("vector", lambda e: e.reduce_sum(
                out=tmp1[:, 1:2], in_=tF1[:, 0:D], axis=AX.X))
            P.op("vector", lambda e: e.tensor_scalar_mul(
                out=tmp1[:, 2:3], in0=tmp1[:, 0:1], scalar1=1.0 / D))
            P.op("vector", lambda e: e.tensor_mul(
                out=tmp1[:, 3:4], in0=tmp1[:, 2:3], in1=tmp1[:, 2:3]))
            P.op("vector", lambda e: e.scalar_tensor_tensor(
                out=tmp1[:, 4:5], in0=tmp1[:, 1:2], scalar=1.0 / D,
                in1=tmp1[:, 3:4], op0=ALU.mult, op1=ALU.subtract))
            ve = P.op("vector", lambda e: e.tensor_scalar_add(
                out=tmp1[:, 4:5], in0=tmp1[:, 4:5], scalar1=1e-5), inc=("mV", 1))
            prev_ve = ve
            P.wait("scalar", "mV", ve)
            se = P.op("scalar", lambda e: e.activation(
                out=tmp1[:, 5:6], in_=tmp1[:, 4:5], func=AF.Sqrt), inc=("mS", 1))
            P.wait("vector", "mS", se)
            P.opw("vector", lambda e: e.reciprocal(out=tmp1[:, 6:7], in_=tmp1[:, 5:6]))
            vm = P.op("vector", lambda e: e.scalar_tensor_tensor(
                out=tmp1[:, 7:8], in0=tmp1[:, 2:3], scalar=-1.0,
                in1=tmp1[:, 6:7], op0=ALU.mult, op1=ALU.mult), inc=("mV", 1))
            P.wait("scalar", "mV", vm)
            P.wait("scalar", "qLN", prev_st)
            lno = P.op("scalar", lambda e, xt=xt: e.activation(
                out=arB[:, LNO:LNO + D], in_=xt, func=AF.Identity,
                scale=tmp1[:, 6:7], bias=tmp1[:, 7:8]), inc=("mS", 1))
            P.wait("sync", "mS", lno)
            prev_st = P.dma("sync", ln_h[i * 128:(i + 1) * 128, :], arB[:, LNO:LNO + D],
                            "qLN")
            if need_max:
                P.wait("vector", "mS", lno)
                P.opw("vector", lambda e: e.tensor_reduce(
                    out=tmp1[:, 0:1], in_=arB[:, LNO:LNO + D], axis=AX.X, op=ALU.max,
                    apply_absolute_value=True))
                pv = P.op("vector", lambda e: e.tensor_max(
                    out=acc[:, 0:1], in0=acc[:, 0:1], in1=tmp1[:, 0:1]), inc=("mV", 1))
                prev_free = ("mV", pv)
            else:
                prev_free = ("mS", lno)

    bar_n = [0]
    dbg1 = [True]

    def barrier(nq, post):
        bi = bar_n[0]
        bar_n[0] += 1
        n = 128 * nq
        vp = P.op("vector", lambda e, nq=nq: e.tensor_copy(
            out=tmp1[:, 0:nq], in_=acc[:, 0:nq]), inc=("mV", 1))
        P.wait("gpsimd", "mV", vp)
        d1 = P.dma("gpsimd", bar_i[bi][0:n], tmp1[:, 0:nq], "qB")
        P.wait("gpsimd", "qB", d1)
        ccv = P.op("gpsimd", lambda e, bi=bi, n=n: e.collective_compute(
            "AllReduce", ALU.max, replica_groups=[list(range(NCORES))],
            ins=[bar_i[bi][0:n].opt()], outs=[bar_o[bi][0:n].opt()]), inc=("cc", 1))
        P.wait("gpsimd", "cc", ccv)
        d2 = P.dma("gpsimd", brow[:, 0:n], bar_o[bi][0:n], "qB")
        P.wait("vector", "qB", d2)
        for q in range(nq):
            qv = brow[:, q:n:nq]
            P.opw("vector", lambda e, q=q, qv=qv: e.tensor_reduce(
                out=gmax[:, q:q + 1], in_=qv, axis=AX.X, op=ALU.max))
        m = post()
        vb = P.op("vector", lambda e: e.tensor_copy(
            out=grow[:, 7:8], in_=grow[:, 0:1]), inc=("mV", 1))
        P.wait("tensor", "mV", vb)
        P.wait("tensor", "mS", P.snap("mS"))
        P.wait("tensor", "qW", consts_ld)
        tb = P.op("tensor", lambda e, m=m: e.matmul(
            pF[:, 0:m], ones32[:], grow[:, 0:m], start=True, stop=True),
            inc=("mT", 1))
        P.wait("scalar", "mT", tb)
        sc = P.op("scalar", lambda e, m=m: e.copy(out=scl[:, 0:m], in_=pF[:, 0:m]),
                  inc=("mS", 1))
        return sc

    def ffn(x_src, x_dtype, out_dst, out_dtype):
        if stop_after < 1:
            return
        par("ln")
        ln_pass(x_src, x_dtype, True)
        par("bar")
        if stop_after < 2:
            return

        def post1():
            P.op("vector", lambda e: e.tensor_scalar(
                out=gmax[:, 1:2], in0=gmax[:, 0:1], scalar1=1.0 / QMAX,
                scalar2=1e-12, op0=ALU.mult, op1=ALU.max))
            P.opw("vector", lambda e: e.reciprocal(out=grow[:, 0:1], in_=gmax[:, 1:2]))
            P.op("vector", lambda e: e.tensor_copy(out=grow[:, 1:2], in_=gmax[:, 1:2]))
            return 2
        b1 = barrier(1, post1)
        if stop_after < 3:
            return

        par("tr")
        P.wait_all("sync", "qLN")
        for c in range(KC):
            P.dma("sync", arA[:, c * T:(c + 1) * T], ln_h[:, c * 128:(c + 1) * 128],
                  "qTR", transpose=True)
        trs = P.snap("qTR")

        P.wait("scalar", "qTR", trs)
        P.wait("scalar", "mS", b1)
        for c in range(KC):
            for q in range(NQ4):
                o0 = c * T + q * QSZ
                P.op("scalar", lambda e, o0=o0: e.activation(
                    out=tI1[:, 0:QSZ], in_=arA[:, o0:o0 + QSZ], func=AF.Identity,
                    scale=scl[:, 0:1]), inc=("mS", 1))
                P.op("scalar", lambda e, o0=o0: e.activation(
                    out=arA[:, o0:o0 + QSZ], in_=tI1[:, 0:QSZ], func=AF.Identity,
                    scale=scl[:, 1:2]), inc=("mS", 1))
        xq_done = P.snap("mS")
        if stop_after < 4:
            return
        if dbg and dbg_big:
            P.wait("sync", "mS", xq_done)
            P.dma("sync", dbg_arA[:, 0:KC * T], arA[:, 0:KC * T], "qX") if dbg_big else None

        par("p2")
        P.opw("vector", lambda e: e.memset(acc[:, 0:2], 0.0))
        P.wait("tensor", "mS", xq_done)
        bank_free = {}
        st_prev = {"u": None, "g": None}
        slot_mm = [None, None]
        for nb in range(NB):
            wslot = nb % 2
            up_l = wbf[:, wslot * 2048:wslot * 2048 + 1024]
            ga_l = wbf[:, wslot * 2048 + 1024:wslot * 2048 + 2048]
            P.wait("sync", "mT", slot_mm[wslot])
            lu = P.dma("sync", up_l.rearrange("p (c x) -> p c x", c=KC),
                       wupT[nb].rearrange("c p x -> p c x"), "qW")
            lg = P.dma("sync", ga_l.rearrange("p (c x) -> p c x", c=KC),
                       wgT[nb].rearrange("c p x -> p c x"), "qW")
            P.wait("tensor", "qW", lg)
            for g in range(NG):
                bu = (pC, pD)[g % 2]
                bg = (pE, pF)[g % 2]
                for key in ((id(bu)), (id(bg))):
                    if key in bank_free:
                        sH, vH = bank_free[key]
                        P.wait("tensor", "mS", sH)
                        P.wait("tensor", "mV", vH)
                for c in range(KC):
                    P.op("tensor", lambda e, c=c, bu=bu, up_l=up_l, g=g: e.matmul(
                        bu[:], up_l[:, c * 128:(c + 1) * 128],
                        arA[:, c * T + g * 512:c * T + g * 512 + 512],
                        start=(c == 0), stop=(c == KC - 1)),
                        inc=("mT", 1) if c == KC - 1 else None)
                mT_u = P.snap("mT")
                if dbg and dbg1[0] and nb == 0 and g == 0:
                    P.wait("vector", "mT", mT_u)
                    dv = P.op("vector", lambda e, bu=bu: e.tensor_copy(
                        out=outstg[:, 0:512], in_=bu[:]), inc=("mV", 1))
                    P.wait("sync", "mV", dv)
                    P.dma("sync", dbg_ps, outstg[:, 0:512], "qX")
                for c in range(KC):
                    P.op("tensor", lambda e, c=c, bg=bg, ga_l=ga_l, g=g: e.matmul(
                        bg[:], ga_l[:, c * 128:(c + 1) * 128],
                        arA[:, c * T + g * 512:c * T + g * 512 + 512],
                        start=(c == 0), stop=(c == KC - 1)),
                        inc=("mT", 1) if c == KC - 1 else None)
                mT_g = P.snap("mT")
                slot_mm[wslot] = mT_g
                for (tag, bank, stg, accc, mTv) in (
                        ("u", bu, stA, 0, mT_u), ("g", bg, stB, 1, mT_g)):
                    P.wait("scalar", "mT", mTv)
                    if g == 0:
                        P.wait("scalar", "qSP", st_prev[tag])
                    sE = P.op("scalar", lambda e, bank=bank, stg=stg, g=g: e.copy(
                        out=stg[:, g * 512:(g + 1) * 512], in_=bank[:]), inc=("mS", 1))
                    P.wait("vector", "mS", sE)
                    P.opw("vector", lambda e, stg=stg, g=g: e.tensor_reduce(
                        out=tmp1[:, 0:1], in_=stg[:, g * 512:(g + 1) * 512],
                        axis=AX.X, op=ALU.max, apply_absolute_value=True))
                    vE = P.op("vector", lambda e, accc=accc: e.tensor_max(
                        out=acc[:, accc:accc + 1], in0=acc[:, accc:accc + 1],
                        in1=tmp1[:, 0:1]), inc=("mV", 1))
                    bank_free[id(bank)] = (sE, vE)
                    if g == NG - 1 and "nospill" not in p2mode:
                        P.wait("gpsimd", "mS", sE)
                        dst = spU if tag == "u" else spG
                        st_prev[tag] = P.dma("gpsimd", dst[nb], stg[:, 0:T], "qSP")
        p2_sp = P.snap("qSP")
        if stop_after < 5:
            return

        par("bar")

        def post2():
            P.op("vector", lambda e: e.tensor_scalar(
                out=gmax[:, 2:4], in0=gmax[:, 0:2], scalar1=1.0 / QMAX,
                scalar2=1e-12, op0=ALU.mult, op1=ALU.max))
            P.opw("vector", lambda e: e.reciprocal(out=grow[:, 0:1], in_=gmax[:, 2:3]))
            P.opw("vector", lambda e: e.reciprocal(out=grow[:, 1:2], in_=gmax[:, 3:4]))
            P.op("vector", lambda e: e.tensor_copy(out=grow[:, 2:4], in_=gmax[:, 2:4]))
            P.op("vector", lambda e: e.tensor_scalar_mul(
                out=grow[:, 4:5], in0=gmax[:, 3:4], scalar1=1.0 / 65536.0))
            return 5
        b2 = barrier(2, post2)
        import os as _os
        if dbg and dbg1[0] and _os.environ.get('DBG2','1')=='1':
            dbg1[0] = False
            vd = P.op("vector", lambda e: e.tensor_copy(out=tmp1[:, 0:2],
                                                        in_=acc[:, 0:2]), inc=("mV", 1))
            P.wait("sync", "mV", vd)
            P.dma("sync", dbg_acc[:, 0:2], tmp1[:, 0:2], "qX")
            P.dma("sync", dbg_acc[0:1, 2:10], gmax[:, 0:8], "qX")
            P.dma("sync", dbg_acc[0:1, 10:18], grow[:, 0:8], "qX")
            P.wait("scalar", "mS", b2)
            P.dma("sync", dbg_stA, stA[:, 0:T], "qX")
        if stop_after < 6:
            return
        P.op("vector", lambda e: e.tensor_copy(out=sv1[:, 0:1], in_=grow[:, 4:5]))
        P.op("vector", lambda e: e.tensor_copy(out=sv1[:, 1:2], in_=grow[:, 0:1]))
        P.op("vector", lambda e: e.tensor_copy(out=sv1[:, 2:3], in_=grow[:, 2:3]))

        par("p3")
        P.opw("vector", lambda e: e.memset(acc[:, 0:1], 0.0))
        P.wait("sync", "qSP", p2_sp)
        P.wait("scalar", "mS", b2)
        st_p = None
        ld_p = [None, None]
        prev_v1 = None
        for nb in range(NB):
            slot = nb % 2
            gbuf = arA[:, slot * T:(slot + 1) * T]
            P.wait("sync", "mS", ld_p[slot])
            ldg = P.dma("sync", gbuf, spG[nb], "qSL")
            P.wait("scalar", "qSL", ldg)
            P.wait("scalar", "qSP", st_p)
            s4 = None
            for q in range(NQ4):
                q0 = q * QSZ
                P.wait("scalar", "mV", prev_v1)
                P.op("scalar", lambda e, gbuf=gbuf, q0=q0: e.activation(
                    out=tI1[:, 0:QSZ], in_=gbuf[:, q0:q0 + QSZ], func=AF.Identity,
                    scale=scl[:, 1:2]), inc=("mS", 1))
                P.op("scalar", lambda e: e.activation(
                    out=tF1[:, 0:QSZ], in_=tI1[:, 0:QSZ], func=AF.Sigmoid,
                    scale=scl[:, 3:4]), inc=("mS", 1))
                c3 = P.op("scalar", lambda e: e.activation(
                    out=tI2[:, 0:QSZ], in_=tF1[:, 0:QSZ], func=AF.Identity,
                    scale=65536.0), inc=("mS", 1))
                P.wait("vector", "mS", c3)
                v1 = P.op("vector", lambda e: e.tensor_tensor(
                    out=tF1[:, 0:QSZ], in0=tI1[:, 0:QSZ], in1=tI2[:, 0:QSZ],
                    op=ALU.mult), inc=("mV", 1))
                P.opw("vector", lambda e: e.tensor_reduce(
                    out=tmp1[:, 0:1], in_=tF1[:, 0:QSZ], axis=AX.X, op=ALU.max,
                    apply_absolute_value=True))
                prev_v1 = P.op("vector", lambda e: e.tensor_max(
                    out=acc[:, 0:1], in0=acc[:, 0:1], in1=tmp1[:, 0:1]), inc=("mV", 1))
                P.wait("scalar", "mV", prev_v1)
                s4 = P.op("scalar", lambda e, q0=q0: e.activation(
                    out=stA[:, q0:q0 + QSZ], in_=tF1[:, 0:QSZ], func=AF.Identity,
                    scale=scl[:, 4:5]), inc=("mS", 1))
            ld_p[slot] = P.cnt["mS"]
            P.wait("gpsimd", "mS", s4)
            st_p = P.dma("gpsimd", spT[nb], stA[:, 0:T], "qSP")
        p3_sp = P.snap("qSP")
        if stop_after < 7:
            return

        par("bar")

        def post3():
            P.op("vector", lambda e: e.tensor_mul(
                out=gmax[:, 1:2], in0=gmax[:, 0:1], in1=sv1[:, 0:1]))
            P.op("vector", lambda e: e.tensor_scalar(
                out=gmax[:, 2:3], in0=gmax[:, 1:2], scalar1=1.0 / QMAX,
                scalar2=1e-12, op0=ALU.mult, op1=ALU.max))
            P.opw("vector", lambda e: e.reciprocal(out=grow[:, 0:1], in_=gmax[:, 2:3]))
            P.op("vector", lambda e: e.tensor_copy(out=grow[:, 1:2], in_=gmax[:, 2:3]))
            P.op("vector", lambda e: e.tensor_copy(out=grow[:, 2:3], in_=sv1[:, 1:2]))
            P.op("vector", lambda e: e.tensor_copy(out=grow[:, 3:4], in_=sv1[:, 2:3]))
            return 4
        b3 = barrier(1, post3)
        P.op("vector", lambda e: e.tensor_copy(out=sv1[:, 3:4], in_=grow[:, 1:2]))

        par("p4")
        P.opw("vector", lambda e: e.memset(acc[:, 0:1], 0.0))
        P.wait("sync", "qSP", p3_sp)
        P.wait("scalar", "mS", b3)
        st_z = None
        ld_p4 = [None, None]
        prev_v1 = None
        for nb in range(NB):
            slot = nb % 2
            tbuf = arA[:, slot * T:(slot + 1) * T]
            ubuf = arA[:, (2 + slot) * T:(3 + slot) * T]
            P.wait("sync", "mS", ld_p4[slot])
            P.dma("sync", tbuf, spT[nb], "qSL")
            ldu = P.dma("sync", ubuf, spU[nb], "qSL")
            P.wait("scalar", "qSL", ldu)
            P.wait("scalar", "qSP", st_z)
            s4 = None
            for q in range(NQ4):
                q0 = q * QSZ
                P.wait("scalar", "mV", prev_v1)
                P.op("scalar", lambda e, tbuf=tbuf, q0=q0: e.activation(
                    out=tI1[:, 0:QSZ], in_=tbuf[:, q0:q0 + QSZ], func=AF.Identity,
                    scale=scl[:, 0:1]), inc=("mS", 1))
                P.op("scalar", lambda e, ubuf=ubuf, q0=q0: e.activation(
                    out=tI2[:, 0:QSZ], in_=ubuf[:, q0:q0 + QSZ], func=AF.Identity,
                    scale=scl[:, 2:3]), inc=("mS", 1))
                c3 = P.op("scalar", lambda e: e.activation(
                    out=tH1[:, 0:QSZ], in_=tI2[:, 0:QSZ], func=AF.Identity,
                    scale=scl[:, 3:4]), inc=("mS", 1))
                P.wait("vector", "mS", c3)
                v1 = P.op("vector", lambda e: e.tensor_tensor(
                    out=tF1[:, 0:QSZ], in0=tI1[:, 0:QSZ], in1=tH1[:, 0:QSZ],
                    op=ALU.mult), inc=("mV", 1))
                P.opw("vector", lambda e: e.tensor_reduce(
                    out=tmp1[:, 0:1], in_=tF1[:, 0:QSZ], axis=AX.X, op=ALU.max,
                    apply_absolute_value=True))
                prev_v1 = P.op("vector", lambda e: e.tensor_max(
                    out=acc[:, 0:1], in0=acc[:, 0:1], in1=tmp1[:, 0:1]), inc=("mV", 1))
                P.wait("scalar", "mV", prev_v1)
                s4 = P.op("scalar", lambda e, q0=q0: e.activation(
                    out=stB[:, q0:q0 + QSZ], in_=tF1[:, 0:QSZ], func=AF.Identity,
                    scale=scl[:, 1:2]), inc=("mS", 1))
            ld_p4[slot] = P.cnt["mS"]
            P.wait("gpsimd", "mS", s4)
            st_z = P.dma("gpsimd", spZ[nb], stB[:, 0:T], "qSP")
        p4_sp = P.snap("qSP")
        if stop_after < 8:
            return

        par("bar")

        def post4():
            P.op("vector", lambda e: e.tensor_mul(
                out=gmax[:, 1:2], in0=gmax[:, 0:1], in1=sv1[:, 3:4]))
            P.op("vector", lambda e: e.tensor_scalar(
                out=gmax[:, 2:3], in0=gmax[:, 1:2], scalar1=1.0 / QMAX,
                scalar2=1e-12, op0=ALU.mult, op1=ALU.max))
            P.opw("vector", lambda e: e.reciprocal(out=grow[:, 0:1], in_=gmax[:, 2:3]))
            P.op("vector", lambda e: e.tensor_scalar_mul(
                out=grow[:, 1:2], in0=gmax[:, 2:3], scalar1=0.5))
            return 2
        b4 = barrier(1, post4)

        par("p5")
        P.wait("sync", "qSP", p4_sp)
        for nb2 in range(NB):
            P.dma("sync", arA[:, nb2 * D:nb2 * D + D], wdnT[nb2], "qW")
        wdn_ld = P.snap("qW")
        P.wait("tensor", "qW", wdn_ld)
        P.wait("scalar", "mS", b4)
        OBUF = 2048
        o32_v = tI1[:, 0:512]
        zs0 = AR - 1024
        down_free = {}
        out_prev = None
        for g in range(NG):
            last_oc = None
            for nb in range(NB):
                zslot = nb % 2
                zb = arB[:, zs0 + zslot * 512:zs0 + (zslot + 1) * 512]
                P.wait("sync", "mS", last_oc if nb >= 2 else None)
                ldz = P.dma("sync", zb, spZ[nb][:, g * 512:(g + 1) * 512], "qSL")
                P.wait("scalar", "qSL", ldz)
                P.op("scalar", lambda e, zb=zb: e.activation(
                    out=o32_v, in_=zb, func=AF.Identity, scale=scl[:, 0:1]),
                    inc=("mS", 1))
                last_oc = P.op("scalar", lambda e, nb=nb: e.activation(
                    out=arB[:, OBUF + nb * 512:OBUF + (nb + 1) * 512], in_=o32_v,
                    func=AF.Copy), inc=("mS", 1))
            for tt in range(TPG):
                bank = (pA, pB)[tt % 2]
                if id(bank) in down_free:
                    P.wait("tensor", "mV", down_free[id(bank)])
                P.wait("tensor", "mS", last_oc)
                mmv = None
                for nb in range(NB):
                    lhs = arB[:, OBUF + nb * 512 + tt * 128:OBUF + nb * 512 + (tt + 1) * 128]
                    for hh in range(2):
                        mmv = P.op("tensor", lambda e, bank=bank, lhs=lhs, nb=nb, hh=hh:
                                   e.matmul(bank[:, hh * 512:(hh + 1) * 512], lhs,
                                            arA[:, nb * D + hh * 512:nb * D + (hh + 1) * 512],
                                            start=(nb == 0), stop=(nb == NB - 1)),
                                   inc=("mT", 1) if (nb == NB - 1 and hh == 1) else None)
                ti = g * TPG + tt
                xslot = ti % 2
                if x_dtype == F32:
                    xin_ap = xf[:, xslot * D:(xslot + 1) * D]
                else:
                    xin_ap = xf16[:, xslot * 2 * D:xslot * 2 * D + D]
                P.wait("sync", "mV", down_free.get(id(bank)))
                ldx = P.dma("sync", xin_ap, x_src[ti * 128:(ti + 1) * 128, :], "qR")
                P.wait("vector", "qR", ldx)
                P.wait("vector", "mT", mmv)
                P.wait("vector", "qO", out_prev)
                out_ap = outstg[:, 0:D] if out_dtype == F32 else tH1[:, 0:D]
                ov = P.op("vector", lambda e, bank=bank, xin_ap=xin_ap, out_ap=out_ap:
                          e.scalar_tensor_tensor(
                              out=out_ap, in0=bank[:], scalar=scl[:, 1:2],
                              in1=xin_ap, op0=ALU.mult, op1=ALU.add), inc=("mV", 1))
                down_free[id(bank)] = ov
                P.wait("sync", "mV", ov)
                out_prev = P.dma("sync", out_dst[ti * 128:(ti + 1) * 128, :], out_ap,
                                 "qO")
        P.wait_all("sync", "qO")

    def attention():
        par("ln")
        ln_pass(res_a, F16, False)
        par("attq")
        P.wait_all("sync", "qLN")
        for c in range(KC):
            P.dma("sync", arA[:, c * T:(c + 1) * T], ln_h[:, c * 128:(c + 1) * 128],
                  "qTR", transpose=True)
        trs = P.snap("qTR")
        P.wait("tensor", "qTR", trs)

        def rope_tile(buf16, i):
            b3 = buf16.rearrange("p (h x) -> p h x", h=H)
            t_o = rpb[:, 3 * D:3 * D + 16][:, :, None]
            t_e = rpb[:, 3 * D + 16:3 * D + 32][:, :, None]
            for p5 in range(5):
                cp = ctok[:, i * 5 + p5:i * 5 + p5 + 1]
                sp = stok[:, i * 5 + p5:i * 5 + p5 + 1]
                ke = b3[:, :, 2 * p5:2 * p5 + 1]
                ko = b3[:, :, 2 * p5 + 1:2 * p5 + 2]
                P.op("vector", lambda e, ko=ko, sp=sp: e.tensor_scalar_mul(
                    out=t_o, in0=ko, scalar1=sp))
                P.op("vector", lambda e, ke=ke, sp=sp: e.tensor_scalar_mul(
                    out=t_e, in0=ke, scalar1=sp))
                P.op("vector", lambda e, ke=ke, cp=cp: e.scalar_tensor_tensor(
                    out=ke, in0=ke, scalar=cp, in1=t_o, op0=ALU.mult,
                    op1=ALU.subtract))
                P.op("vector", lambda e, ko=ko, cp=cp: e.scalar_tensor_tensor(
                    out=ko, in0=ko, scalar=cp, in1=t_e, op0=ALU.mult, op1=ALU.add))

        wq_l = wbf[:, 0:KC * 1024]
        lw = P.dma("sync", wq_l.rearrange("p (c x) -> p c x", c=KC),
                   wat16[0].rearrange("(c p) x -> p c x", p=128), "qW")
        P.wait("tensor", "qW", lw)
        q16 = rpb[:, 2 * D:3 * D]
        qb_free = {}
        vq = None
        st_q = None
        for i in range(NTT):
            bank = (pA, pB)[i % 2]
            if id(bank) in qb_free:
                P.wait("tensor", "mS", qb_free[id(bank)])
            for c in range(KC):
                for hh in range(2):
                    P.op("tensor", lambda e, c=c, hh=hh, i=i, bank=bank: e.matmul(
                        bank[:, hh * 512:(hh + 1) * 512],
                        arA[:, c * T + i * 128:c * T + (i + 1) * 128],
                        wq_l[:, c * 1024 + hh * 512:c * 1024 + (hh + 1) * 512],
                        start=(c == 0), stop=False))
            P.op("tensor", lambda e, bank=bank: e.matmul(
                bank[:, 0:512], ones16[:], brows[:, 0:512], start=False, stop=True))
            qmm = P.op("tensor", lambda e, bank=bank: e.matmul(
                bank[:, 512:1024], ones16[:], brows[:, 512:1024], start=False,
                stop=True), inc=("mT", 1))
            P.wait("scalar", "mT", qmm)
            P.wait("scalar", "mV", vq)
            P.wait("scalar", "qSP", st_q)
            sQ = P.op("scalar", lambda e, bank=bank: e.copy(out=q16, in_=bank[:]),
                      inc=("mS", 1))
            qb_free[id(bank)] = sQ
            P.wait("vector", "mS", sQ)
            rope_tile(q16, i)
            vq = P.op("vector", lambda e: e.tensor_copy(
                out=tmp1[:, 1:2], in_=tmp1[:, 1:2]), inc=("mV", 1))
            P.wait("gpsimd", "mV", vq)
            st_q = P.dma("gpsimd", q_h[i * 128:(i + 1) * 128, :], q16, "qSP")
        P.wait("sync", "qSP", P.snap("qSP"))
        for c in range(KC):
            P.dma("sync", arB[:, c * T:(c + 1) * T], q_h[:, c * 128:(c + 1) * 128],
                  "qTR", transpose=True)
        trsQ = P.snap("qTR")

        par("attkv")
        P.wait("sync", "mT", P.snap("mT"))
        lwk = P.dma("sync", wbf[:, 0:KC * 1024].rearrange("p (c x) -> p c x", c=KC),
                    wat16[1].rearrange("(c p) x -> p c x", p=128), "qW")
        lv1 = P.dma("sync", stA[:, 0:4096].rearrange("p (c x) -> p c x", c=4),
                    wat16[2].rearrange("(c p) x -> c p x", p=128)[0:4]
                    .rearrange("c p x -> p c x"), "qW")
        lv2 = P.dma("sync", stB[:, 0:4096].rearrange("p (c x) -> p c x", c=4),
                    wat16[2].rearrange("(c p) x -> c p x", p=128)[4:8]
                    .rearrange("c p x -> p c x"), "qW")
        P.wait("tensor", "qW", lv2)
        k16 = rpb[:, 0:D]
        v16 = rpb[:, D:2 * D]
        kv_free = None
        m1_tile = None
        for i in range(NTT):
            if kv_free is not None:
                P.wait("tensor", "mV", kv_free)
            for c in range(KC):
                for hh in range(2):
                    P.op("tensor", lambda e, c=c, hh=hh, i=i: e.matmul(
                        pA[:, hh * 512:(hh + 1) * 512],
                        arA[:, c * T + i * 128:c * T + (i + 1) * 128],
                        wbf[:, c * 1024 + hh * 512:c * 1024 + (hh + 1) * 512],
                        start=(c == 0), stop=False))
            P.op("tensor", lambda e: e.matmul(
                pA[:, 0:512], ones16[:], brows[:, 1024:1536], start=False, stop=True))
            mk = P.op("tensor", lambda e: e.matmul(
                pA[:, 512:1024], ones16[:], brows[:, 1536:2048], start=False,
                stop=True), inc=("mT", 1))
            for c in range(KC):
                wsrc = stA if c < 4 else stB
                cc = c % 4
                for hh in range(2):
                    P.op("tensor", lambda e, c=c, cc=cc, hh=hh, i=i, wsrc=wsrc:
                         e.matmul(
                             pB[:, hh * 512:(hh + 1) * 512],
                             arA[:, c * T + i * 128:c * T + (i + 1) * 128],
                             wsrc[:, cc * 1024 + hh * 512:cc * 1024 + (hh + 1) * 512],
                             start=(c == 0), stop=False))
            P.op("tensor", lambda e: e.matmul(
                pB[:, 0:512], ones16[:], brows[:, 2048:2560], start=False, stop=True))
            mv_ = P.op("tensor", lambda e: e.matmul(
                pB[:, 512:1024], ones16[:], brows[:, 2560:3072], start=False,
                stop=True), inc=("mT", 1))
            P.wait("scalar", "mT", mv_)
            if kv_free is not None:
                P.wait("scalar", "mV", kv_free)
            if i > 0:
                P.wait("scalar", "mT", m1_tile)
            P.op("scalar", lambda e: e.copy(out=k16, in_=pA[:]), inc=("mS", 1))
            sV = P.op("scalar", lambda e: e.copy(out=v16, in_=pB[:]), inc=("mS", 1))
            P.wait("vector", "mS", sV)
            k3 = k16.rearrange("p (h x) -> p h x", h=H)
            for p5 in range(5):
                cp = ctok[:, i * 5 + p5:i * 5 + p5 + 1]
                sp = stok[:, i * 5 + p5:i * 5 + p5 + 1]
                ke = k3[:, :, 2 * p5:2 * p5 + 1]
                ko = k3[:, :, 2 * p5 + 1:2 * p5 + 2]
                t_o = rpb[:, 2 * D:2 * D + 16][:, :, None]
                t_e = rpb[:, 2 * D + 16:2 * D + 32][:, :, None]
                P.op("vector", lambda e, ko=ko, sp=sp, t_o=t_o: e.tensor_scalar_mul(
                    out=t_o, in0=ko, scalar1=sp))
                P.op("vector", lambda e, ke=ke, sp=sp, t_e=t_e: e.tensor_scalar_mul(
                    out=t_e, in0=ke, scalar1=sp))
                P.op("vector", lambda e, ke=ke, cp=cp, t_o=t_o: e.scalar_tensor_tensor(
                    out=ke, in0=ke, scalar=cp, in1=t_o, op0=ALU.mult,
                    op1=ALU.subtract))
                P.op("vector", lambda e, ko=ko, cp=cp, t_e=t_e: e.scalar_tensor_tensor(
                    out=ko, in0=ko, scalar=cp, in1=t_e, op0=ALU.mult, op1=ALU.add))
            kv_free = P.op("vector", lambda e: e.tensor_copy(
                out=tmp1[:, 0:1], in_=tmp1[:, 0:1]), inc=("mV", 1))
            P.wait("tensor", "mV", kv_free)
            for h in range(H):
                P.op("tensor", lambda e, h=h, i=i: e.matmul(
                    pE[64 * (h % 2):64 * (h % 2) + 64,
                       (h // 2) * 64:(h // 2) * 64 + 64],
                    k16[:, h * 64:(h + 1) * 64], v16[:, h * 64:(h + 1) * 64],
                    start=(i == 0 and h <= 1), stop=(i == NTT - 1),
                    skip_group_check=True),
                    inc=("mT", 1) if h == H - 1 else None)
            m1_tile = P.snap("mT")
        m1_mm = P.snap("mT")
        P.wait("scalar", "mT", m1_mm)
        sM = P.op("scalar", lambda e: e.activation(
            out=m1sb[:], in_=pE[:], func=AF.Copy, scale=0.125), inc=("mS", 1))
        if dbg and dbg_big:
            P.wait("sync", "mS", sM)
            P.dma("sync", dbg_m1, m1sb[:], "qX")

        par("attc")
        P.wait("sync", "mT", m1_mm)
        lwo = P.dma("sync", wbf[:, 0:KC * 1024].rearrange("p (c x) -> p c x", c=KC),
                    wat16[3].rearrange("(c p) x -> p c x", p=128), "qW")
        P.wait("tensor", "qW", lwo)
        P.wait("tensor", "qTR", trsQ)
        P.wait("tensor", "mS", sM)
        ctx_free = {}
        wo_free = {}
        out_prev = None
        wo_last = None
        for g in range(NG):
            for jc in range(KC):
                bank = (pC, pD)[jc % 2]
                if id(bank) in ctx_free:
                    P.wait("tensor", "mS", ctx_free[id(bank)])
                if g > 0 and jc == 0:
                    P.wait("scalar", "mT", wo_last)
                mm = None
                for sub in range(2):
                    h = jc * 2 + sub
                    mm = P.op("tensor", lambda e, h=h, g=g, bank=bank: e.matmul(
                        bank[64 * (h % 2):64 * (h % 2) + 64, :],
                        m1sb[64 * (h % 2):64 * (h % 2) + 64,
                             (h // 2) * 64:(h // 2) * 64 + 64],
                        arB[64 * (h % 2):64 * (h % 2) + 64,
                            (h // 2) * T + g * 512:(h // 2) * T + (g + 1) * 512],
                        start=True, stop=True, skip_group_check=True,
                        tile_position=(64 * (h % 2), 64 * (h % 2))),
                        inc=("mT", 1) if sub == 1 else None)
                P.wait("scalar", "mT", mm)
                sE = P.op("scalar", lambda e, jc=jc, bank=bank: e.copy(
                    out=stA[:, jc * 512:(jc + 1) * 512], in_=bank[:]), inc=("mS", 1))
                ctx_free[id(bank)] = sE
            ctx_done = P.snap("mS")
            P.wait("tensor", "mS", ctx_done)
            for tt in range(TPG):
                bank = (pA, pB)[tt % 2]
                if id(bank) in wo_free:
                    P.wait("tensor", "mV", wo_free[id(bank)])
                mm = None
                for jc in range(KC):
                    for hh in range(2):
                        P.op("tensor", lambda e, jc=jc, hh=hh, tt=tt, bank=bank:
                             e.matmul(
                                 bank[:, hh * 512:(hh + 1) * 512],
                                 stA[:, jc * 512 + tt * 128:jc * 512 + (tt + 1) * 128],
                                 wbf[:, jc * 1024 + hh * 512:jc * 1024 + (hh + 1) * 512],
                                 start=(jc == 0), stop=False))
                P.op("tensor", lambda e, bank=bank: e.matmul(
                    bank[:, 0:512], ones16[:], brows[:, 3072:3584], start=False,
                    stop=True))
                mm = P.op("tensor", lambda e, bank=bank: e.matmul(
                    bank[:, 512:1024], ones16[:], brows[:, 3584:4096], start=False,
                    stop=True), inc=("mT", 1))
                wo_last = mm
                ti = g * TPG + tt
                xslot = ti % 2
                xt16 = xf16[:, xslot * 2 * D:xslot * 2 * D + D]
                P.wait("sync", "mV", wo_free.get(id(bank)))
                ldx = P.dma("sync", xt16, res_a[ti * 128:(ti + 1) * 128, :], "qR")
                P.wait("vector", "qR", ldx)
                P.wait("vector", "mT", mm)
                P.wait("vector", "qO", out_prev)
                ov = P.op("vector", lambda e, bank=bank, xt16=xt16: e.tensor_tensor(
                    out=tH1[:, 0:D], in0=bank[:], in1=xt16, op=ALU.add),
                    inc=("mV", 1))
                wo_free[id(bank)] = ov
                P.wait("sync", "mV", ov)
                out_prev = P.dma("sync", res_b[ti * 128:(ti + 1) * 128, :],
                                 tH1[:, 0:D], "qO")
        P.wait_all("sync", "qO")

    ffn(x_in, F32, res_a, F16)
    if phases >= 2:
        attention()
    if phases >= 3:
        ffn(res_b, F16, x_out, F32)

    if dbg:
        vdb = P.op("vector", lambda e: e.tensor_copy(out=tmp1[:, 0:1],
                                                     in_=tmp1[:, 0:1]), inc=("mV", 1))
        P.wait("sync", "mV", vdb)
        P.dma("sync", dbg_scl[:, 0:8], scl[:, 0:8], "qO")
        P.wait_all("sync", "qSP")
        P.wait_all("sync", "qB")
        if dbg_big:
            P.dma("sync", dbg_ra, res_a, "qO")
            P.dma("sync", dbg_rb, res_b, "qO")

    if dump_src is not None:
        for s in ("qO", "qSP", "qB", "qSL", "qX", "qLN", "qTR", "qW", "qR"):
            P.wait_all("sync", s)
        P.wait("sync", "mS", P.snap("mS"))
        P.wait("sync", "mV", P.snap("mV"))
        P.wait("sync", "mT", P.snap("mT"))
        srcs = {"res_a": res_a, "res_b": res_b, "ln_h": ln_h, "q_h": q_h}
        xo16 = x_out.bitcast(F16)
        if dump_src in srcs:
            P.dma("sync", xo16[0:T, 0:D], srcs[dump_src], "qO")
        elif dump_src.startswith("sp"):
            t_ = {"spU0": spU, "spG0": spG, "spT0": spT, "spZ0": spZ}[dump_src]
            P.dma("sync", xo16[0:128, 0:T], t_[0], "qO")
        elif dump_src == "scl":
            P.dma("sync", x_out[0:128, 0:8], scl[:, 0:8], "qO")

    for e in ENGS:
        P.wait_all(e, "qO")
        P.wait_all(e, "qSP")
        P.wait_all(e, "qB")

    P.build()
    stk.close()
    return nc


def _rope_tables(T, hd=64):
    scale = np.arange(0, hd, 2, dtype=np.float32)[: hd // 2]
    with np.errstate(over="ignore"):
        denom = np.float32(10000.0) ** scale
    freqs = (np.float32(1.0) / denom).astype(np.float32)
    ang = (np.arange(T, dtype=np.float32)[:, None] * freqs[None, :]).astype(np.float32)
    return np.cos(ang).astype(np.float32), np.sin(ang).astype(np.float32)


def make_inputs(x_core, Wq, bq, Wk, bk, Wv, bv, Wo, bo, w_gate, w_up, w_down, T):
    NTT = T // 128
    cos, sin = _rope_tables(T)
    cos5, sin5 = cos[:, :5], sin[:, :5]
    costok = cos5.reshape(NTT, 128, 5).transpose(1, 0, 2).reshape(128, NTT * 5)
    sintok = sin5.reshape(NTT, 128, 5).transpose(1, 0, 2).reshape(128, NTT * 5)
    return {
        "x_in": np.ascontiguousarray(x_core, dtype=np.float32),
        "w_q": np.ascontiguousarray(Wq, dtype=np.float32),
        "w_k": np.ascontiguousarray(Wk, dtype=np.float32),
        "w_v": np.ascontiguousarray(Wv, dtype=np.float32),
        "w_o": np.ascontiguousarray(Wo, dtype=np.float32),
        "b_all": np.ascontiguousarray(
            np.stack([np.asarray(b, dtype=np.float32) for b in (bq, bk, bv, bo)])),
        "w_up": np.ascontiguousarray(w_up, dtype=np.float32),
        "w_gate": np.ascontiguousarray(w_gate, dtype=np.float32),
        "w_down": np.ascontiguousarray(w_down, dtype=np.float32),
        "costok": np.ascontiguousarray(costok.astype(np.float32)),
        "sintok": np.ascontiguousarray(sintok.astype(np.float32)),
    }


def kernel(L0_attn_x, L0_attn_Wq, L0_attn_bq, L0_attn_Wk, L0_attn_bk,
           L0_attn_Wv, L0_attn_bv, L0_attn_Wo, L0_attn_bo,
           L0_ffn_w_gate, L0_ffn_w_up, L0_ffn_w_down):
    x = np.ascontiguousarray(np.asarray(L0_attn_x, dtype=np.float32))
    B, T, d = x.shape
    assert B == NCORES and d == D
    nc = build_nc(T)
    in_maps = [
        make_inputs(x[i], L0_attn_Wq, L0_attn_bq, L0_attn_Wk, L0_attn_bk,
                    L0_attn_Wv, L0_attn_bv, L0_attn_Wo, L0_attn_bo,
                    L0_ffn_w_gate, L0_ffn_w_up, L0_ffn_w_down, T)
        for i in range(NCORES)
    ]
    res = run_bass_kernel_spmd(nc, in_maps, core_ids=list(range(NCORES)))
    out = np.stack([np.asarray(res.results[i]["x_out"]) for i in range(NCORES)])
    return out.astype(np.float32)


# version 04
# speedup vs baseline: 1.2245x; 1.2245x over previous
import contextlib
import numpy as np
import concourse.bass as bass
import concourse.mybir as mybir
from concourse.bass_utils import run_bass_kernel_spmd

F32 = mybir.dt.float32
F16 = mybir.dt.float16
I32 = mybir.dt.int32
I8 = mybir.dt.int8
AF = mybir.ActivationFunctionType
ALU = mybir.AluOpType
AX = mybir.AxisListType

D = 1024
DI = 3072
H = 16
KC = D // 128
NB = DI // 128
NCORES = 8
QMAX = 32767.0

ENGS = ("sync", "scalar", "vector", "tensor", "gpsimd")


class Prog:

    def __init__(self, nc, stk, paranoid=False):
        self.nc = nc
        self.ops = {e: [] for e in ENGS}
        self.sems = {}
        self.cnt = {}
        self.stk = stk
        self.paranoid = paranoid
        self.gcnt = 0

    def sem(self, *names):
        for n in names:
            self.sems[n] = self.stk.enter_context(self.nc.semaphore(n))
            self.cnt[n] = 0

    def _ser(self, eng):
        if not self.paranoid:
            return
        g = self.sems["gseq"]
        v = self.gcnt
        self.ops[eng].append(lambda e, g=g, v=v: e.wait_ge(g, v))

    def op(self, eng, fn, inc=None):
        if eng == "vector" and not self.paranoid:
            self.wait("vector", "mV", self.cnt["mV"])
            if inc is None:
                inc = ("mV", 1)
            elif inc[0] != "mV":
                raise ValueError("vector ops must inc mV")
        self._ser(eng)
        if self.paranoid:
            g = self.sems["gseq"]
            self.gcnt += 1
            if inc is None:
                self.ops[eng].append(lambda e, fn=fn, g=g: fn(e).then_inc(g, 1))
                return None
            s, by = inc
            self.cnt[s] += by
            sh = self.sems[s]

            def run2(e, fn=fn, sh=sh, by=by, g=g):
                ins = fn(e)
                ins.then_inc(sh, by)
                e.sem_inc(g, 1)
            self.ops[eng].append(run2)
            return self.cnt[s]
        if inc is None:
            self.ops[eng].append(lambda e, fn=fn: fn(e))
            return None
        s, by = inc
        self.cnt[s] += by
        sh = self.sems[s]

        def run(e, fn=fn, sh=sh, by=by):
            fn(e).then_inc(sh, by)

        self.ops[eng].append(run)
        return self.cnt[s]

    def dma(self, eng, out, in_, sem, transpose=False, **kw):
        self._ser(eng)
        self.cnt[sem] += 16
        sh = self.sems[sem]
        if transpose:
            def run(e, out=out, in_=in_, sh=sh):
                e.dma_start_transpose(out=out, in_=in_).then_inc(sh, 16)
        else:
            def run(e, out=out, in_=in_, sh=sh, kw=kw):
                e.dma_start(out=out, in_=in_, **kw).then_inc(sh, 16)
        self.ops[eng].append(run)
        if self.paranoid:
            g = self.sems["gseq"]
            v = self.cnt[sem]
            self.gcnt += 1
            self.ops[eng].append(lambda e, sh=sh, v=v, g=g: (e.wait_ge(sh, v),
                                                            e.sem_inc(g, 1)))
        return self.cnt[sem]


    def opw(self, eng, fn):
        sname = {"vector": "mV", "scalar": "mS", "gpsimd": "cc"}[eng]
        v = self.op(eng, fn, inc=(sname, 1))
        self.wait(eng, sname, v)
        return v

    def wait(self, eng, sem, val):
        if val is None or val <= 0:
            return
        sh = self.sems[sem]
        self.ops[eng].append(lambda e, sh=sh, v=val: e.wait_ge(sh, v))

    def wait_all(self, eng, sem):
        self.wait(eng, sem, self.cnt[sem])

    def snap(self, sem):
        return self.cnt[sem]

    def build(self):
        with self.nc.Block() as block:
            @block.sync
            def _(sync):
                for fn in self.ops["sync"]:
                    fn(sync)

            @block.scalar
            def _(scalar):
                for fn in self.ops["scalar"]:
                    fn(scalar)

            @block.vector
            def _(vector):
                for fn in self.ops["vector"]:
                    fn(vector)

            @block.tensor
            def _(tensor):
                for fn in self.ops["tensor"]:
                    fn(tensor)

            @block.gpsimd
            def _(gpsimd):
                for fn in self.ops["gpsimd"]:
                    fn(gpsimd)


def build_nc(T, dbg=False, phases=3, stop_after=99, p2mode="", dbg_big=False, paranoid=False, parmask="", dump_src=None):
    NTT = T // 128
    NG = T // 512
    TPG = 4
    TH = T // 2

    nc = bass.Bass("TRN2", target_bir_lowering=False, detect_race_conditions=False)
    kdbg = "ExternalOutput" if dbg else "Internal"
    kbig = "ExternalOutput" if (dbg and dbg_big) else "Internal"

    def dram(name, shape, dt, kind="Internal"):
        return nc.dram_tensor(name, shape, dt, kind=kind).ap()

    x_in = dram("x_in", [T, D], F32, "ExternalInput")
    w_q = dram("w_q", [D, D], F32, "ExternalInput")
    w_k = dram("w_k", [D, D], F32, "ExternalInput")
    w_v = dram("w_v", [D, D], F32, "ExternalInput")
    w_o = dram("w_o", [D, D], F32, "ExternalInput")
    b_all = dram("b_all", [4, D], F32, "ExternalInput")
    w_up = dram("w_up", [DI, D], F32, "ExternalInput")
    w_gate = dram("w_gate", [DI, D], F32, "ExternalInput")
    w_down = dram("w_down", [D, DI], F32, "ExternalInput")
    costok = dram("costok", [128, NTT * 5], F32, "ExternalInput")
    sintok = dram("sintok", [128, NTT * 5], F32, "ExternalInput")
    x_out = dram("x_out", [T, D], F32, "ExternalOutput")

    res_a = dram("res_a", [T, D], F16)
    res_b = dram("res_b", [T, D], F16)
    ln_h = dram("ln_h", [T, D], F16)
    spU = dram("spU", [NB, 128, T], F16)
    spG = dram("spG", [NB, 128, T], F16)
    spT = dram("spT", [NB, 128, T], F16)
    spZ = dram("spZ", [NB, 128, T], F16)
    wnat = dram("wnat", [DI * D], F16)
    wupT = dram("wupT", [NB, KC, 128, 128], F16)
    wgT = dram("wgT", [NB, KC, 128, 128], F16)
    wdnT = dram("wdnT", [NB, 128, D], F16)
    wat16 = dram("wat16", [4, D, D], F16)
    q_h = dram("q_h", [T, D], F16)
    dbg_arA = dram("dbg_arA", [128, KC * T], F16, "ExternalOutput") if (dbg and dbg_big) else None
    dbg_ps = dram("dbg_ps", [128, 512], F32, "ExternalOutput") if dbg else None
    dbg_m1 = dram("dbg_m1", [128, 512], F16, "ExternalOutput") if dbg else None
    dbg_acc = dram("dbg_acc", [128, 18], F32, "ExternalOutput") if dbg else None
    dbg_stA = dram("dbg_stA", [128, T], F16, "ExternalOutput") if dbg else None
    bar_i = [dram(f"bar_i{i}", [256], F32) for i in range(8)]
    bar_o = [dram(f"bar_o{i}", [256], F32) for i in range(8)]
    dbg_scl = dram("dbg_scl", [128, 8], F32, "ExternalOutput") if dbg else None
    dbg_ra = dram("dbg_ra", [T, D], F16, "ExternalOutput") if dbg else None
    dbg_rb = dram("dbg_rb", [T, D], F16, "ExternalOutput") if dbg else None
    dbg_lh = dram("dbg_lh", [T, D], F16, "ExternalOutput") if (dbg and dbg_big) else None
    dbg_qh = dram("dbg_qh", [T, D], F16, "ExternalOutput") if (dbg and dbg_big) else None
    dbg_spU = dram("dbg_spU", [NB * 128 * T], F16, "ExternalOutput") if (dbg and dbg_big) else None
    dbg_spG = dram("dbg_spG", [NB * 128 * T], F16, "ExternalOutput") if (dbg and dbg_big) else None
    dbg_spT = dram("dbg_spT", [NB * 128 * T], F16, "ExternalOutput") if (dbg and dbg_big) else None
    dbg_wu = dram("dbg_wu", [NB * KC * 128 * 128], F16, "ExternalOutput") if (dbg and dbg_big) else None

    stk = contextlib.ExitStack()

    def sb(name, shape, dt):
        return stk.enter_context(nc.sbuf_tensor(name, shape, dt))

    AR = 32768
    arA = sb("arA", [128, AR], F16)
    arB = sb("arB", [128, AR], F16)
    stA = sb("stA", [128, 4096], F16)
    stB = sb("stB", [128, 4096], F16)
    rpb = sb("rpb", [128, 4096], F16)
    wbf = sb("wbf", [128, 8192], F16)
    xf = sb("xf", [128, 2 * D], F16)
    acc = sb("acc", [128, 4], F32)
    tmp1 = sb("tmp1", [128, 8], F32)
    scl = sb("scl", [128, 8], F32)
    ctok = sb("ctok", [128, NTT * 5], F32)
    stok = sb("stok", [128, NTT * 5], F32)
    brow = sb("brow", [1, 256], F32)
    gmax = sb("gmax", [1, 8], F32)
    grow = sb("grow", [1, 8], F32)
    ones16 = sb("ones16", [1, 128], F16)
    ones32 = sb("ones32", [1, 128], F32)
    brows = sb("brows", [1, 4 * D], F16)
    m1sb = sb("m1sb", [128, 512], F16)
    sv1 = sb("sv1", [1, 4], F32)
    tI1 = sb("tI1", [128, 2048], I32)
    tI2 = sb("tI2", [128, 1024], I32)
    tF1 = sb("tF1", [128, 2048], F32)
    tH1 = sb("tH1", [128, 1024], F16)

    pA = stk.enter_context(nc.psum_tensor("pA", [128, 1024], F32))
    pB = stk.enter_context(nc.psum_tensor("pB", [128, 1024], F32))
    pC = stk.enter_context(nc.psum_tensor("pC", [128, 512], F32))
    pD = stk.enter_context(nc.psum_tensor("pD", [128, 512], F32))
    pE = stk.enter_context(nc.psum_tensor("pE", [128, 512], F32))
    pF = stk.enter_context(nc.psum_tensor("pF", [128, 512], F32))

    P = Prog(nc, stk, paranoid=paranoid)
    PM = set(parmask.split(",")) if parmask else set()

    def par(tag):
        if not paranoid:
            P.paranoid = (tag in PM) or ("all" in PM)
    P.sem("gseq","qW", "qLN", "qTR", "qSP", "qSL", "qR", "qO", "qB", "qX",
          "qXG", "qRG",
          "mS", "mV", "mT", "cc")

    arA32 = arA.ap().bitcast(F32)
    wdtmp32 = arA32[:, 0:DI]
    wtmpq = arA32[:, DI:2 * DI]
    wq8i = arA.ap().bitcast(I8)[:, 49152:49152 + 2 * DI]
    wsc = arA32[:, 14000:14192]
    wrec = arA32[:, 14208:14400]
    wdtmp32 = arA32[:, 0:DI]

    par("prep1")
    P.opw("vector", lambda e: e.memset(ones16[:], 1.0))
    P.opw("vector", lambda e: e.memset(ones32[:], 1.0))
    P.opw("vector", lambda e: e.memset(scl[:], 0.0))
    P.opw("vector", lambda e: e.memset(tmp1[:], 0.0))
    P.opw("vector", lambda e: e.memset(gmax[:], 0.0))
    P.opw("vector", lambda e: e.memset(grow[:], 0.0))
    P.opw("vector", lambda e: e.memset(brow[:], 0.0))

    for wi, wsrc in enumerate((w_q, w_k, w_v, w_o)):
        for c in range(KC):
            ld = P.dma("sync", wdtmp32[:, 0:D], wsrc[c * 128:(c + 1) * 128, :], "qW")
            P.wait("scalar", "qW", ld)
            ev = P.op("scalar", lambda e: e.copy(out=wbf[:, 0:D], in_=wdtmp32[:, 0:D]),
                      inc=("mS", 1))
            P.wait("gpsimd", "mS", ev)
            st = P.dma("gpsimd", wat16[wi, c * 128:(c + 1) * 128, :], wbf[:, 0:D], "qSP")
            P.wait("sync", "mS", ev)
            P.wait("scalar", "qSP", st)
    for j in range(4):
        ld = P.dma("sync", wdtmp32[0:1, 0:D], b_all[j:j + 1, :], "qW")
        P.wait("scalar", "qW", ld)
        bev = P.op("scalar", lambda e, j=j: e.copy(
            out=brows[:, j * D:(j + 1) * D], in_=wdtmp32[0:1, 0:D]), inc=("mS", 1))
        P.wait("sync", "mS", bev)
    P.dma("sync", ctok[:], costok, "qW")
    P.dma("sync", stok[:], sintok, "qW")
    consts_ld = P.snap("qW")

    def prep_ffn_weight(wsrc, rows, cols, store_T, tag="prep2"):
        nt = rows // 128
        nbl = cols // 32
        wnat_v = wnat[0:rows * cols].rearrange("(r c) -> r c", c=cols)
        par(tag + "a")
        hist = {}
        for t in range(nt):
            sl = t % 2
            wdt = arA32[:, sl * DI:sl * DI + cols]
            wtq = arA32[:, (2 + sl) * DI:(2 + sl) * DI + cols]
            wqd = wq8i[:, sl * DI:sl * DI + cols]
            wbo = wbf[:, sl * 4096:sl * 4096 + cols]
            P.wait("sync", "mV", hist.get(("mv", sl)))
            ld = P.dma("sync", wdt, wsrc[t * 128:(t + 1) * 128, :], "qW")
            P.wait("vector", "qW", ld)
            v3 = wdt.rearrange("p (b i) -> p b i", i=32)
            P.op("vector", lambda e, v3=v3, nbl=nbl, sl=sl: e.tensor_reduce(
                out=wsc[:, sl * 96:sl * 96 + nbl], in_=v3, axis=AX.X, op=ALU.max,
                apply_absolute_value=True))
            P.op("vector", lambda e, nbl=nbl, sl=sl: e.tensor_scalar(
                out=wsc[:, sl * 96:sl * 96 + nbl],
                in0=wsc[:, sl * 96:sl * 96 + nbl],
                scalar1=1.0 / 7.0, scalar2=1e-12, op0=ALU.mult, op1=ALU.max))
            P.op("vector", lambda e, nbl=nbl, sl=sl: e.reciprocal(
                out=wrec[:, sl * 96:sl * 96 + nbl],
                in_=wsc[:, sl * 96:sl * 96 + nbl]))
            rb = wrec[:, sl * 96:sl * 96 + nbl, None].broadcast_to([128, nbl, 32])
            P.wait("vector", "mS", hist.get(("cv", sl)))
            mv = P.op("vector", lambda e, v3=v3, rb=rb, wtq=wtq: e.tensor_tensor(
                out=wtq, in0=v3, in1=rb, op=ALU.mult))
            hist[("mv", sl)] = mv
            P.wait("scalar", "mV", mv)
            cv = P.op("scalar", lambda e, wqd=wqd, wtq=wtq: e.copy(
                out=wqd, in_=wtq), inc=("mS", 1))
            hist[("cv", sl)] = cv
            P.wait("vector", "mS", cv)
            sbb = wsc[:, sl * 96:sl * 96 + nbl, None].broadcast_to([128, nbl, 32])
            q3 = wqd.rearrange("p (b i) -> p b i", i=32)
            P.wait("vector", "qSP", hist.get(("st", sl)))
            dq = P.op("vector", lambda e, q3=q3, sbb=sbb, wbo=wbo: e.tensor_tensor(
                out=wbo, in0=q3, in1=sbb, op=ALU.mult))
            P.wait("gpsimd", "mV", dq)
            st = P.dma("gpsimd", wnat_v[t * 128:(t + 1) * 128, :], wbo, "qSP")
            hist[("st", sl)] = st
        P.wait("sync", "qSP", P.snap("qSP"))
        par(tag + "b")
        store_T(wnat_v)

    def store_T_upgate(dst):
        def go(wnat_v):
            for c in range(KC):
                tr = P.dma("sync", wbf[:, 0:DI], wnat_v[:, c * 128:(c + 1) * 128],
                           "qTR", transpose=True)
                P.wait("gpsimd", "qTR", tr)
                src = wbf.ap()[:, 0:DI].rearrange("p (nb x) -> p nb x", x=128)
                st = P.dma("gpsimd", dst[:, c, :, :].rearrange("nb p x -> p nb x"),
                           src, "qSP")
                P.wait("sync", "qSP", st)
        return go

    def store_T_down(wnat_v):
        for nb2 in range(NB):
            tr = P.dma("sync", wbf[:, 0:D], wnat_v[:, nb2 * 128:(nb2 + 1) * 128],
                       "qTR", transpose=True)
            P.wait("gpsimd", "qTR", tr)
            st = P.dma("gpsimd", wdnT[nb2], wbf[:, 0:D], "qSP")
            P.wait("sync", "qSP", st)

    prep_ffn_weight(w_up, DI, D, store_T_upgate(wupT), "prep2")
    prep_ffn_weight(w_gate, DI, D, store_T_upgate(wgT), "prep3")
    prep_ffn_weight(w_down, D, DI, store_T_down, "prep4")

    prep_done = P.snap("qSP")
    for e in ENGS:
        P.wait(e, "qSP", prep_done)

    LNO = 2048
    QSZ = min(1024, T)
    NQ4 = T // QSZ

    def ln_pass(x_src, x_dtype, need_max):
        if need_max:
            P.opw("vector", lambda e: e.memset(acc[:, 0:1], 0.0))
        prev_free = None
        prev_ve = None
        prev_st = None
        for i in range(NTT):
            slot = i % 2
            xt = xf[:, slot * D:(slot + 1) * D]
            eng_ld = "gpsimd" if x_dtype == F32 else "sync"
            qn = "qXG" if eng_ld == "gpsimd" else "qX"
            if prev_free is not None:
                P.wait(eng_ld, prev_free[0], prev_free[1])
            ld = P.dma(eng_ld, xt, x_src[i * 128:(i + 1) * 128, :], qn)
            P.wait("vector", qn, ld)
            P.wait("scalar", qn, ld)
            P.opw("vector", lambda e, xt=xt: e.reduce_sum(
                out=tmp1[:, 0:1], in_=xt, axis=AX.X))
            P.wait("scalar", "mV", prev_ve)
            sqv = P.op("scalar", lambda e, xt=xt: e.activation(
                out=tF1[:, 0:D], in_=xt, func=AF.Square), inc=("mS", 1))
            P.wait("vector", "mS", sqv)
            P.opw("vector", lambda e: e.reduce_sum(
                out=tmp1[:, 1:2], in_=tF1[:, 0:D], axis=AX.X))
            P.op("vector", lambda e: e.tensor_scalar_mul(
                out=tmp1[:, 2:3], in0=tmp1[:, 0:1], scalar1=1.0 / D))
            P.op("vector", lambda e: e.tensor_mul(
                out=tmp1[:, 3:4], in0=tmp1[:, 2:3], in1=tmp1[:, 2:3]))
            P.op("vector", lambda e: e.scalar_tensor_tensor(
                out=tmp1[:, 4:5], in0=tmp1[:, 1:2], scalar=1.0 / D,
                in1=tmp1[:, 3:4], op0=ALU.mult, op1=ALU.subtract))
            ve = P.op("vector", lambda e: e.tensor_scalar_add(
                out=tmp1[:, 4:5], in0=tmp1[:, 4:5], scalar1=1e-5), inc=("mV", 1))
            prev_ve = ve
            P.wait("scalar", "mV", ve)
            se = P.op("scalar", lambda e: e.activation(
                out=tmp1[:, 5:6], in_=tmp1[:, 4:5], func=AF.Sqrt), inc=("mS", 1))
            P.wait("vector", "mS", se)
            P.opw("vector", lambda e: e.reciprocal(out=tmp1[:, 6:7], in_=tmp1[:, 5:6]))
            vm = P.op("vector", lambda e: e.scalar_tensor_tensor(
                out=tmp1[:, 7:8], in0=tmp1[:, 2:3], scalar=-1.0,
                in1=tmp1[:, 6:7], op0=ALU.mult, op1=ALU.mult), inc=("mV", 1))
            P.wait("scalar", "mV", vm)
            P.wait("scalar", "qLN", prev_st)
            lno = P.op("scalar", lambda e, xt=xt: e.activation(
                out=arB[:, LNO:LNO + D], in_=xt, func=AF.Identity,
                scale=tmp1[:, 6:7], bias=tmp1[:, 7:8]), inc=("mS", 1))
            P.wait("sync", "mS", lno)
            prev_st = P.dma("sync", ln_h[i * 128:(i + 1) * 128, :], arB[:, LNO:LNO + D],
                            "qLN")
            if need_max:
                P.wait("vector", "mS", lno)
                P.opw("vector", lambda e: e.tensor_reduce(
                    out=tmp1[:, 0:1], in_=arB[:, LNO:LNO + D], axis=AX.X, op=ALU.max,
                    apply_absolute_value=True))
                pv = P.op("vector", lambda e: e.tensor_max(
                    out=acc[:, 0:1], in0=acc[:, 0:1], in1=tmp1[:, 0:1]), inc=("mV", 1))
                prev_free = ("mV", pv)
            else:
                prev_free = ("mS", lno)

    bar_n = [0]
    dbg1 = [True]

    def barrier(nq, post):
        bi = bar_n[0]
        bar_n[0] += 1
        n = 128 * nq
        vp = P.op("vector", lambda e, nq=nq: e.tensor_copy(
            out=tmp1[:, 0:nq], in_=acc[:, 0:nq]), inc=("mV", 1))
        P.wait("gpsimd", "mV", vp)
        d1 = P.dma("gpsimd", bar_i[bi][0:n], tmp1[:, 0:nq], "qB")
        P.wait("gpsimd", "qB", d1)
        ccv = P.op("gpsimd", lambda e, bi=bi, n=n: e.collective_compute(
            "AllReduce", ALU.max, replica_groups=[list(range(NCORES))],
            ins=[bar_i[bi][0:n].opt()], outs=[bar_o[bi][0:n].opt()]), inc=("cc", 1))
        P.wait("gpsimd", "cc", ccv)
        d2 = P.dma("gpsimd", brow[:, 0:n], bar_o[bi][0:n], "qB")
        P.wait("vector", "qB", d2)
        for q in range(nq):
            qv = brow[:, q:n:nq]
            P.opw("vector", lambda e, q=q, qv=qv: e.tensor_reduce(
                out=gmax[:, q:q + 1], in_=qv, axis=AX.X, op=ALU.max))
        m = post()
        vb = P.op("vector", lambda e: e.tensor_copy(
            out=grow[:, 7:8], in_=grow[:, 0:1]), inc=("mV", 1))
        P.wait("tensor", "mV", vb)
        P.wait("tensor", "mS", P.snap("mS"))
        P.wait("tensor", "qW", consts_ld)
        tb = P.op("tensor", lambda e, m=m: e.matmul(
            pF[:, 0:m], ones32[:], grow[:, 0:m], start=True, stop=True),
            inc=("mT", 1))
        P.wait("scalar", "mT", tb)
        sc = P.op("scalar", lambda e, m=m: e.copy(out=scl[:, 0:m], in_=pF[:, 0:m]),
                  inc=("mS", 1))
        return sc

    def ffn(x_src, x_dtype, out_dst, out_dtype):
        if stop_after < 1:
            return
        par("ln")
        ln_pass(x_src, x_dtype, True)
        par("bar")
        if stop_after < 2:
            return

        def post1():
            P.op("vector", lambda e: e.tensor_scalar(
                out=gmax[:, 1:2], in0=gmax[:, 0:1], scalar1=1.0 / QMAX,
                scalar2=1e-12, op0=ALU.mult, op1=ALU.max))
            P.opw("vector", lambda e: e.reciprocal(out=grow[:, 0:1], in_=gmax[:, 1:2]))
            P.op("vector", lambda e: e.tensor_copy(out=grow[:, 1:2], in_=gmax[:, 1:2]))
            return 2
        b1 = barrier(1, post1)
        if stop_after < 3:
            return

        par("tr")
        P.wait_all("sync", "qLN")
        for c in range(KC):
            P.dma("sync", arA[:, c * T:(c + 1) * T], ln_h[:, c * 128:(c + 1) * 128],
                  "qTR", transpose=True)
        trs = P.snap("qTR")

        P.wait("scalar", "qTR", trs)
        P.wait("scalar", "mS", b1)
        for c in range(KC):
            for q in range(NQ4):
                o0 = c * T + q * QSZ
                P.op("scalar", lambda e, o0=o0: e.activation(
                    out=tI1[:, 0:QSZ], in_=arA[:, o0:o0 + QSZ], func=AF.Identity,
                    scale=scl[:, 0:1]), inc=("mS", 1))
                P.op("scalar", lambda e, o0=o0: e.activation(
                    out=arA[:, o0:o0 + QSZ], in_=tI1[:, 0:QSZ], func=AF.Identity,
                    scale=scl[:, 1:2]), inc=("mS", 1))
        xq_done = P.snap("mS")
        if stop_after < 4:
            return
        if dbg and dbg_big:
            P.wait("sync", "mS", xq_done)
            P.dma("sync", dbg_arA[:, 0:KC * T], arA[:, 0:KC * T], "qX") if dbg_big else None

        par("p2")
        P.opw("vector", lambda e: e.memset(acc[:, 0:2], 0.0))
        P.wait("tensor", "mS", xq_done)
        bank_free = {}
        st_prev = {"u": None, "g": None}
        slot_mm = [None, None]
        for nb in range(NB):
            wslot = nb % 2
            up_l = wbf[:, wslot * 2048:wslot * 2048 + 1024]
            ga_l = wbf[:, wslot * 2048 + 1024:wslot * 2048 + 2048]
            P.wait("sync", "mT", slot_mm[wslot])
            lu = P.dma("sync", up_l.rearrange("p (c x) -> p c x", c=KC),
                       wupT[nb].rearrange("c p x -> p c x"), "qW")
            lg = P.dma("sync", ga_l.rearrange("p (c x) -> p c x", c=KC),
                       wgT[nb].rearrange("c p x -> p c x"), "qW")
            P.wait("tensor", "qW", lg)
            for g in range(NG):
                bu = (pC, pD)[g % 2]
                bg = (pE, pF)[g % 2]
                for key in ((id(bu)), (id(bg))):
                    if key in bank_free:
                        sH, vH = bank_free[key]
                        P.wait("tensor", "mS", sH)
                        P.wait("tensor", "mV", vH)
                for c in range(KC):
                    P.op("tensor", lambda e, c=c, bu=bu, up_l=up_l, g=g: e.matmul(
                        bu[:], up_l[:, c * 128:(c + 1) * 128],
                        arA[:, c * T + g * 512:c * T + g * 512 + 512],
                        start=(c == 0), stop=(c == KC - 1)),
                        inc=("mT", 1) if c == KC - 1 else None)
                mT_u = P.snap("mT")
                if dbg and dbg1[0] and nb == 0 and g == 0:
                    P.wait("vector", "mT", mT_u)
                    dv = P.op("vector", lambda e, bu=bu: e.tensor_copy(
                        out=tF1[:, 0:512], in_=bu[:]), inc=("mV", 1))
                    P.wait("sync", "mV", dv)
                    P.dma("sync", dbg_ps, tF1[:, 0:512], "qX")
                for c in range(KC):
                    P.op("tensor", lambda e, c=c, bg=bg, ga_l=ga_l, g=g: e.matmul(
                        bg[:], ga_l[:, c * 128:(c + 1) * 128],
                        arA[:, c * T + g * 512:c * T + g * 512 + 512],
                        start=(c == 0), stop=(c == KC - 1)),
                        inc=("mT", 1) if c == KC - 1 else None)
                mT_g = P.snap("mT")
                slot_mm[wslot] = mT_g
                for (tag, bank, stg, accc, mTv) in (
                        ("u", bu, stA, 0, mT_u), ("g", bg, stB, 1, mT_g)):
                    P.wait("scalar", "mT", mTv)
                    if g == 0:
                        P.wait("scalar", "qSP", st_prev[tag])
                    sE = P.op("scalar", lambda e, bank=bank, stg=stg, g=g: e.copy(
                        out=stg[:, g * 512:(g + 1) * 512], in_=bank[:]), inc=("mS", 1))
                    P.wait("vector", "mS", sE)
                    P.opw("vector", lambda e, stg=stg, g=g: e.tensor_reduce(
                        out=tmp1[:, 0:1], in_=stg[:, g * 512:(g + 1) * 512],
                        axis=AX.X, op=ALU.max, apply_absolute_value=True))
                    vE = P.op("vector", lambda e, accc=accc: e.tensor_max(
                        out=acc[:, accc:accc + 1], in0=acc[:, accc:accc + 1],
                        in1=tmp1[:, 0:1]), inc=("mV", 1))
                    bank_free[id(bank)] = (sE, vE)
                    if g == NG - 1 and "nospill" not in p2mode:
                        P.wait("gpsimd", "mS", sE)
                        dst = spU if tag == "u" else spG
                        st_prev[tag] = P.dma("gpsimd", dst[nb], stg[:, 0:T], "qSP")
        p2_sp = P.snap("qSP")
        if stop_after < 5:
            return

        par("bar")

        def post2():
            P.op("vector", lambda e: e.tensor_scalar(
                out=gmax[:, 2:4], in0=gmax[:, 0:2], scalar1=1.0 / QMAX,
                scalar2=1e-12, op0=ALU.mult, op1=ALU.max))
            P.opw("vector", lambda e: e.reciprocal(out=grow[:, 0:1], in_=gmax[:, 2:3]))
            P.opw("vector", lambda e: e.reciprocal(out=grow[:, 1:2], in_=gmax[:, 3:4]))
            P.op("vector", lambda e: e.tensor_copy(out=grow[:, 2:4], in_=gmax[:, 2:4]))
            P.op("vector", lambda e: e.tensor_scalar_mul(
                out=grow[:, 4:5], in0=gmax[:, 3:4], scalar1=1.0 / 65536.0))
            return 5
        b2 = barrier(2, post2)
        import os as _os
        if dbg and dbg1[0] and _os.environ.get('DBG2','1')=='1':
            dbg1[0] = False
            vd = P.op("vector", lambda e: e.tensor_copy(out=tmp1[:, 0:2],
                                                        in_=acc[:, 0:2]), inc=("mV", 1))
            P.wait("sync", "mV", vd)
            P.dma("sync", dbg_acc[:, 0:2], tmp1[:, 0:2], "qX")
            P.dma("sync", dbg_acc[0:1, 2:10], gmax[:, 0:8], "qX")
            P.dma("sync", dbg_acc[0:1, 10:18], grow[:, 0:8], "qX")
            P.wait("scalar", "mS", b2)
            P.dma("sync", dbg_stA, stA[:, 0:T], "qX")
        if stop_after < 6:
            return
        P.op("vector", lambda e: e.tensor_copy(out=sv1[:, 0:1], in_=grow[:, 4:5]))
        P.op("vector", lambda e: e.tensor_copy(out=sv1[:, 1:2], in_=grow[:, 0:1]))
        P.op("vector", lambda e: e.tensor_copy(out=sv1[:, 2:3], in_=grow[:, 2:3]))

        par("p3")
        P.opw("vector", lambda e: e.memset(acc[:, 0:1], 0.0))
        P.wait("sync", "qSP", p2_sp)
        P.wait("scalar", "mS", b2)
        st_hist = [None, None]
        ld_p = [None, None]
        vfree = {}
        i2f = None
        qg = 0
        for nb in range(NB):
            slot = nb % 2
            gbuf = arA[:, slot * T:(slot + 1) * T]
            stg = stA[:, 0:T] if slot == 0 else rpb[:, 0:T]
            P.wait("sync", "mS", ld_p[slot])
            ldg = P.dma("sync", gbuf, spG[nb], "qSL")
            P.wait("scalar", "qSL", ldg)
            P.wait("scalar", "qSP", st_hist[slot])
            s4 = None
            for q in range(NQ4):
                q0 = q * QSZ
                ts = qg % 2
                i1 = tI1[:, ts * 1024:ts * 1024 + QSZ]
                i2 = tI2[:, 0:QSZ]
                f1 = tF1[:, ts * 1024:ts * 1024 + QSZ]
                P.wait("scalar", "mV", vfree.get(ts))
                P.op("scalar", lambda e, gbuf=gbuf, q0=q0, i1=i1: e.activation(
                    out=i1, in_=gbuf[:, q0:q0 + QSZ], func=AF.Identity,
                    scale=scl[:, 1:2]), inc=("mS", 1))
                P.op("scalar", lambda e, i1=i1, f1=f1: e.activation(
                    out=f1, in_=i1, func=AF.Sigmoid, scale=scl[:, 3:4]), inc=("mS", 1))
                P.wait("scalar", "mV", i2f)
                c3 = P.op("scalar", lambda e, f1=f1, i2=i2: e.activation(
                    out=i2, in_=f1, func=AF.Identity, scale=65536.0), inc=("mS", 1))
                P.wait("vector", "mS", c3)
                i2f = P.op("vector", lambda e, i1=i1, i2=i2, f1=f1: e.tensor_tensor(
                    out=f1, in0=i1, in1=i2, op=ALU.mult))
                P.op("vector", lambda e, f1=f1: e.tensor_reduce(
                    out=tmp1[:, 0:1], in_=f1, axis=AX.X, op=ALU.max,
                    apply_absolute_value=True))
                vmark = P.op("vector", lambda e: e.tensor_max(
                    out=acc[:, 0:1], in0=acc[:, 0:1], in1=tmp1[:, 0:1]))
                P.wait("scalar", "mV", vmark)
                s4 = P.op("scalar", lambda e, q0=q0, f1=f1, stg=stg: e.activation(
                    out=stg[:, q0:q0 + QSZ], in_=f1, func=AF.Identity,
                    scale=scl[:, 4:5]), inc=("mS", 1))
                vfree[ts] = vmark
                qg += 1
            ld_p[slot] = P.cnt["mS"]
            P.wait("gpsimd", "mS", s4)
            st_hist[slot] = P.dma("gpsimd", spT[nb], stg, "qSP")
        p3_sp = P.snap("qSP")
        if stop_after < 7:
            return

        par("bar")

        def post3():
            P.op("vector", lambda e: e.tensor_mul(
                out=gmax[:, 1:2], in0=gmax[:, 0:1], in1=sv1[:, 0:1]))
            P.op("vector", lambda e: e.tensor_scalar(
                out=gmax[:, 2:3], in0=gmax[:, 1:2], scalar1=1.0 / QMAX,
                scalar2=1e-12, op0=ALU.mult, op1=ALU.max))
            P.opw("vector", lambda e: e.reciprocal(out=grow[:, 0:1], in_=gmax[:, 2:3]))
            P.op("vector", lambda e: e.tensor_copy(out=grow[:, 1:2], in_=gmax[:, 2:3]))
            P.op("vector", lambda e: e.tensor_copy(out=grow[:, 2:3], in_=sv1[:, 1:2]))
            P.op("vector", lambda e: e.tensor_copy(out=grow[:, 3:4], in_=sv1[:, 2:3]))
            return 4
        b3 = barrier(1, post3)
        P.op("vector", lambda e: e.tensor_copy(out=sv1[:, 3:4], in_=grow[:, 1:2]))

        par("p4")
        P.opw("vector", lambda e: e.memset(acc[:, 0:1], 0.0))
        P.wait("sync", "qSP", p3_sp)
        P.wait("scalar", "mS", b3)
        st_hist = [None, None]
        ld_p4 = [None, None]
        vfree = {}
        h1f = None
        qg = 0
        for nb in range(NB):
            slot = nb % 2
            tbuf = arA[:, slot * T:(slot + 1) * T]
            ubuf = arA[:, (2 + slot) * T:(3 + slot) * T]
            stg = stB[:, 0:T] if slot == 0 else arB[:, 0:T]
            P.wait("sync", "mS", ld_p4[slot])
            P.dma("sync", tbuf, spT[nb], "qSL")
            ldu = P.dma("sync", ubuf, spU[nb], "qSL")
            P.wait("scalar", "qSL", ldu)
            P.wait("scalar", "qSP", st_hist[slot])
            s4 = None
            for q in range(NQ4):
                q0 = q * QSZ
                ts = qg % 2
                i1 = tI1[:, ts * 1024:ts * 1024 + QSZ]
                i2 = tI2[:, 0:QSZ]
                f1 = tF1[:, ts * 1024:ts * 1024 + QSZ]
                h1 = tH1[:, 0:QSZ]
                P.wait("scalar", "mV", vfree.get(ts))
                P.op("scalar", lambda e, tbuf=tbuf, q0=q0, i1=i1: e.activation(
                    out=i1, in_=tbuf[:, q0:q0 + QSZ], func=AF.Identity,
                    scale=scl[:, 0:1]), inc=("mS", 1))
                P.op("scalar", lambda e, ubuf=ubuf, q0=q0, i2=i2: e.activation(
                    out=i2, in_=ubuf[:, q0:q0 + QSZ], func=AF.Identity,
                    scale=scl[:, 2:3]), inc=("mS", 1))
                P.wait("scalar", "mV", h1f)
                c3 = P.op("scalar", lambda e, i2=i2, h1=h1: e.activation(
                    out=h1, in_=i2, func=AF.Identity, scale=scl[:, 3:4]), inc=("mS", 1))
                P.wait("vector", "mS", c3)
                h1f = P.op("vector", lambda e, i1=i1, h1=h1, f1=f1: e.tensor_tensor(
                    out=f1, in0=i1, in1=h1, op=ALU.mult))
                P.op("vector", lambda e, f1=f1: e.tensor_reduce(
                    out=tmp1[:, 0:1], in_=f1, axis=AX.X, op=ALU.max,
                    apply_absolute_value=True))
                vmark = P.op("vector", lambda e: e.tensor_max(
                    out=acc[:, 0:1], in0=acc[:, 0:1], in1=tmp1[:, 0:1]))
                P.wait("scalar", "mV", vmark)
                s4 = P.op("scalar", lambda e, q0=q0, f1=f1, stg=stg: e.activation(
                    out=stg[:, q0:q0 + QSZ], in_=f1, func=AF.Identity,
                    scale=scl[:, 1:2]), inc=("mS", 1))
                vfree[ts] = vmark
                qg += 1
            ld_p4[slot] = P.cnt["mS"]
            P.wait("gpsimd", "mS", s4)
            st_hist[slot] = P.dma("gpsimd", spZ[nb], stg, "qSP")
        p4_sp = P.snap("qSP")
        if stop_after < 8:
            return

        par("bar")

        def post4():
            P.op("vector", lambda e: e.tensor_mul(
                out=gmax[:, 1:2], in0=gmax[:, 0:1], in1=sv1[:, 3:4]))
            P.op("vector", lambda e: e.tensor_scalar(
                out=gmax[:, 2:3], in0=gmax[:, 1:2], scalar1=1.0 / QMAX,
                scalar2=1e-12, op0=ALU.mult, op1=ALU.max))
            P.opw("vector", lambda e: e.reciprocal(out=grow[:, 0:1], in_=gmax[:, 2:3]))
            P.op("vector", lambda e: e.tensor_scalar_mul(
                out=grow[:, 1:2], in0=gmax[:, 2:3], scalar1=0.5))
            return 2
        b4 = barrier(1, post4)

        par("p5")
        P.wait("sync", "qSP", p4_sp)
        for nb2 in range(NB):
            P.dma("sync", arA[:, nb2 * D:nb2 * D + D], wdnT[nb2], "qW")
        wdn_ld = P.snap("qW")
        P.wait("tensor", "qW", wdn_ld)
        P.wait("scalar", "mS", b4)
        OBUF = 2048
        o32_v = tI1[:, 0:512]
        zs0 = AR - 1024
        down_free = {}
        out_prev = None
        for g in range(NG):
            ocs = []
            for nb in range(NB):
                zslot = nb % 2
                zb = arB[:, zs0 + zslot * 512:zs0 + (zslot + 1) * 512]
                P.wait("sync", "mS", ocs[nb - 2] if nb >= 2 else None)
                ldz = P.dma("sync", zb, spZ[nb][:, g * 512:(g + 1) * 512], "qSL")
                P.wait("scalar", "qSL", ldz)
                P.op("scalar", lambda e, zb=zb: e.activation(
                    out=o32_v, in_=zb, func=AF.Identity, scale=scl[:, 0:1]),
                    inc=("mS", 1))
                ocs.append(P.op("scalar", lambda e, nb=nb: e.activation(
                    out=arB[:, OBUF + nb * 512:OBUF + (nb + 1) * 512], in_=o32_v,
                    func=AF.Copy), inc=("mS", 1)))
            for tt in range(TPG):
                bank = (pA, pB)[tt % 2]
                if id(bank) in down_free:
                    P.wait("tensor", "mV", down_free[id(bank)])
                mmv = None
                for nb in range(NB):
                    P.wait("tensor", "mS", ocs[nb])
                    lhs = arB[:, OBUF + nb * 512 + tt * 128:OBUF + nb * 512 + (tt + 1) * 128]
                    for hh in range(2):
                        mmv = P.op("tensor", lambda e, bank=bank, lhs=lhs, nb=nb, hh=hh:
                                   e.matmul(bank[:, hh * 512:(hh + 1) * 512], lhs,
                                            arA[:, nb * D + hh * 512:nb * D + (hh + 1) * 512],
                                            start=(nb == 0), stop=(nb == NB - 1)),
                                   inc=("mT", 1) if (nb == NB - 1 and hh == 1) else None)
                ti = g * TPG + tt
                xslot = ti % 2
                xin_ap = xf[:, xslot * D:(xslot + 1) * D]
                eng_ld = "gpsimd" if x_dtype == F32 else "sync"
                qn = "qRG" if eng_ld == "gpsimd" else "qR"
                P.wait(eng_ld, "mV", down_free.get(id(bank)))
                ldx = P.dma(eng_ld, xin_ap, x_src[ti * 128:(ti + 1) * 128, :], qn)
                P.wait("vector", qn, ldx)
                P.wait("vector", "mT", mmv)
                P.wait("vector", "qO", out_prev)
                out_ap = tF1[:, 1024:1024 + D] if out_dtype == F32 else tH1[:, 0:D]
                ov = P.op("vector", lambda e, bank=bank, xin_ap=xin_ap, out_ap=out_ap:
                          e.scalar_tensor_tensor(
                              out=out_ap, in0=bank[:], scalar=scl[:, 1:2],
                              in1=xin_ap, op0=ALU.mult, op1=ALU.add), inc=("mV", 1))
                down_free[id(bank)] = ov
                P.wait("sync", "mV", ov)
                out_prev = P.dma("sync", out_dst[ti * 128:(ti + 1) * 128, :], out_ap,
                                 "qO")
        P.wait_all("sync", "qO")

    def attention():
        par("ln")
        ln_pass(res_a, F16, False)
        par("attq")
        P.wait_all("sync", "qLN")
        for c in range(KC):
            P.dma("sync", arA[:, c * T:(c + 1) * T], ln_h[:, c * 128:(c + 1) * 128],
                  "qTR", transpose=True)
        trs = P.snap("qTR")
        P.wait("tensor", "qTR", trs)

        def rope_tile(buf16, i):
            b3 = buf16.rearrange("p (h x) -> p h x", h=H)
            t_o = rpb[:, 3 * D:3 * D + 16][:, :, None]
            t_e = rpb[:, 3 * D + 16:3 * D + 32][:, :, None]
            for p5 in range(5):
                cp = ctok[:, i * 5 + p5:i * 5 + p5 + 1]
                sp = stok[:, i * 5 + p5:i * 5 + p5 + 1]
                ke = b3[:, :, 2 * p5:2 * p5 + 1]
                ko = b3[:, :, 2 * p5 + 1:2 * p5 + 2]
                P.op("vector", lambda e, ko=ko, sp=sp: e.tensor_scalar_mul(
                    out=t_o, in0=ko, scalar1=sp))
                P.op("vector", lambda e, ke=ke, sp=sp: e.tensor_scalar_mul(
                    out=t_e, in0=ke, scalar1=sp))
                P.op("vector", lambda e, ke=ke, cp=cp: e.scalar_tensor_tensor(
                    out=ke, in0=ke, scalar=cp, in1=t_o, op0=ALU.mult,
                    op1=ALU.subtract))
                P.op("vector", lambda e, ko=ko, cp=cp: e.scalar_tensor_tensor(
                    out=ko, in0=ko, scalar=cp, in1=t_e, op0=ALU.mult, op1=ALU.add))

        wq_l = wbf[:, 0:KC * 1024]
        lw = P.dma("sync", wq_l.rearrange("p (c x) -> p c x", c=KC),
                   wat16[0].rearrange("(c p) x -> p c x", p=128), "qW")
        P.wait("tensor", "qW", lw)
        q16 = rpb[:, 2 * D:3 * D]
        qb_free = {}
        vq = None
        st_q = None
        for i in range(NTT):
            bank = (pA, pB)[i % 2]
            if id(bank) in qb_free:
                P.wait("tensor", "mS", qb_free[id(bank)])
            for c in range(KC):
                for hh in range(2):
                    P.op("tensor", lambda e, c=c, hh=hh, i=i, bank=bank: e.matmul(
                        bank[:, hh * 512:(hh + 1) * 512],
                        arA[:, c * T + i * 128:c * T + (i + 1) * 128],
                        wq_l[:, c * 1024 + hh * 512:c * 1024 + (hh + 1) * 512],
                        start=(c == 0), stop=False))
            P.op("tensor", lambda e, bank=bank: e.matmul(
                bank[:, 0:512], ones16[:], brows[:, 0:512], start=False, stop=True))
            qmm = P.op("tensor", lambda e, bank=bank: e.matmul(
                bank[:, 512:1024], ones16[:], brows[:, 512:1024], start=False,
                stop=True), inc=("mT", 1))
            P.wait("scalar", "mT", qmm)
            P.wait("scalar", "mV", vq)
            P.wait("scalar", "qSP", st_q)
            sQ = P.op("scalar", lambda e, bank=bank: e.copy(out=q16, in_=bank[:]),
                      inc=("mS", 1))
            qb_free[id(bank)] = sQ
            P.wait("vector", "mS", sQ)
            rope_tile(q16, i)
            vq = P.op("vector", lambda e: e.tensor_copy(
                out=tmp1[:, 1:2], in_=tmp1[:, 1:2]), inc=("mV", 1))
            P.wait("gpsimd", "mV", vq)
            st_q = P.dma("gpsimd", q_h[i * 128:(i + 1) * 128, :], q16, "qSP")
        P.wait("sync", "qSP", P.snap("qSP"))
        for c in range(KC):
            P.dma("sync", arB[:, c * T:(c + 1) * T], q_h[:, c * 128:(c + 1) * 128],
                  "qTR", transpose=True)
        trsQ = P.snap("qTR")

        par("attkv")
        P.wait("sync", "mT", P.snap("mT"))
        lwk = P.dma("sync", wbf[:, 0:KC * 1024].rearrange("p (c x) -> p c x", c=KC),
                    wat16[1].rearrange("(c p) x -> p c x", p=128), "qW")
        lv1 = P.dma("sync", stA[:, 0:4096].rearrange("p (c x) -> p c x", c=4),
                    wat16[2].rearrange("(c p) x -> c p x", p=128)[0:4]
                    .rearrange("c p x -> p c x"), "qW")
        lv2 = P.dma("sync", stB[:, 0:4096].rearrange("p (c x) -> p c x", c=4),
                    wat16[2].rearrange("(c p) x -> c p x", p=128)[4:8]
                    .rearrange("c p x -> p c x"), "qW")
        P.wait("tensor", "qW", lv2)
        k16 = rpb[:, 0:D]
        v16 = rpb[:, D:2 * D]
        kv_free = None
        m1_tile = None
        for i in range(NTT):
            if kv_free is not None:
                P.wait("tensor", "mV", kv_free)
            for c in range(KC):
                for hh in range(2):
                    P.op("tensor", lambda e, c=c, hh=hh, i=i: e.matmul(
                        pA[:, hh * 512:(hh + 1) * 512],
                        arA[:, c * T + i * 128:c * T + (i + 1) * 128],
                        wbf[:, c * 1024 + hh * 512:c * 1024 + (hh + 1) * 512],
                        start=(c == 0), stop=False))
            P.op("tensor", lambda e: e.matmul(
                pA[:, 0:512], ones16[:], brows[:, 1024:1536], start=False, stop=True))
            mk = P.op("tensor", lambda e: e.matmul(
                pA[:, 512:1024], ones16[:], brows[:, 1536:2048], start=False,
                stop=True), inc=("mT", 1))
            for c in range(KC):
                wsrc = stA if c < 4 else stB
                cc = c % 4
                for hh in range(2):
                    P.op("tensor", lambda e, c=c, cc=cc, hh=hh, i=i, wsrc=wsrc:
                         e.matmul(
                             pB[:, hh * 512:(hh + 1) * 512],
                             arA[:, c * T + i * 128:c * T + (i + 1) * 128],
                             wsrc[:, cc * 1024 + hh * 512:cc * 1024 + (hh + 1) * 512],
                             start=(c == 0), stop=False))
            P.op("tensor", lambda e: e.matmul(
                pB[:, 0:512], ones16[:], brows[:, 2048:2560], start=False, stop=True))
            mv_ = P.op("tensor", lambda e: e.matmul(
                pB[:, 512:1024], ones16[:], brows[:, 2560:3072], start=False,
                stop=True), inc=("mT", 1))
            P.wait("scalar", "mT", mv_)
            if kv_free is not None:
                P.wait("scalar", "mV", kv_free)
            if i > 0:
                P.wait("scalar", "mT", m1_tile)
            P.op("scalar", lambda e: e.copy(out=k16, in_=pA[:]), inc=("mS", 1))
            sV = P.op("scalar", lambda e: e.copy(out=v16, in_=pB[:]), inc=("mS", 1))
            P.wait("vector", "mS", sV)
            k3 = k16.rearrange("p (h x) -> p h x", h=H)
            for p5 in range(5):
                cp = ctok[:, i * 5 + p5:i * 5 + p5 + 1]
                sp = stok[:, i * 5 + p5:i * 5 + p5 + 1]
                ke = k3[:, :, 2 * p5:2 * p5 + 1]
                ko = k3[:, :, 2 * p5 + 1:2 * p5 + 2]
                t_o = rpb[:, 2 * D:2 * D + 16][:, :, None]
                t_e = rpb[:, 2 * D + 16:2 * D + 32][:, :, None]
                P.op("vector", lambda e, ko=ko, sp=sp, t_o=t_o: e.tensor_scalar_mul(
                    out=t_o, in0=ko, scalar1=sp))
                P.op("vector", lambda e, ke=ke, sp=sp, t_e=t_e: e.tensor_scalar_mul(
                    out=t_e, in0=ke, scalar1=sp))
                P.op("vector", lambda e, ke=ke, cp=cp, t_o=t_o: e.scalar_tensor_tensor(
                    out=ke, in0=ke, scalar=cp, in1=t_o, op0=ALU.mult,
                    op1=ALU.subtract))
                P.op("vector", lambda e, ko=ko, cp=cp, t_e=t_e: e.scalar_tensor_tensor(
                    out=ko, in0=ko, scalar=cp, in1=t_e, op0=ALU.mult, op1=ALU.add))
            kv_free = P.op("vector", lambda e: e.tensor_copy(
                out=tmp1[:, 0:1], in_=tmp1[:, 0:1]), inc=("mV", 1))
            P.wait("tensor", "mV", kv_free)
            for h in range(H):
                P.op("tensor", lambda e, h=h, i=i: e.matmul(
                    pE[64 * (h % 2):64 * (h % 2) + 64,
                       (h // 2) * 64:(h // 2) * 64 + 64],
                    k16[:, h * 64:(h + 1) * 64], v16[:, h * 64:(h + 1) * 64],
                    start=(i == 0 and h <= 1), stop=(i == NTT - 1),
                    skip_group_check=True),
                    inc=("mT", 1) if h == H - 1 else None)
            m1_tile = P.snap("mT")
        m1_mm = P.snap("mT")
        P.wait("scalar", "mT", m1_mm)
        sM = P.op("scalar", lambda e: e.activation(
            out=m1sb[:], in_=pE[:], func=AF.Copy, scale=0.125), inc=("mS", 1))
        if dbg and dbg_big:
            P.wait("sync", "mS", sM)
            P.dma("sync", dbg_m1, m1sb[:], "qX")

        par("attc")
        P.wait("sync", "mT", m1_mm)
        lwo = P.dma("sync", wbf[:, 0:KC * 1024].rearrange("p (c x) -> p c x", c=KC),
                    wat16[3].rearrange("(c p) x -> p c x", p=128), "qW")
        P.wait("tensor", "qW", lwo)
        P.wait("tensor", "qTR", trsQ)
        P.wait("tensor", "mS", sM)
        ctx_free = {}
        wo_free = {}
        out_prev = None
        wo_last = None
        for g in range(NG):
            for jc in range(KC):
                bank = (pC, pD)[jc % 2]
                if id(bank) in ctx_free:
                    P.wait("tensor", "mS", ctx_free[id(bank)])
                if g > 0 and jc == 0:
                    P.wait("scalar", "mT", wo_last)
                mm = None
                for sub in range(2):
                    h = jc * 2 + sub
                    mm = P.op("tensor", lambda e, h=h, g=g, bank=bank: e.matmul(
                        bank[64 * (h % 2):64 * (h % 2) + 64, :],
                        m1sb[64 * (h % 2):64 * (h % 2) + 64,
                             (h // 2) * 64:(h // 2) * 64 + 64],
                        arB[64 * (h % 2):64 * (h % 2) + 64,
                            (h // 2) * T + g * 512:(h // 2) * T + (g + 1) * 512],
                        start=True, stop=True, skip_group_check=True,
                        tile_position=(64 * (h % 2), 64 * (h % 2))),
                        inc=("mT", 1) if sub == 1 else None)
                P.wait("scalar", "mT", mm)
                sE = P.op("scalar", lambda e, jc=jc, bank=bank: e.copy(
                    out=stA[:, jc * 512:(jc + 1) * 512], in_=bank[:]), inc=("mS", 1))
                ctx_free[id(bank)] = sE
            ctx_done = P.snap("mS")
            P.wait("tensor", "mS", ctx_done)
            for tt in range(TPG):
                bank = (pA, pB)[tt % 2]
                if id(bank) in wo_free:
                    P.wait("tensor", "mV", wo_free[id(bank)])
                mm = None
                for jc in range(KC):
                    for hh in range(2):
                        P.op("tensor", lambda e, jc=jc, hh=hh, tt=tt, bank=bank:
                             e.matmul(
                                 bank[:, hh * 512:(hh + 1) * 512],
                                 stA[:, jc * 512 + tt * 128:jc * 512 + (tt + 1) * 128],
                                 wbf[:, jc * 1024 + hh * 512:jc * 1024 + (hh + 1) * 512],
                                 start=(jc == 0), stop=False))
                P.op("tensor", lambda e, bank=bank: e.matmul(
                    bank[:, 0:512], ones16[:], brows[:, 3072:3584], start=False,
                    stop=True))
                mm = P.op("tensor", lambda e, bank=bank: e.matmul(
                    bank[:, 512:1024], ones16[:], brows[:, 3584:4096], start=False,
                    stop=True), inc=("mT", 1))
                wo_last = mm
                ti = g * TPG + tt
                xslot = ti % 2
                xt16 = xf[:, xslot * D:(xslot + 1) * D]
                P.wait("sync", "mV", wo_free.get(id(bank)))
                ldx = P.dma("sync", xt16, res_a[ti * 128:(ti + 1) * 128, :], "qR")
                P.wait("vector", "qR", ldx)
                P.wait("vector", "mT", mm)
                P.wait("vector", "qO", out_prev)
                ov = P.op("vector", lambda e, bank=bank, xt16=xt16: e.tensor_tensor(
                    out=tH1[:, 0:D], in0=bank[:], in1=xt16, op=ALU.add),
                    inc=("mV", 1))
                wo_free[id(bank)] = ov
                P.wait("sync", "mV", ov)
                out_prev = P.dma("sync", res_b[ti * 128:(ti + 1) * 128, :],
                                 tH1[:, 0:D], "qO")
        P.wait_all("sync", "qO")

    ffn(x_in, F32, res_a, F16)
    if phases >= 2:
        attention()
    if phases >= 3:
        ffn(res_b, F16, x_out, F32)

    if dbg:
        vdb = P.op("vector", lambda e: e.tensor_copy(out=tmp1[:, 0:1],
                                                     in_=tmp1[:, 0:1]), inc=("mV", 1))
        P.wait("sync", "mV", vdb)
        P.dma("sync", dbg_scl[:, 0:8], scl[:, 0:8], "qO")
        P.wait_all("sync", "qSP")
        P.wait_all("sync", "qB")
        if dbg_big:
            P.dma("sync", dbg_ra, res_a, "qO")
            P.dma("sync", dbg_rb, res_b, "qO")

    if dump_src is not None:
        for s in ("qO", "qSP", "qB", "qSL", "qX", "qLN", "qTR", "qW", "qR",
                  "qXG", "qRG"):
            P.wait_all("sync", s)
        P.wait("sync", "mS", P.snap("mS"))
        P.wait("sync", "mV", P.snap("mV"))
        P.wait("sync", "mT", P.snap("mT"))
        srcs = {"res_a": res_a, "res_b": res_b, "ln_h": ln_h, "q_h": q_h}
        xo16 = x_out.bitcast(F16)
        if dump_src in srcs:
            P.dma("sync", xo16[0:T, 0:D], srcs[dump_src], "qO")
        elif dump_src.startswith("sp"):
            t_ = {"spU0": spU, "spG0": spG, "spT0": spT, "spZ0": spZ}[dump_src]
            P.dma("sync", xo16[0:128, 0:T], t_[0], "qO")
        elif dump_src == "scl":
            P.dma("sync", x_out[0:128, 0:8], scl[:, 0:8], "qO")

    for e in ENGS:
        P.wait_all(e, "qO")
        P.wait_all(e, "qSP")
        P.wait_all(e, "qB")

    P.build()
    stk.close()
    return nc


def _rope_tables(T, hd=64):
    scale = np.arange(0, hd, 2, dtype=np.float32)[: hd // 2]
    with np.errstate(over="ignore"):
        denom = np.float32(10000.0) ** scale
    freqs = (np.float32(1.0) / denom).astype(np.float32)
    ang = (np.arange(T, dtype=np.float32)[:, None] * freqs[None, :]).astype(np.float32)
    return np.cos(ang).astype(np.float32), np.sin(ang).astype(np.float32)


def make_inputs(x_core, Wq, bq, Wk, bk, Wv, bv, Wo, bo, w_gate, w_up, w_down, T):
    NTT = T // 128
    cos, sin = _rope_tables(T)
    cos5, sin5 = cos[:, :5], sin[:, :5]
    costok = cos5.reshape(NTT, 128, 5).transpose(1, 0, 2).reshape(128, NTT * 5)
    sintok = sin5.reshape(NTT, 128, 5).transpose(1, 0, 2).reshape(128, NTT * 5)
    return {
        "x_in": np.ascontiguousarray(x_core, dtype=np.float32),
        "w_q": np.ascontiguousarray(Wq, dtype=np.float32),
        "w_k": np.ascontiguousarray(Wk, dtype=np.float32),
        "w_v": np.ascontiguousarray(Wv, dtype=np.float32),
        "w_o": np.ascontiguousarray(Wo, dtype=np.float32),
        "b_all": np.ascontiguousarray(
            np.stack([np.asarray(b, dtype=np.float32) for b in (bq, bk, bv, bo)])),
        "w_up": np.ascontiguousarray(w_up, dtype=np.float32),
        "w_gate": np.ascontiguousarray(w_gate, dtype=np.float32),
        "w_down": np.ascontiguousarray(w_down, dtype=np.float32),
        "costok": np.ascontiguousarray(costok.astype(np.float32)),
        "sintok": np.ascontiguousarray(sintok.astype(np.float32)),
    }


def kernel(L0_attn_x, L0_attn_Wq, L0_attn_bq, L0_attn_Wk, L0_attn_bk,
           L0_attn_Wv, L0_attn_bv, L0_attn_Wo, L0_attn_bo,
           L0_ffn_w_gate, L0_ffn_w_up, L0_ffn_w_down):
    x = np.ascontiguousarray(np.asarray(L0_attn_x, dtype=np.float32))
    B, T, d = x.shape
    assert B == NCORES and d == D
    nc = build_nc(T)
    in_maps = [
        make_inputs(x[i], L0_attn_Wq, L0_attn_bq, L0_attn_Wk, L0_attn_bk,
                    L0_attn_Wv, L0_attn_bv, L0_attn_Wo, L0_attn_bo,
                    L0_ffn_w_gate, L0_ffn_w_up, L0_ffn_w_down, T)
        for i in range(NCORES)
    ]
    res = run_bass_kernel_spmd(nc, in_maps, core_ids=list(range(NCORES)))
    out = np.stack([np.asarray(res.results[i]["x_out"]) for i in range(NCORES)])
    return out.astype(np.float32)


# version 05
# speedup vs baseline: 1.4348x; 1.1718x over previous
import contextlib
import numpy as np
import concourse.bass as bass
import concourse.mybir as mybir
from concourse.bass_utils import run_bass_kernel_spmd

F32 = mybir.dt.float32
F16 = mybir.dt.float16
I32 = mybir.dt.int32
I8 = mybir.dt.int8
AF = mybir.ActivationFunctionType
ALU = mybir.AluOpType
AX = mybir.AxisListType

D = 1024
DI = 3072
H = 16
KC = D // 128
NB = DI // 128
NCORES = 8
QMAX = 32767.0

ENGS = ("sync", "scalar", "vector", "tensor", "gpsimd")


class Prog:

    def __init__(self, nc, stk, paranoid=False):
        self.nc = nc
        self.ops = {e: [] for e in ENGS}
        self.sems = {}
        self.cnt = {}
        self.stk = stk
        self.paranoid = paranoid
        self.gcnt = 0

    def sem(self, *names):
        for n in names:
            self.sems[n] = self.stk.enter_context(self.nc.semaphore(n))
            self.cnt[n] = 0

    def _ser(self, eng):
        if not self.paranoid:
            return
        g = self.sems["gseq"]
        v = self.gcnt
        self.ops[eng].append(lambda e, g=g, v=v: e.wait_ge(g, v))

    def op(self, eng, fn, inc=None):
        if eng == "vector" and not self.paranoid:
            self.wait("vector", "mV", self.cnt["mV"])
            if inc is None:
                inc = ("mV", 1)
            elif inc[0] != "mV":
                raise ValueError("vector ops must inc mV")
        self._ser(eng)
        if self.paranoid:
            g = self.sems["gseq"]
            self.gcnt += 1
            if inc is None:
                self.ops[eng].append(lambda e, fn=fn, g=g: fn(e).then_inc(g, 1))
                return None
            s, by = inc
            self.cnt[s] += by
            sh = self.sems[s]

            def run2(e, fn=fn, sh=sh, by=by, g=g):
                ins = fn(e)
                ins.then_inc(sh, by)
                e.sem_inc(g, 1)
            self.ops[eng].append(run2)
            return self.cnt[s]
        if inc is None:
            self.ops[eng].append(lambda e, fn=fn: fn(e))
            return None
        s, by = inc
        self.cnt[s] += by
        sh = self.sems[s]

        def run(e, fn=fn, sh=sh, by=by):
            fn(e).then_inc(sh, by)

        self.ops[eng].append(run)
        return self.cnt[s]

    def dma(self, eng, out, in_, sem, transpose=False, **kw):
        self._ser(eng)
        self.cnt[sem] += 16
        sh = self.sems[sem]
        if transpose:
            def run(e, out=out, in_=in_, sh=sh):
                e.dma_start_transpose(out=out, in_=in_).then_inc(sh, 16)
        else:
            def run(e, out=out, in_=in_, sh=sh, kw=kw):
                e.dma_start(out=out, in_=in_, **kw).then_inc(sh, 16)
        self.ops[eng].append(run)
        if self.paranoid:
            g = self.sems["gseq"]
            v = self.cnt[sem]
            self.gcnt += 1
            self.ops[eng].append(lambda e, sh=sh, v=v, g=g: (e.wait_ge(sh, v),
                                                            e.sem_inc(g, 1)))
        return self.cnt[sem]


    def opw(self, eng, fn):
        sname = {"vector": "mV", "scalar": "mS", "gpsimd": "cc"}[eng]
        v = self.op(eng, fn, inc=(sname, 1))
        self.wait(eng, sname, v)
        return v

    def wait(self, eng, sem, val):
        if val is None or val <= 0:
            return
        sh = self.sems[sem]
        self.ops[eng].append(lambda e, sh=sh, v=val: e.wait_ge(sh, v))

    def wait_all(self, eng, sem):
        self.wait(eng, sem, self.cnt[sem])

    def snap(self, sem):
        return self.cnt[sem]

    def build(self):
        with self.nc.Block() as block:
            @block.sync
            def _(sync):
                for fn in self.ops["sync"]:
                    fn(sync)

            @block.scalar
            def _(scalar):
                for fn in self.ops["scalar"]:
                    fn(scalar)

            @block.vector
            def _(vector):
                for fn in self.ops["vector"]:
                    fn(vector)

            @block.tensor
            def _(tensor):
                for fn in self.ops["tensor"]:
                    fn(tensor)

            @block.gpsimd
            def _(gpsimd):
                for fn in self.ops["gpsimd"]:
                    fn(gpsimd)


def build_nc(T, dbg=False, phases=3, stop_after=99, p2mode="", dbg_big=False, paranoid=False, parmask="", dump_src=None):
    NTT = T // 128
    NG = T // 512
    TPG = 4
    TH = T // 2

    nc = bass.Bass("TRN2", target_bir_lowering=False, detect_race_conditions=False)
    kdbg = "ExternalOutput" if dbg else "Internal"
    kbig = "ExternalOutput" if (dbg and dbg_big) else "Internal"

    def dram(name, shape, dt, kind="Internal"):
        return nc.dram_tensor(name, shape, dt, kind=kind).ap()

    x_in = dram("x_in", [T, D], F32, "ExternalInput")
    w_q = dram("w_q", [D, D], F32, "ExternalInput")
    w_k = dram("w_k", [D, D], F32, "ExternalInput")
    w_v = dram("w_v", [D, D], F32, "ExternalInput")
    w_o = dram("w_o", [D, D], F32, "ExternalInput")
    b_all = dram("b_all", [4, D], F32, "ExternalInput")
    w_up = dram("w_up", [DI, D], F32, "ExternalInput")
    w_gate = dram("w_gate", [DI, D], F32, "ExternalInput")
    w_down = dram("w_down", [D, DI], F32, "ExternalInput")
    costok = dram("costok", [128, NTT * 5], F32, "ExternalInput")
    sintok = dram("sintok", [128, NTT * 5], F32, "ExternalInput")
    x_out = dram("x_out", [T, D], F32, "ExternalOutput")

    res_a = dram("res_a", [T, D], F16)
    res_b = dram("res_b", [T, D], F16)
    ln_h = dram("ln_h", [T, D], F16)
    spU = dram("spU", [NB, 128, T], F16)
    spG = dram("spG", [NB, 128, T], F16)
    spT = dram("spT", [NB, 128, T], F16)
    spZ = dram("spZ", [NB, 128, T], F16)
    wnat_up = dram("wnat_up", [DI, D], F16)
    wnat_g = dram("wnat_g", [DI, D], F16)
    wnat_dn = dram("wnat_dn", [D, DI], F16)
    wupT = dram("wupT", [NB, KC, 128, 128], F16)
    wgT = dram("wgT", [NB, KC, 128, 128], F16)
    wdnT = dram("wdnT", [NB, 128, D], F16)
    wat16 = dram("wat16", [4, D, D], F16)
    q_h = dram("q_h", [T, D], F16)
    dbg_arA = dram("dbg_arA", [128, KC * T], F16, "ExternalOutput") if (dbg and dbg_big) else None
    dbg_ps = dram("dbg_ps", [128, 512], F32, "ExternalOutput") if dbg else None
    dbg_m1 = dram("dbg_m1", [128, 512], F16, "ExternalOutput") if dbg else None
    dbg_acc = dram("dbg_acc", [128, 18], F32, "ExternalOutput") if dbg else None
    dbg_stA = dram("dbg_stA", [128, T], F16, "ExternalOutput") if dbg else None
    bar_i = [dram(f"bar_i{i}", [256], F32) for i in range(8)]
    bar_o = [dram(f"bar_o{i}", [256], F32) for i in range(8)]
    dbg_scl = dram("dbg_scl", [128, 8], F32, "ExternalOutput") if dbg else None
    dbg_ra = dram("dbg_ra", [T, D], F16, "ExternalOutput") if dbg else None
    dbg_rb = dram("dbg_rb", [T, D], F16, "ExternalOutput") if dbg else None
    dbg_lh = dram("dbg_lh", [T, D], F16, "ExternalOutput") if (dbg and dbg_big) else None
    dbg_qh = dram("dbg_qh", [T, D], F16, "ExternalOutput") if (dbg and dbg_big) else None
    dbg_spU = dram("dbg_spU", [NB * 128 * T], F16, "ExternalOutput") if (dbg and dbg_big) else None
    dbg_spG = dram("dbg_spG", [NB * 128 * T], F16, "ExternalOutput") if (dbg and dbg_big) else None
    dbg_spT = dram("dbg_spT", [NB * 128 * T], F16, "ExternalOutput") if (dbg and dbg_big) else None
    dbg_wu = dram("dbg_wu", [NB * KC * 128 * 128], F16, "ExternalOutput") if (dbg and dbg_big) else None

    stk = contextlib.ExitStack()

    def sb(name, shape, dt):
        return stk.enter_context(nc.sbuf_tensor(name, shape, dt))

    AR = 32768
    arA = sb("arA", [128, AR], F16)
    arB = sb("arB", [128, AR], F16)
    stA = sb("stA", [128, 4096], F16)
    stB = sb("stB", [128, 4096], F16)
    rpb = sb("rpb", [128, 4096], F16)
    wbf = sb("wbf", [128, 8192], F16)
    xf = sb("xf", [128, 2 * D], F16)
    acc = sb("acc", [128, 4], F32)
    tmp1 = sb("tmp1", [128, 8], F32)
    scl = sb("scl", [128, 8], F32)
    ctok = sb("ctok", [128, NTT * 5], F32)
    stok = sb("stok", [128, NTT * 5], F32)
    brow = sb("brow", [1, 256], F32)
    gmax = sb("gmax", [1, 8], F32)
    grow = sb("grow", [1, 8], F32)
    ones16 = sb("ones16", [1, 128], F16)
    ones32 = sb("ones32", [1, 128], F32)
    brows = sb("brows", [1, 4 * D], F16)
    m1sb = sb("m1sb", [128, 512], F16)
    sv1 = sb("sv1", [1, 4], F32)
    tI1 = sb("tI1", [128, 2048], I32)
    tI2 = sb("tI2", [128, 1024], I32)
    tF1 = sb("tF1", [128, 2048], F32)
    tH1 = sb("tH1", [128, 1024], F16)

    pA = stk.enter_context(nc.psum_tensor("pA", [128, 1024], F32))
    pB = stk.enter_context(nc.psum_tensor("pB", [128, 1024], F32))
    pC = stk.enter_context(nc.psum_tensor("pC", [128, 512], F32))
    pD = stk.enter_context(nc.psum_tensor("pD", [128, 512], F32))
    pE = stk.enter_context(nc.psum_tensor("pE", [128, 512], F32))
    pF = stk.enter_context(nc.psum_tensor("pF", [128, 512], F32))

    P = Prog(nc, stk, paranoid=paranoid)
    PM = set(parmask.split(",")) if parmask else set()

    def par(tag):
        if not paranoid:
            P.paranoid = (tag in PM) or ("all" in PM)
    P.sem("gseq","qW", "qLN", "qTR", "qSP", "qSL", "qR", "qO", "qB", "qX",
          "qXG", "qRG", "qW2", "qN", "qBk",
          "mS", "mV", "mT", "cc")

    rpbF32 = rpb.ap().bitcast(F32)
    tI2i8 = tI2.ap().bitcast(I8)
    tI2f = tI2.ap().bitcast(F32)

    par("prep1")
    P.opw("vector", lambda e: e.memset(ones16[:], 1.0))
    P.opw("vector", lambda e: e.memset(ones32[:], 1.0))
    P.opw("vector", lambda e: e.memset(scl[:], 0.0))
    P.opw("vector", lambda e: e.memset(tmp1[:], 0.0))
    P.opw("vector", lambda e: e.memset(gmax[:], 0.0))
    P.opw("vector", lambda e: e.memset(grow[:], 0.0))
    P.opw("vector", lambda e: e.memset(brow[:], 0.0))

    for wi, wsrc in enumerate((w_q, w_k, w_v, w_o)):
        P.dma("gpsimd", wat16[wi], wsrc, "qW2")
    at16_ld = P.snap("qW2")
    for j in range(4):
        ld = P.dma("sync", rpbF32[0:1, 0:D], b_all[j:j + 1, :], "qW")
        P.wait("scalar", "qW", ld)
        bev = P.op("scalar", lambda e, j=j: e.copy(
            out=brows[:, j * D:(j + 1) * D], in_=rpbF32[0:1, 0:D]), inc=("mS", 1))
        P.wait("sync", "mS", bev)
    P.dma("sync", ctok[:], costok, "qW")
    P.dma("sync", stok[:], sintok, "qW")
    consts_ld = P.snap("qW")

    NBL = 32
    pp = {"s": 0, "mult": {}, "cv": {}, "dq": {}, "nat": {}, "tr": {},
          "blk": {}, "pend": None}

    def _prep_flush_one(item):
        ps, tr_pairs, blk_fn, done_cb = item
        sl = ps % 2
        stg = wbf[:, 6144 + sl * 1024:6144 + sl * 1024 + 1024]
        P.wait("sync", "qN", pp["nat"][ps])
        P.wait("sync", "qBk", pp["blk"].get(ps - 2))
        for (tsrc, col) in tr_pairs:
            pp["tr"][ps] = P.dma("sync", stg[:, col:col + 128], tsrc, "qTR",
                                 transpose=True)
        P.wait("gpsimd", "qTR", pp["tr"][ps])
        dst, src = blk_fn(stg)
        pp["blk"][ps] = P.dma("gpsimd", dst, src, "qBk")
        done_cb(pp["blk"][ps])

    def prep_emit(src_ap, nat_ap, tr_pairs, blk_fn, done_cb):
        s = pp["s"]
        pp["s"] += 1
        sl = s % 2
        wdt = rpbF32[:, sl * 1024:sl * 1024 + 1024]
        wtq = tF1[:, sl * 1024:sl * 1024 + 1024]
        wqd = tI2i8[:, sl * 1024:sl * 1024 + 1024]
        wsc = tI2f[:, 512 + sl * 32:544 + sl * 32]
        wrec = tI2f[:, 576 + sl * 32:608 + sl * 32]
        wbo = wbf[:, 4096 + sl * 1024:4096 + sl * 1024 + 1024]
        P.wait("sync", "mV", pp["mult"].get(s - 2))
        ld = P.dma("sync", wdt, src_ap, "qW")
        P.wait("vector", "qW", ld)
        v3 = wdt.rearrange("p (b i) -> p b i", i=32)
        P.op("vector", lambda e, v3=v3, wsc=wsc: e.tensor_reduce(
            out=wsc, in_=v3, axis=AX.X, op=ALU.max, apply_absolute_value=True))
        P.op("vector", lambda e, wsc=wsc: e.tensor_scalar(
            out=wsc, in0=wsc, scalar1=1.0 / 7.0, scalar2=1e-12,
            op0=ALU.mult, op1=ALU.max))
        P.op("vector", lambda e, wsc=wsc, wrec=wrec: e.reciprocal(
            out=wrec, in_=wsc))
        rb = wrec[:, :, None].broadcast_to([128, NBL, 32])
        P.wait("vector", "mS", pp["cv"].get(s - 2))
        mv = P.op("vector", lambda e, v3=v3, rb=rb, wtq=wtq: e.tensor_tensor(
            out=wtq, in0=v3, in1=rb, op=ALU.mult))
        pp["mult"][s] = mv
        P.wait("scalar", "mV", mv)
        P.wait("scalar", "mV", pp["dq"].get(s - 2))
        cv = P.op("scalar", lambda e, wqd=wqd, wtq=wtq: e.copy(
            out=wqd, in_=wtq), inc=("mS", 1))
        pp["cv"][s] = cv
        P.wait("vector", "mS", cv)
        sbb = wsc[:, :, None].broadcast_to([128, NBL, 32])
        q3 = wqd.rearrange("p (b i) -> p b i", i=32)
        P.wait("vector", "qN", pp["nat"].get(s - 2))
        dq = P.op("vector", lambda e, q3=q3, sbb=sbb, wbo=wbo: e.tensor_tensor(
            out=wbo, in0=q3, in1=sbb, op=ALU.mult))
        pp["dq"][s] = dq
        P.wait("gpsimd", "mV", dq)
        pp["nat"][s] = P.dma("gpsimd", nat_ap, wbo, "qN")
        prev = pp["pend"]
        pp["pend"] = (s, tr_pairs, blk_fn, done_cb)
        if prev is not None:
            _prep_flush_one(prev)

    def prep_flush():
        prev = pp["pend"]
        pp["pend"] = None
        if prev is not None:
            _prep_flush_one(prev)

    upT_ready = [0] * NB
    gT_ready = [0] * NB
    prepst = {"dn": 0}

    def upgate_tile(wsrc, wnat_v, dstT, ready, t):
        src = wsrc[t * 128:(t + 1) * 128, :]
        nat = wnat_v[t * 128:(t + 1) * 128, :]
        trp = [(wnat_v[t * 128:(t + 1) * 128, c * 128:(c + 1) * 128], c * 128)
               for c in range(KC)]

        def blk(stg, t=t):
            return (dstT[t].rearrange("c p x -> p c x"),
                    stg.rearrange("p (c x) -> p c x", c=KC))

        prep_emit(src, nat, trp, blk,
                  lambda v, t=t: ready.__setitem__(t, v))

    def down_tile(s):
        qc, t = s // 8, s % 8
        c0 = qc * 1024
        src = w_down[t * 128:(t + 1) * 128, c0:c0 + 1024]
        nat = wnat_dn[t * 128:(t + 1) * 128, c0:c0 + 1024]
        trp = [(wnat_dn[t * 128:(t + 1) * 128, c0 + j * 128:c0 + (j + 1) * 128],
                j * 128) for j in range(8)]

        def blk(stg, qc=qc, t=t):
            dst = wdnT[qc * 8:(qc + 1) * 8, :, t * 128:(t + 1) * 128] \
                .rearrange("j p x -> p j x")
            return dst, stg.rearrange("p (j x) -> p j x", j=8)

        def done(v):
            prepst["dn"] = v

        prep_emit(src, nat, trp, blk, done)

    LNO = 2048
    QSZ = min(1024, T)
    NQ4 = T // QSZ

    def ln_pass(x_src, x_dtype, need_max):
        if need_max:
            P.opw("vector", lambda e: e.memset(acc[:, 0:1], 0.0))
        prev_free = None
        prev_ve = None
        prev_st = None
        for i in range(NTT):
            slot = i % 2
            xt = xf[:, slot * D:(slot + 1) * D]
            eng_ld = "gpsimd" if x_dtype == F32 else "sync"
            qn = "qXG" if eng_ld == "gpsimd" else "qX"
            if prev_free is not None:
                P.wait(eng_ld, prev_free[0], prev_free[1])
            ld = P.dma(eng_ld, xt, x_src[i * 128:(i + 1) * 128, :], qn)
            P.wait("vector", qn, ld)
            P.wait("scalar", qn, ld)
            P.opw("vector", lambda e, xt=xt: e.reduce_sum(
                out=tmp1[:, 0:1], in_=xt, axis=AX.X))
            P.wait("scalar", "mV", prev_ve)
            sqv = P.op("scalar", lambda e, xt=xt: e.activation(
                out=tF1[:, 0:D], in_=xt, func=AF.Square), inc=("mS", 1))
            P.wait("vector", "mS", sqv)
            P.opw("vector", lambda e: e.reduce_sum(
                out=tmp1[:, 1:2], in_=tF1[:, 0:D], axis=AX.X))
            P.op("vector", lambda e: e.tensor_scalar_mul(
                out=tmp1[:, 2:3], in0=tmp1[:, 0:1], scalar1=1.0 / D))
            P.op("vector", lambda e: e.tensor_mul(
                out=tmp1[:, 3:4], in0=tmp1[:, 2:3], in1=tmp1[:, 2:3]))
            P.op("vector", lambda e: e.scalar_tensor_tensor(
                out=tmp1[:, 4:5], in0=tmp1[:, 1:2], scalar=1.0 / D,
                in1=tmp1[:, 3:4], op0=ALU.mult, op1=ALU.subtract))
            ve = P.op("vector", lambda e: e.tensor_scalar_add(
                out=tmp1[:, 4:5], in0=tmp1[:, 4:5], scalar1=1e-5), inc=("mV", 1))
            prev_ve = ve
            P.wait("scalar", "mV", ve)
            se = P.op("scalar", lambda e: e.activation(
                out=tmp1[:, 5:6], in_=tmp1[:, 4:5], func=AF.Sqrt), inc=("mS", 1))
            P.wait("vector", "mS", se)
            P.opw("vector", lambda e: e.reciprocal(out=tmp1[:, 6:7], in_=tmp1[:, 5:6]))
            vm = P.op("vector", lambda e: e.scalar_tensor_tensor(
                out=tmp1[:, 7:8], in0=tmp1[:, 2:3], scalar=-1.0,
                in1=tmp1[:, 6:7], op0=ALU.mult, op1=ALU.mult), inc=("mV", 1))
            P.wait("scalar", "mV", vm)
            P.wait("scalar", "qLN", prev_st)
            lno = P.op("scalar", lambda e, xt=xt: e.activation(
                out=arB[:, LNO:LNO + D], in_=xt, func=AF.Identity,
                scale=tmp1[:, 6:7], bias=tmp1[:, 7:8]), inc=("mS", 1))
            P.wait("sync", "mS", lno)
            prev_st = P.dma("sync", ln_h[i * 128:(i + 1) * 128, :], arB[:, LNO:LNO + D],
                            "qLN")
            if need_max:
                P.wait("vector", "mS", lno)
                P.opw("vector", lambda e: e.tensor_reduce(
                    out=tmp1[:, 0:1], in_=arB[:, LNO:LNO + D], axis=AX.X, op=ALU.max,
                    apply_absolute_value=True))
                pv = P.op("vector", lambda e: e.tensor_max(
                    out=acc[:, 0:1], in0=acc[:, 0:1], in1=tmp1[:, 0:1]), inc=("mV", 1))
                prev_free = ("mV", pv)
            else:
                prev_free = ("mS", lno)

    bar_n = [0]
    dbg1 = [True]

    def barrier(nq, post):
        bi = bar_n[0]
        bar_n[0] += 1
        n = 128 * nq
        vp = P.op("vector", lambda e, nq=nq: e.tensor_copy(
            out=tmp1[:, 0:nq], in_=acc[:, 0:nq]), inc=("mV", 1))
        P.wait("gpsimd", "mV", vp)
        d1 = P.dma("gpsimd", bar_i[bi][0:n], tmp1[:, 0:nq], "qB")
        P.wait("gpsimd", "qB", d1)
        ccv = P.op("gpsimd", lambda e, bi=bi, n=n: e.collective_compute(
            "AllReduce", ALU.max, replica_groups=[list(range(NCORES))],
            ins=[bar_i[bi][0:n].opt()], outs=[bar_o[bi][0:n].opt()]), inc=("cc", 1))
        P.wait("gpsimd", "cc", ccv)
        d2 = P.dma("gpsimd", brow[:, 0:n], bar_o[bi][0:n], "qB")
        P.wait("vector", "qB", d2)
        for q in range(nq):
            qv = brow[:, q:n:nq]
            P.opw("vector", lambda e, q=q, qv=qv: e.tensor_reduce(
                out=gmax[:, q:q + 1], in_=qv, axis=AX.X, op=ALU.max))
        m = post()
        vb = P.op("vector", lambda e: e.tensor_copy(
            out=grow[:, 7:8], in_=grow[:, 0:1]), inc=("mV", 1))
        P.wait("tensor", "mV", vb)
        P.wait("tensor", "mS", P.snap("mS"))
        P.wait("tensor", "qW", consts_ld)
        tb = P.op("tensor", lambda e, m=m: e.matmul(
            pF[:, 0:m], ones32[:], grow[:, 0:m], start=True, stop=True),
            inc=("mT", 1))
        P.wait("scalar", "mT", tb)
        sc = P.op("scalar", lambda e, m=m: e.copy(out=scl[:, 0:m], in_=pF[:, 0:m]),
                  inc=("mS", 1))
        return sc

    def ffn(x_src, x_dtype, out_dst, out_dtype,
            pre_p2=None, up_cb=None, gate_cb=None):
        if stop_after < 1:
            return
        par("ln")
        ln_pass(x_src, x_dtype, True)
        par("bar")
        if stop_after < 2:
            return

        def post1():
            P.op("vector", lambda e: e.tensor_scalar(
                out=gmax[:, 1:2], in0=gmax[:, 0:1], scalar1=1.0 / QMAX,
                scalar2=1e-12, op0=ALU.mult, op1=ALU.max))
            P.opw("vector", lambda e: e.reciprocal(out=grow[:, 0:1], in_=gmax[:, 1:2]))
            P.op("vector", lambda e: e.tensor_copy(out=grow[:, 1:2], in_=gmax[:, 1:2]))
            return 2
        b1 = barrier(1, post1)
        if pre_p2 is not None:
            pre_p2()
        if stop_after < 3:
            return

        par("tr")
        P.wait_all("sync", "qLN")
        for c in range(KC):
            P.dma("sync", arA[:, c * T:(c + 1) * T], ln_h[:, c * 128:(c + 1) * 128],
                  "qTR", transpose=True)
        trs = P.snap("qTR")

        P.wait("scalar", "qTR", trs)
        P.wait("scalar", "mS", b1)
        for c in range(KC):
            for q in range(NQ4):
                o0 = c * T + q * QSZ
                P.op("scalar", lambda e, o0=o0: e.activation(
                    out=tI1[:, 0:QSZ], in_=arA[:, o0:o0 + QSZ], func=AF.Identity,
                    scale=scl[:, 0:1]), inc=("mS", 1))
                P.op("scalar", lambda e, o0=o0: e.activation(
                    out=arA[:, o0:o0 + QSZ], in_=tI1[:, 0:QSZ], func=AF.Identity,
                    scale=scl[:, 1:2]), inc=("mS", 1))
        xq_done = P.snap("mS")
        if stop_after < 4:
            return
        if dbg and dbg_big:
            P.wait("sync", "mS", xq_done)
            P.dma("sync", dbg_arA[:, 0:KC * T], arA[:, 0:KC * T], "qX") if dbg_big else None

        par("p2")
        P.opw("vector", lambda e: e.memset(acc[:, 0:2], 0.0))
        P.wait("tensor", "mS", xq_done)
        bank_free = {}
        slot_mm = [None, None]

        def p2_pass(ready, wT, stg, spDst, accc, prep_cb):
            st_prev = None
            sE = None
            for nb in range(NB):
                if prep_cb is not None:
                    prep_cb(nb)
                wslot = nb % 2
                w_l = wbf[:, wslot * 1024:(wslot + 1) * 1024]
                P.wait("sync", "mT", slot_mm[wslot])
                P.wait("sync", "qBk", ready[nb])
                lw_ = P.dma("sync", w_l.rearrange("p (c x) -> p c x", c=KC),
                            wT[nb].rearrange("c p x -> p c x"), "qW")
                P.wait("tensor", "qW", lw_)
                for g in range(NG):
                    bank = (pC, pD, pE, pF)[g % 4]
                    if id(bank) in bank_free:
                        sH, vH = bank_free[id(bank)]
                        P.wait("tensor", "mS", sH)
                        P.wait("tensor", "mV", vH)
                    for c in range(KC):
                        P.op("tensor", lambda e, c=c, bank=bank, w_l=w_l, g=g:
                             e.matmul(
                                 bank[:], w_l[:, c * 128:(c + 1) * 128],
                                 arA[:, c * T + g * 512:c * T + g * 512 + 512],
                                 start=(c == 0), stop=(c == KC - 1)),
                             inc=("mT", 1) if c == KC - 1 else None)
                    mTv = P.snap("mT")
                    if g == NG - 1:
                        slot_mm[wslot] = mTv
                    P.wait("scalar", "mT", mTv)
                    if g == 0:
                        P.wait("scalar", "qSP", st_prev)
                    sE = P.op("scalar", lambda e, bank=bank, stg=stg, g=g: e.copy(
                        out=stg[:, g * 512:(g + 1) * 512], in_=bank[:]),
                        inc=("mS", 1))
                    P.wait("vector", "mS", sE)
                    P.opw("vector", lambda e, stg=stg, g=g: e.tensor_reduce(
                        out=tmp1[:, 0:1], in_=stg[:, g * 512:(g + 1) * 512],
                        axis=AX.X, op=ALU.max, apply_absolute_value=True))
                    vE = P.op("vector", lambda e, accc=accc: e.tensor_max(
                        out=acc[:, accc:accc + 1], in0=acc[:, accc:accc + 1],
                        in1=tmp1[:, 0:1]), inc=("mV", 1))
                    bank_free[id(bank)] = (sE, vE)
                P.wait("gpsimd", "mS", sE)
                st_prev = P.dma("gpsimd", spDst[nb], stg[:, 0:T], "qSP")

        p2_pass(upT_ready, wupT, stA, spU, 0, up_cb)
        p2_pass(gT_ready, wgT, stB, spG, 1, gate_cb)
        p2_sp = P.snap("qSP")
        if stop_after < 5:
            return

        par("bar")

        def post2():
            P.op("vector", lambda e: e.tensor_scalar(
                out=gmax[:, 2:4], in0=gmax[:, 0:2], scalar1=1.0 / QMAX,
                scalar2=1e-12, op0=ALU.mult, op1=ALU.max))
            P.opw("vector", lambda e: e.reciprocal(out=grow[:, 0:1], in_=gmax[:, 2:3]))
            P.opw("vector", lambda e: e.reciprocal(out=grow[:, 1:2], in_=gmax[:, 3:4]))
            P.op("vector", lambda e: e.tensor_copy(out=grow[:, 2:4], in_=gmax[:, 2:4]))
            P.op("vector", lambda e: e.tensor_scalar_mul(
                out=grow[:, 4:5], in0=gmax[:, 3:4], scalar1=1.0 / 65536.0))
            return 5
        b2 = barrier(2, post2)
        import os as _os
        if dbg and dbg1[0] and _os.environ.get('DBG2','1')=='1':
            dbg1[0] = False
            vd = P.op("vector", lambda e: e.tensor_copy(out=tmp1[:, 0:2],
                                                        in_=acc[:, 0:2]), inc=("mV", 1))
            P.wait("sync", "mV", vd)
            P.dma("sync", dbg_acc[:, 0:2], tmp1[:, 0:2], "qX")
            P.dma("sync", dbg_acc[0:1, 2:10], gmax[:, 0:8], "qX")
            P.dma("sync", dbg_acc[0:1, 10:18], grow[:, 0:8], "qX")
            P.wait("scalar", "mS", b2)
            P.dma("sync", dbg_stA, stA[:, 0:T], "qX")
        if stop_after < 6:
            return
        P.op("vector", lambda e: e.tensor_copy(out=sv1[:, 0:1], in_=grow[:, 4:5]))
        P.op("vector", lambda e: e.tensor_copy(out=sv1[:, 1:2], in_=grow[:, 0:1]))
        P.op("vector", lambda e: e.tensor_copy(out=sv1[:, 2:3], in_=grow[:, 2:3]))

        par("p3")
        P.opw("vector", lambda e: e.memset(acc[:, 0:1], 0.0))
        P.wait("sync", "qSP", p2_sp)
        P.wait("scalar", "mS", b2)
        st_hist = [None, None]
        ld_p = [None, None]
        vfree = {}
        i2f = None
        pend = [None]
        s4l = [None]

        def flush_s4():
            fn, pend[0] = pend[0], None
            return fn() if fn is not None else None

        qg = 0
        for nb in range(NB):
            slot = nb % 2
            gbuf = arA[:, slot * T:(slot + 1) * T]
            stg = stA[:, 0:T] if slot == 0 else rpb[:, 0:T]
            P.wait("sync", "mS", ld_p[slot])
            ldg = P.dma("sync", gbuf, spG[nb], "qSL")
            P.wait("scalar", "qSL", ldg)
            P.wait("scalar", "qSP", st_hist[slot])
            for q in range(NQ4):
                q0 = q * QSZ
                ts = qg % 2
                i1 = tI1[:, ts * 1024:ts * 1024 + QSZ]
                i2 = tI2[:, 0:QSZ]
                f1 = tF1[:, ts * 1024:ts * 1024 + QSZ]
                P.wait("scalar", "mV", vfree.get(ts))
                P.op("scalar", lambda e, gbuf=gbuf, q0=q0, i1=i1: e.activation(
                    out=i1, in_=gbuf[:, q0:q0 + QSZ], func=AF.Identity,
                    scale=scl[:, 1:2]), inc=("mS", 1))
                P.op("scalar", lambda e, i1=i1, f1=f1: e.activation(
                    out=f1, in_=i1, func=AF.Sigmoid, scale=scl[:, 3:4]), inc=("mS", 1))
                P.wait("scalar", "mV", i2f)
                c3 = P.op("scalar", lambda e, f1=f1, i2=i2: e.activation(
                    out=i2, in_=f1, func=AF.Identity, scale=65536.0), inc=("mS", 1))
                flush_s4()
                P.wait("vector", "mS", c3)
                i2f = P.op("vector", lambda e, i1=i1, i2=i2, f1=f1: e.tensor_tensor(
                    out=f1, in0=i1, in1=i2, op=ALU.mult))
                P.op("vector", lambda e, f1=f1: e.tensor_reduce(
                    out=tmp1[:, 0:1], in_=f1, axis=AX.X, op=ALU.max,
                    apply_absolute_value=True))
                vmark = P.op("vector", lambda e: e.tensor_max(
                    out=acc[:, 0:1], in0=acc[:, 0:1], in1=tmp1[:, 0:1]))

                def mk_s4(vmark=vmark, q0=q0, f1=f1, stg=stg):
                    def go():
                        P.wait("scalar", "mV", vmark)
                        return P.op("scalar", lambda e: e.activation(
                            out=stg[:, q0:q0 + QSZ], in_=f1, func=AF.Identity,
                            scale=scl[:, 4:5]), inc=("mS", 1))
                    return go
                pend[0] = mk_s4()
                vfree[ts] = vmark
                qg += 1
            s4 = flush_s4() or s4l[0]
            ld_p[slot] = P.cnt["mS"]
            P.wait("gpsimd", "mS", s4)
            st_hist[slot] = P.dma("gpsimd", spT[nb], stg, "qSP")
        p3_sp = P.snap("qSP")
        if stop_after < 7:
            return

        par("bar")

        def post3():
            P.op("vector", lambda e: e.tensor_mul(
                out=gmax[:, 1:2], in0=gmax[:, 0:1], in1=sv1[:, 0:1]))
            P.op("vector", lambda e: e.tensor_scalar(
                out=gmax[:, 2:3], in0=gmax[:, 1:2], scalar1=1.0 / QMAX,
                scalar2=1e-12, op0=ALU.mult, op1=ALU.max))
            P.opw("vector", lambda e: e.reciprocal(out=grow[:, 0:1], in_=gmax[:, 2:3]))
            P.op("vector", lambda e: e.tensor_copy(out=grow[:, 1:2], in_=gmax[:, 2:3]))
            P.op("vector", lambda e: e.tensor_copy(out=grow[:, 2:3], in_=sv1[:, 1:2]))
            P.op("vector", lambda e: e.tensor_copy(out=grow[:, 3:4], in_=sv1[:, 2:3]))
            return 4
        b3 = barrier(1, post3)
        P.op("vector", lambda e: e.tensor_copy(out=sv1[:, 3:4], in_=grow[:, 1:2]))

        par("p4")
        P.opw("vector", lambda e: e.memset(acc[:, 0:1], 0.0))
        P.wait("sync", "qSP", p3_sp)
        P.wait("scalar", "mS", b3)
        st_hist = [None, None]
        ld_p4 = [None, None]
        vfree = {}
        h1f = None
        pend = [None]
        s4l = [None]

        def flush_s4b():
            fn, pend[0] = pend[0], None
            return fn() if fn is not None else None

        qg = 0
        for nb in range(NB):
            slot = nb % 2
            tbuf = arA[:, slot * T:(slot + 1) * T]
            ubuf = arA[:, (2 + slot) * T:(3 + slot) * T]
            stg = stB[:, 0:T] if slot == 0 else arB[:, 0:T]
            P.wait("sync", "mS", ld_p4[slot])
            P.dma("sync", tbuf, spT[nb], "qSL")
            ldu = P.dma("sync", ubuf, spU[nb], "qSL")
            P.wait("scalar", "qSL", ldu)
            P.wait("scalar", "qSP", st_hist[slot])
            for q in range(NQ4):
                q0 = q * QSZ
                ts = qg % 2
                i1 = tI1[:, ts * 1024:ts * 1024 + QSZ]
                i2 = tI2[:, 0:QSZ]
                f1 = tF1[:, ts * 1024:ts * 1024 + QSZ]
                h1 = tH1[:, 0:QSZ]
                P.wait("scalar", "mV", vfree.get(ts))
                P.op("scalar", lambda e, tbuf=tbuf, q0=q0, i1=i1: e.activation(
                    out=i1, in_=tbuf[:, q0:q0 + QSZ], func=AF.Identity,
                    scale=scl[:, 0:1]), inc=("mS", 1))
                P.op("scalar", lambda e, ubuf=ubuf, q0=q0, i2=i2: e.activation(
                    out=i2, in_=ubuf[:, q0:q0 + QSZ], func=AF.Identity,
                    scale=scl[:, 2:3]), inc=("mS", 1))
                P.wait("scalar", "mV", h1f)
                c3 = P.op("scalar", lambda e, i2=i2, h1=h1: e.activation(
                    out=h1, in_=i2, func=AF.Identity, scale=scl[:, 3:4]), inc=("mS", 1))
                flush_s4b()
                P.wait("vector", "mS", c3)
                h1f = P.op("vector", lambda e, i1=i1, h1=h1, f1=f1: e.tensor_tensor(
                    out=f1, in0=i1, in1=h1, op=ALU.mult))
                P.op("vector", lambda e, f1=f1: e.tensor_reduce(
                    out=tmp1[:, 0:1], in_=f1, axis=AX.X, op=ALU.max,
                    apply_absolute_value=True))
                vmark = P.op("vector", lambda e: e.tensor_max(
                    out=acc[:, 0:1], in0=acc[:, 0:1], in1=tmp1[:, 0:1]))

                def mk_s4b(vmark=vmark, q0=q0, f1=f1, stg=stg):
                    def go():
                        P.wait("scalar", "mV", vmark)
                        return P.op("scalar", lambda e: e.activation(
                            out=stg[:, q0:q0 + QSZ], in_=f1, func=AF.Identity,
                            scale=scl[:, 1:2]), inc=("mS", 1))
                    return go
                pend[0] = mk_s4b()
                vfree[ts] = vmark
                qg += 1
            s4 = flush_s4b() or s4l[0]
            ld_p4[slot] = P.cnt["mS"]
            P.wait("gpsimd", "mS", s4)
            st_hist[slot] = P.dma("gpsimd", spZ[nb], stg, "qSP")
        p4_sp = P.snap("qSP")
        if stop_after < 8:
            return

        par("bar")

        def post4():
            P.op("vector", lambda e: e.tensor_mul(
                out=gmax[:, 1:2], in0=gmax[:, 0:1], in1=sv1[:, 3:4]))
            P.op("vector", lambda e: e.tensor_scalar(
                out=gmax[:, 2:3], in0=gmax[:, 1:2], scalar1=1.0 / QMAX,
                scalar2=1e-12, op0=ALU.mult, op1=ALU.max))
            P.opw("vector", lambda e: e.reciprocal(out=grow[:, 0:1], in_=gmax[:, 2:3]))
            P.op("vector", lambda e: e.tensor_scalar_mul(
                out=grow[:, 1:2], in0=gmax[:, 2:3], scalar1=0.5))
            return 2
        b4 = barrier(1, post4)

        par("p5")
        P.wait("sync", "qSP", p4_sp)
        P.wait("sync", "qBk", prepst["dn"])
        for nb2 in range(NB):
            P.dma("sync", arA[:, nb2 * D:nb2 * D + D], wdnT[nb2], "qW")
        wdn_ld = P.snap("qW")
        P.wait("tensor", "qW", wdn_ld)
        P.wait("scalar", "mS", b4)
        OBUF = 2048
        o32_v = tI1[:, 0:512]
        zs0 = AR - 1024
        down_free = {}
        out_prev = None
        for g in range(NG):
            ocs = []
            for nb in range(NB):
                zslot = nb % 2
                zb = arB[:, zs0 + zslot * 512:zs0 + (zslot + 1) * 512]
                P.wait("sync", "mS", ocs[nb - 2] if nb >= 2 else None)
                ldz = P.dma("sync", zb, spZ[nb][:, g * 512:(g + 1) * 512], "qSL")
                P.wait("scalar", "qSL", ldz)
                P.op("scalar", lambda e, zb=zb: e.activation(
                    out=o32_v, in_=zb, func=AF.Identity, scale=scl[:, 0:1]),
                    inc=("mS", 1))
                ocs.append(P.op("scalar", lambda e, nb=nb: e.activation(
                    out=arB[:, OBUF + nb * 512:OBUF + (nb + 1) * 512], in_=o32_v,
                    func=AF.Copy), inc=("mS", 1)))
            for tt in range(TPG):
                bank = (pA, pB)[tt % 2]
                if id(bank) in down_free:
                    P.wait("tensor", "mV", down_free[id(bank)])
                mmv = None
                for nb in range(NB):
                    P.wait("tensor", "mS", ocs[nb])
                    lhs = arB[:, OBUF + nb * 512 + tt * 128:OBUF + nb * 512 + (tt + 1) * 128]
                    for hh in range(2):
                        mmv = P.op("tensor", lambda e, bank=bank, lhs=lhs, nb=nb, hh=hh:
                                   e.matmul(bank[:, hh * 512:(hh + 1) * 512], lhs,
                                            arA[:, nb * D + hh * 512:nb * D + (hh + 1) * 512],
                                            start=(nb == 0), stop=(nb == NB - 1)),
                                   inc=("mT", 1) if (nb == NB - 1 and hh == 1) else None)
                ti = g * TPG + tt
                xslot = ti % 2
                xin_ap = xf[:, xslot * D:(xslot + 1) * D]
                eng_ld = "gpsimd" if x_dtype == F32 else "sync"
                qn = "qRG" if eng_ld == "gpsimd" else "qR"
                P.wait(eng_ld, "mV", down_free.get(id(bank)))
                ldx = P.dma(eng_ld, xin_ap, x_src[ti * 128:(ti + 1) * 128, :], qn)
                P.wait("vector", qn, ldx)
                P.wait("vector", "mT", mmv)
                P.wait("vector", "qO", out_prev)
                out_ap = tF1[:, 1024:1024 + D] if out_dtype == F32 else tH1[:, 0:D]
                ov = P.op("vector", lambda e, bank=bank, xin_ap=xin_ap, out_ap=out_ap:
                          e.scalar_tensor_tensor(
                              out=out_ap, in0=bank[:], scalar=scl[:, 1:2],
                              in1=xin_ap, op0=ALU.mult, op1=ALU.add), inc=("mV", 1))
                down_free[id(bank)] = ov
                P.wait("sync", "mV", ov)
                out_prev = P.dma("sync", out_dst[ti * 128:(ti + 1) * 128, :], out_ap,
                                 "qO")
        P.wait_all("sync", "qO")

    def attention():
        par("ln")
        ln_pass(res_a, F16, False)
        par("attq")
        P.wait_all("sync", "qLN")
        for c in range(KC):
            P.dma("sync", arA[:, c * T:(c + 1) * T], ln_h[:, c * 128:(c + 1) * 128],
                  "qTR", transpose=True)
        trs = P.snap("qTR")
        P.wait("tensor", "qTR", trs)

        def rope_tile(buf16, i):
            b3 = buf16.rearrange("p (h x) -> p h x", h=H)
            t_o = rpb[:, 3 * D:3 * D + 16][:, :, None]
            t_e = rpb[:, 3 * D + 16:3 * D + 32][:, :, None]
            for p5 in range(5):
                cp = ctok[:, i * 5 + p5:i * 5 + p5 + 1]
                sp = stok[:, i * 5 + p5:i * 5 + p5 + 1]
                ke = b3[:, :, 2 * p5:2 * p5 + 1]
                ko = b3[:, :, 2 * p5 + 1:2 * p5 + 2]
                P.op("vector", lambda e, ko=ko, sp=sp: e.tensor_scalar_mul(
                    out=t_o, in0=ko, scalar1=sp))
                P.op("vector", lambda e, ke=ke, sp=sp: e.tensor_scalar_mul(
                    out=t_e, in0=ke, scalar1=sp))
                P.op("vector", lambda e, ke=ke, cp=cp: e.scalar_tensor_tensor(
                    out=ke, in0=ke, scalar=cp, in1=t_o, op0=ALU.mult,
                    op1=ALU.subtract))
                P.op("vector", lambda e, ko=ko, cp=cp: e.scalar_tensor_tensor(
                    out=ko, in0=ko, scalar=cp, in1=t_e, op0=ALU.mult, op1=ALU.add))

        wq_l = wbf[:, 0:KC * 1024]
        P.wait("sync", "qW2", at16_ld)
        lw = P.dma("sync", wq_l.rearrange("p (c x) -> p c x", c=KC),
                   wat16[0].rearrange("(c p) x -> p c x", p=128), "qW")
        P.wait("tensor", "qW", lw)
        q16 = rpb[:, 2 * D:3 * D]
        qb_free = {}
        vq = None
        st_q = None
        for i in range(NTT):
            bank = (pA, pB)[i % 2]
            if id(bank) in qb_free:
                P.wait("tensor", "mS", qb_free[id(bank)])
            for c in range(KC):
                for hh in range(2):
                    P.op("tensor", lambda e, c=c, hh=hh, i=i, bank=bank: e.matmul(
                        bank[:, hh * 512:(hh + 1) * 512],
                        arA[:, c * T + i * 128:c * T + (i + 1) * 128],
                        wq_l[:, c * 1024 + hh * 512:c * 1024 + (hh + 1) * 512],
                        start=(c == 0), stop=False))
            P.op("tensor", lambda e, bank=bank: e.matmul(
                bank[:, 0:512], ones16[:], brows[:, 0:512], start=False, stop=True))
            qmm = P.op("tensor", lambda e, bank=bank: e.matmul(
                bank[:, 512:1024], ones16[:], brows[:, 512:1024], start=False,
                stop=True), inc=("mT", 1))
            P.wait("scalar", "mT", qmm)
            P.wait("scalar", "mV", vq)
            P.wait("scalar", "qSP", st_q)
            sQ = P.op("scalar", lambda e, bank=bank: e.copy(out=q16, in_=bank[:]),
                      inc=("mS", 1))
            qb_free[id(bank)] = sQ
            P.wait("vector", "mS", sQ)
            rope_tile(q16, i)
            vq = P.op("vector", lambda e: e.tensor_copy(
                out=tmp1[:, 1:2], in_=tmp1[:, 1:2]), inc=("mV", 1))
            P.wait("gpsimd", "mV", vq)
            st_q = P.dma("gpsimd", q_h[i * 128:(i + 1) * 128, :], q16, "qSP")
        P.wait("sync", "qSP", P.snap("qSP"))
        for c in range(KC):
            P.dma("sync", arB[:, c * T:(c + 1) * T], q_h[:, c * 128:(c + 1) * 128],
                  "qTR", transpose=True)
        trsQ = P.snap("qTR")

        par("attkv")
        P.wait("sync", "mT", P.snap("mT"))
        lwk = P.dma("sync", wbf[:, 0:KC * 1024].rearrange("p (c x) -> p c x", c=KC),
                    wat16[1].rearrange("(c p) x -> p c x", p=128), "qW")
        lv1 = P.dma("sync", stA[:, 0:4096].rearrange("p (c x) -> p c x", c=4),
                    wat16[2].rearrange("(c p) x -> c p x", p=128)[0:4]
                    .rearrange("c p x -> p c x"), "qW")
        lv2 = P.dma("sync", stB[:, 0:4096].rearrange("p (c x) -> p c x", c=4),
                    wat16[2].rearrange("(c p) x -> c p x", p=128)[4:8]
                    .rearrange("c p x -> p c x"), "qW")
        P.wait("tensor", "qW", lv2)
        k16 = rpb[:, 0:D]
        v16 = rpb[:, D:2 * D]
        kv_free = None
        m1_tile = None
        for i in range(NTT):
            if kv_free is not None:
                P.wait("tensor", "mV", kv_free)
            for c in range(KC):
                for hh in range(2):
                    P.op("tensor", lambda e, c=c, hh=hh, i=i: e.matmul(
                        pA[:, hh * 512:(hh + 1) * 512],
                        arA[:, c * T + i * 128:c * T + (i + 1) * 128],
                        wbf[:, c * 1024 + hh * 512:c * 1024 + (hh + 1) * 512],
                        start=(c == 0), stop=False))
            P.op("tensor", lambda e: e.matmul(
                pA[:, 0:512], ones16[:], brows[:, 1024:1536], start=False, stop=True))
            mk = P.op("tensor", lambda e: e.matmul(
                pA[:, 512:1024], ones16[:], brows[:, 1536:2048], start=False,
                stop=True), inc=("mT", 1))
            for c in range(KC):
                wsrc = stA if c < 4 else stB
                cc = c % 4
                for hh in range(2):
                    P.op("tensor", lambda e, c=c, cc=cc, hh=hh, i=i, wsrc=wsrc:
                         e.matmul(
                             pB[:, hh * 512:(hh + 1) * 512],
                             arA[:, c * T + i * 128:c * T + (i + 1) * 128],
                             wsrc[:, cc * 1024 + hh * 512:cc * 1024 + (hh + 1) * 512],
                             start=(c == 0), stop=False))
            P.op("tensor", lambda e: e.matmul(
                pB[:, 0:512], ones16[:], brows[:, 2048:2560], start=False, stop=True))
            mv_ = P.op("tensor", lambda e: e.matmul(
                pB[:, 512:1024], ones16[:], brows[:, 2560:3072], start=False,
                stop=True), inc=("mT", 1))
            P.wait("scalar", "mT", mv_)
            if kv_free is not None:
                P.wait("scalar", "mV", kv_free)
            if i > 0:
                P.wait("scalar", "mT", m1_tile)
            P.op("scalar", lambda e: e.copy(out=k16, in_=pA[:]), inc=("mS", 1))
            sV = P.op("scalar", lambda e: e.copy(out=v16, in_=pB[:]), inc=("mS", 1))
            P.wait("vector", "mS", sV)
            k3 = k16.rearrange("p (h x) -> p h x", h=H)
            for p5 in range(5):
                cp = ctok[:, i * 5 + p5:i * 5 + p5 + 1]
                sp = stok[:, i * 5 + p5:i * 5 + p5 + 1]
                ke = k3[:, :, 2 * p5:2 * p5 + 1]
                ko = k3[:, :, 2 * p5 + 1:2 * p5 + 2]
                t_o = rpb[:, 2 * D:2 * D + 16][:, :, None]
                t_e = rpb[:, 2 * D + 16:2 * D + 32][:, :, None]
                P.op("vector", lambda e, ko=ko, sp=sp, t_o=t_o: e.tensor_scalar_mul(
                    out=t_o, in0=ko, scalar1=sp))
                P.op("vector", lambda e, ke=ke, sp=sp, t_e=t_e: e.tensor_scalar_mul(
                    out=t_e, in0=ke, scalar1=sp))
                P.op("vector", lambda e, ke=ke, cp=cp, t_o=t_o: e.scalar_tensor_tensor(
                    out=ke, in0=ke, scalar=cp, in1=t_o, op0=ALU.mult,
                    op1=ALU.subtract))
                P.op("vector", lambda e, ko=ko, cp=cp, t_e=t_e: e.scalar_tensor_tensor(
                    out=ko, in0=ko, scalar=cp, in1=t_e, op0=ALU.mult, op1=ALU.add))
            kv_free = P.op("vector", lambda e: e.tensor_copy(
                out=tmp1[:, 0:1], in_=tmp1[:, 0:1]), inc=("mV", 1))
            P.wait("tensor", "mV", kv_free)
            for h in range(H):
                P.op("tensor", lambda e, h=h, i=i: e.matmul(
                    pE[64 * (h % 2):64 * (h % 2) + 64,
                       (h // 2) * 64:(h // 2) * 64 + 64],
                    k16[:, h * 64:(h + 1) * 64], v16[:, h * 64:(h + 1) * 64],
                    start=(i == 0 and h <= 1), stop=(i == NTT - 1),
                    skip_group_check=True),
                    inc=("mT", 1) if h == H - 1 else None)
            m1_tile = P.snap("mT")
        m1_mm = P.snap("mT")
        P.wait("scalar", "mT", m1_mm)
        sM = P.op("scalar", lambda e: e.activation(
            out=m1sb[:], in_=pE[:], func=AF.Copy, scale=0.125), inc=("mS", 1))
        if dbg and dbg_big:
            P.wait("sync", "mS", sM)
            P.dma("sync", dbg_m1, m1sb[:], "qX")

        par("attc")
        P.wait("sync", "mT", m1_mm)
        lwo = P.dma("sync", wbf[:, 0:KC * 1024].rearrange("p (c x) -> p c x", c=KC),
                    wat16[3].rearrange("(c p) x -> p c x", p=128), "qW")
        P.wait("tensor", "qW", lwo)
        P.wait("tensor", "qTR", trsQ)
        P.wait("tensor", "mS", sM)
        ctx_free = {}
        wo_free = {}
        out_prev = None
        wo_last = None
        for g in range(NG):
            for jc in range(KC):
                bank = (pC, pD)[jc % 2]
                if id(bank) in ctx_free:
                    P.wait("tensor", "mS", ctx_free[id(bank)])
                if g > 0 and jc == 0:
                    P.wait("scalar", "mT", wo_last)
                mm = None
                for sub in range(2):
                    h = jc * 2 + sub
                    mm = P.op("tensor", lambda e, h=h, g=g, bank=bank: e.matmul(
                        bank[64 * (h % 2):64 * (h % 2) + 64, :],
                        m1sb[64 * (h % 2):64 * (h % 2) + 64,
                             (h // 2) * 64:(h // 2) * 64 + 64],
                        arB[64 * (h % 2):64 * (h % 2) + 64,
                            (h // 2) * T + g * 512:(h // 2) * T + (g + 1) * 512],
                        start=True, stop=True, skip_group_check=True,
                        tile_position=(64 * (h % 2), 64 * (h % 2))),
                        inc=("mT", 1) if sub == 1 else None)
                P.wait("scalar", "mT", mm)
                sE = P.op("scalar", lambda e, jc=jc, bank=bank: e.copy(
                    out=stA[:, jc * 512:(jc + 1) * 512], in_=bank[:]), inc=("mS", 1))
                ctx_free[id(bank)] = sE
            ctx_done = P.snap("mS")
            P.wait("tensor", "mS", ctx_done)
            for tt in range(TPG):
                bank = (pA, pB)[tt % 2]
                if id(bank) in wo_free:
                    P.wait("tensor", "mV", wo_free[id(bank)])
                mm = None
                for jc in range(KC):
                    for hh in range(2):
                        P.op("tensor", lambda e, jc=jc, hh=hh, tt=tt, bank=bank:
                             e.matmul(
                                 bank[:, hh * 512:(hh + 1) * 512],
                                 stA[:, jc * 512 + tt * 128:jc * 512 + (tt + 1) * 128],
                                 wbf[:, jc * 1024 + hh * 512:jc * 1024 + (hh + 1) * 512],
                                 start=(jc == 0), stop=False))
                P.op("tensor", lambda e, bank=bank: e.matmul(
                    bank[:, 0:512], ones16[:], brows[:, 3072:3584], start=False,
                    stop=True))
                mm = P.op("tensor", lambda e, bank=bank: e.matmul(
                    bank[:, 512:1024], ones16[:], brows[:, 3584:4096], start=False,
                    stop=True), inc=("mT", 1))
                wo_last = mm
                ti = g * TPG + tt
                xslot = ti % 2
                xt16 = xf[:, xslot * D:(xslot + 1) * D]
                P.wait("sync", "mV", wo_free.get(id(bank)))
                ldx = P.dma("sync", xt16, res_a[ti * 128:(ti + 1) * 128, :], "qR")
                P.wait("vector", "qR", ldx)
                P.wait("vector", "mT", mm)
                P.wait("vector", "qO", out_prev)
                ov = P.op("vector", lambda e, bank=bank, xt16=xt16: e.tensor_tensor(
                    out=tH1[:, 0:D], in0=bank[:], in1=xt16, op=ALU.add),
                    inc=("mV", 1))
                wo_free[id(bank)] = ov
                P.wait("sync", "mV", ov)
                out_prev = P.dma("sync", res_b[ti * 128:(ti + 1) * 128, :],
                                 tH1[:, 0:D], "qO")
        P.wait_all("sync", "qO")

    def pre_p2_ffn1():
        for t in range(NB):
            upgate_tile(w_up, wnat_up, wupT, upT_ready, t)

    def gate_cb_ffn1(nb):
        down_tile(nb)
        if nb == NB - 1:
            prep_flush()

    ffn(x_in, F32, res_a, F16,
        pre_p2=pre_p2_ffn1,
        up_cb=lambda nb: upgate_tile(w_gate, wnat_g, wgT, gT_ready, nb),
        gate_cb=gate_cb_ffn1)
    if phases >= 2:
        attention()
    if phases >= 3:
        ffn(res_b, F16, x_out, F32)

    if dbg:
        vdb = P.op("vector", lambda e: e.tensor_copy(out=tmp1[:, 0:1],
                                                     in_=tmp1[:, 0:1]), inc=("mV", 1))
        P.wait("sync", "mV", vdb)
        P.dma("sync", dbg_scl[:, 0:8], scl[:, 0:8], "qO")
        P.wait_all("sync", "qSP")
        P.wait_all("sync", "qB")
        if dbg_big:
            P.dma("sync", dbg_ra, res_a, "qO")
            P.dma("sync", dbg_rb, res_b, "qO")
            P.dma("sync", dbg_wu.rearrange("(a c p x) -> a c p x", a=NB, c=KC,
                                           p=128), wupT, "qO")

    if dump_src is not None:
        for s in ("qO", "qSP", "qB", "qSL", "qX", "qLN", "qTR", "qW", "qR",
                  "qXG", "qRG", "qW2", "qN", "qBk"):
            P.wait_all("sync", s)
        P.wait("sync", "mS", P.snap("mS"))
        P.wait("sync", "mV", P.snap("mV"))
        P.wait("sync", "mT", P.snap("mT"))
        srcs = {"res_a": res_a, "res_b": res_b, "ln_h": ln_h, "q_h": q_h}
        xo16 = x_out.bitcast(F16)
        if dump_src in srcs:
            P.dma("sync", xo16[0:T, 0:D], srcs[dump_src], "qO")
        elif dump_src.startswith("sp"):
            t_ = {"spU0": spU, "spG0": spG, "spT0": spT, "spZ0": spZ}[dump_src]
            P.dma("sync", xo16[0:128, 0:T], t_[0], "qO")
        elif dump_src == "scl":
            P.dma("sync", x_out[0:128, 0:8], scl[:, 0:8], "qO")

    for e in ENGS:
        P.wait_all(e, "qO")
        P.wait_all(e, "qSP")
        P.wait_all(e, "qB")
        P.wait_all(e, "qW2")
        P.wait_all(e, "qN")
        P.wait_all(e, "qBk")

    P.build()
    stk.close()
    return nc


def _rope_tables(T, hd=64):
    scale = np.arange(0, hd, 2, dtype=np.float32)[: hd // 2]
    with np.errstate(over="ignore"):
        denom = np.float32(10000.0) ** scale
    freqs = (np.float32(1.0) / denom).astype(np.float32)
    ang = (np.arange(T, dtype=np.float32)[:, None] * freqs[None, :]).astype(np.float32)
    return np.cos(ang).astype(np.float32), np.sin(ang).astype(np.float32)


def make_inputs(x_core, Wq, bq, Wk, bk, Wv, bv, Wo, bo, w_gate, w_up, w_down, T):
    NTT = T // 128
    cos, sin = _rope_tables(T)
    cos5, sin5 = cos[:, :5], sin[:, :5]
    costok = cos5.reshape(NTT, 128, 5).transpose(1, 0, 2).reshape(128, NTT * 5)
    sintok = sin5.reshape(NTT, 128, 5).transpose(1, 0, 2).reshape(128, NTT * 5)
    return {
        "x_in": np.ascontiguousarray(x_core, dtype=np.float32),
        "w_q": np.ascontiguousarray(Wq, dtype=np.float32),
        "w_k": np.ascontiguousarray(Wk, dtype=np.float32),
        "w_v": np.ascontiguousarray(Wv, dtype=np.float32),
        "w_o": np.ascontiguousarray(Wo, dtype=np.float32),
        "b_all": np.ascontiguousarray(
            np.stack([np.asarray(b, dtype=np.float32) for b in (bq, bk, bv, bo)])),
        "w_up": np.ascontiguousarray(w_up, dtype=np.float32),
        "w_gate": np.ascontiguousarray(w_gate, dtype=np.float32),
        "w_down": np.ascontiguousarray(w_down, dtype=np.float32),
        "costok": np.ascontiguousarray(costok.astype(np.float32)),
        "sintok": np.ascontiguousarray(sintok.astype(np.float32)),
    }


def kernel(L0_attn_x, L0_attn_Wq, L0_attn_bq, L0_attn_Wk, L0_attn_bk,
           L0_attn_Wv, L0_attn_bv, L0_attn_Wo, L0_attn_bo,
           L0_ffn_w_gate, L0_ffn_w_up, L0_ffn_w_down):
    x = np.ascontiguousarray(np.asarray(L0_attn_x, dtype=np.float32))
    B, T, d = x.shape
    assert B == NCORES and d == D
    nc = build_nc(T)
    in_maps = [
        make_inputs(x[i], L0_attn_Wq, L0_attn_bq, L0_attn_Wk, L0_attn_bk,
                    L0_attn_Wv, L0_attn_bv, L0_attn_Wo, L0_attn_bo,
                    L0_ffn_w_gate, L0_ffn_w_up, L0_ffn_w_down, T)
        for i in range(NCORES)
    ]
    res = run_bass_kernel_spmd(nc, in_maps, core_ids=list(range(NCORES)))
    out = np.stack([np.asarray(res.results[i]["x_out"]) for i in range(NCORES)])
    return out.astype(np.float32)


# version 06
# speedup vs baseline: 1.5623x; 1.0888x over previous
import contextlib
import numpy as np
import concourse.bass as bass
import concourse.mybir as mybir
from concourse.bass_utils import run_bass_kernel_spmd

F32 = mybir.dt.float32
F16 = mybir.dt.float16
I32 = mybir.dt.int32
I8 = mybir.dt.int8
AF = mybir.ActivationFunctionType
ALU = mybir.AluOpType
AX = mybir.AxisListType

D = 1024
DI = 3072
H = 16
KC = D // 128
NB = DI // 128
NCORES = 8
QMAX = 32767.0

ENGS = ("sync", "scalar", "vector", "tensor", "gpsimd")


class Prog:

    def __init__(self, nc, stk, paranoid=False):
        self.nc = nc
        self.ops = {e: [] for e in ENGS}
        self.sems = {}
        self.cnt = {}
        self.stk = stk
        self.paranoid = paranoid
        self.gcnt = 0

    def sem(self, *names):
        for n in names:
            self.sems[n] = self.stk.enter_context(self.nc.semaphore(n))
            self.cnt[n] = 0

    def _ser(self, eng):
        if not self.paranoid:
            return
        g = self.sems["gseq"]
        v = self.gcnt
        self.ops[eng].append(lambda e, g=g, v=v: e.wait_ge(g, v))

    def op(self, eng, fn, inc=None):
        if eng == "vector" and not self.paranoid:
            self.wait("vector", "mV", self.cnt["mV"])
            if inc is None:
                inc = ("mV", 1)
            elif inc[0] != "mV":
                raise ValueError("vector ops must inc mV")
        self._ser(eng)
        if self.paranoid:
            g = self.sems["gseq"]
            self.gcnt += 1
            if inc is None:
                self.ops[eng].append(lambda e, fn=fn, g=g: fn(e).then_inc(g, 1))
                return None
            s, by = inc
            self.cnt[s] += by
            sh = self.sems[s]

            def run2(e, fn=fn, sh=sh, by=by, g=g):
                ins = fn(e)
                ins.then_inc(sh, by)
                e.sem_inc(g, 1)
            self.ops[eng].append(run2)
            return self.cnt[s]
        if inc is None:
            self.ops[eng].append(lambda e, fn=fn: fn(e))
            return None
        s, by = inc
        self.cnt[s] += by
        sh = self.sems[s]

        def run(e, fn=fn, sh=sh, by=by):
            fn(e).then_inc(sh, by)

        self.ops[eng].append(run)
        return self.cnt[s]

    def dma(self, eng, out, in_, sem, transpose=False, **kw):
        self._ser(eng)
        self.cnt[sem] += 16
        sh = self.sems[sem]
        if transpose:
            def run(e, out=out, in_=in_, sh=sh):
                e.dma_start_transpose(out=out, in_=in_).then_inc(sh, 16)
        else:
            def run(e, out=out, in_=in_, sh=sh, kw=kw):
                e.dma_start(out=out, in_=in_, **kw).then_inc(sh, 16)
        self.ops[eng].append(run)
        if self.paranoid:
            g = self.sems["gseq"]
            v = self.cnt[sem]
            self.gcnt += 1
            self.ops[eng].append(lambda e, sh=sh, v=v, g=g: (e.wait_ge(sh, v),
                                                            e.sem_inc(g, 1)))
        return self.cnt[sem]


    def opw(self, eng, fn):
        sname = {"vector": "mV", "scalar": "mS", "gpsimd": "cc"}[eng]
        v = self.op(eng, fn, inc=(sname, 1))
        self.wait(eng, sname, v)
        return v

    def wait(self, eng, sem, val):
        if val is None or val <= 0:
            return
        sh = self.sems[sem]
        self.ops[eng].append(lambda e, sh=sh, v=val: e.wait_ge(sh, v))

    def wait_all(self, eng, sem):
        self.wait(eng, sem, self.cnt[sem])

    def snap(self, sem):
        return self.cnt[sem]

    def build(self):
        with self.nc.Block() as block:
            @block.sync
            def _(sync):
                for fn in self.ops["sync"]:
                    fn(sync)

            @block.scalar
            def _(scalar):
                for fn in self.ops["scalar"]:
                    fn(scalar)

            @block.vector
            def _(vector):
                for fn in self.ops["vector"]:
                    fn(vector)

            @block.tensor
            def _(tensor):
                for fn in self.ops["tensor"]:
                    fn(tensor)

            @block.gpsimd
            def _(gpsimd):
                for fn in self.ops["gpsimd"]:
                    fn(gpsimd)


def build_nc(T, dbg=False, phases=3, stop_after=99, p2mode="", dbg_big=False, paranoid=False, parmask="", dump_src=None):
    NTT = T // 128
    NG = T // 512
    TPG = 4
    TH = T // 2

    nc = bass.Bass("TRN2", target_bir_lowering=False, detect_race_conditions=False)
    kdbg = "ExternalOutput" if dbg else "Internal"
    kbig = "ExternalOutput" if (dbg and dbg_big) else "Internal"

    def dram(name, shape, dt, kind="Internal"):
        return nc.dram_tensor(name, shape, dt, kind=kind).ap()

    x_in = dram("x_in", [T, D], F32, "ExternalInput")
    w_q = dram("w_q", [D, D], F32, "ExternalInput")
    w_k = dram("w_k", [D, D], F32, "ExternalInput")
    w_v = dram("w_v", [D, D], F32, "ExternalInput")
    w_o = dram("w_o", [D, D], F32, "ExternalInput")
    b_all = dram("b_all", [4, D], F32, "ExternalInput")
    w_up = dram("w_up", [DI, D], F32, "ExternalInput")
    w_gate = dram("w_gate", [DI, D], F32, "ExternalInput")
    w_down = dram("w_down", [D, DI], F32, "ExternalInput")
    costok = dram("costok", [128, NTT * 5], F32, "ExternalInput")
    sintok = dram("sintok", [128, NTT * 5], F32, "ExternalInput")
    idn_h = dram("idn_h", [128, 128], F16, "ExternalInput")
    x_out = dram("x_out", [T, D], F32, "ExternalOutput")

    res_a = dram("res_a", [T, D], F16)
    res_b = dram("res_b", [T, D], F16)
    ln_h = dram("ln_h", [T, D], F16)
    spU = dram("spU", [NB, 128, T], F16)
    spG = dram("spG", [NB, 128, T], F16)
    spT = dram("spT", [NB, 128, T], F16)
    spZ = dram("spZ", [NB, 128, T], F16)
    wupT = dram("wupT", [NB, KC, 128, 128], F16)
    wgT = dram("wgT", [NB, KC, 128, 128], F16)
    wdnT = dram("wdnT", [NB, 128, D], F16)
    wat16 = dram("wat16", [4, D, D], F16)
    q_h = dram("q_h", [T, D], F16)
    dbg_arA = dram("dbg_arA", [128, KC * T], F16, "ExternalOutput") if (dbg and dbg_big) else None
    dbg_ps = dram("dbg_ps", [128, 512], F32, "ExternalOutput") if dbg else None
    dbg_m1 = dram("dbg_m1", [128, 512], F16, "ExternalOutput") if dbg else None
    dbg_acc = dram("dbg_acc", [128, 18], F32, "ExternalOutput") if dbg else None
    dbg_stA = dram("dbg_stA", [128, T], F16, "ExternalOutput") if dbg else None
    bar_i = [dram(f"bar_i{i}", [256], F32) for i in range(8)]
    bar_o = [dram(f"bar_o{i}", [256], F32) for i in range(8)]
    dbg_scl = dram("dbg_scl", [128, 8], F32, "ExternalOutput") if dbg else None
    dbg_ra = dram("dbg_ra", [T, D], F16, "ExternalOutput") if dbg else None
    dbg_rb = dram("dbg_rb", [T, D], F16, "ExternalOutput") if dbg else None
    dbg_lh = dram("dbg_lh", [T, D], F16, "ExternalOutput") if (dbg and dbg_big) else None
    dbg_qh = dram("dbg_qh", [T, D], F16, "ExternalOutput") if (dbg and dbg_big) else None
    dbg_spU = dram("dbg_spU", [NB * 128 * T], F16, "ExternalOutput") if (dbg and dbg_big) else None
    dbg_spG = dram("dbg_spG", [NB * 128 * T], F16, "ExternalOutput") if (dbg and dbg_big) else None
    dbg_spT = dram("dbg_spT", [NB * 128 * T], F16, "ExternalOutput") if (dbg and dbg_big) else None
    dbg_wu = dram("dbg_wu", [NB * KC * 128 * 128], F16, "ExternalOutput") if (dbg and dbg_big) else None

    stk = contextlib.ExitStack()

    def sb(name, shape, dt):
        return stk.enter_context(nc.sbuf_tensor(name, shape, dt))

    AR = 32768
    arA = sb("arA", [128, AR], F16)
    arB = sb("arB", [128, AR], F16)
    stA = sb("stA", [128, 4096], F16)
    stB = sb("stB", [128, 4096], F16)
    rpb = sb("rpb", [128, 4096], F16)
    wbf = sb("wbf", [128, 8192], F16)
    xf = sb("xf", [128, 2 * D], F16)
    acc = sb("acc", [128, 4], F32)
    tmp1 = sb("tmp1", [128, 8], F32)
    scl = sb("scl", [128, 8], F32)
    ctok = sb("ctok", [128, NTT * 5], F32)
    stok = sb("stok", [128, NTT * 5], F32)
    brow = sb("brow", [1, 256], F32)
    gmax = sb("gmax", [1, 8], F32)
    grow = sb("grow", [1, 8], F32)
    ones16 = sb("ones16", [1, 128], F16)
    ones32 = sb("ones32", [1, 128], F32)
    brows = sb("brows", [1, 4 * D], F16)
    m1sb = sb("m1sb", [128, 512], F16)
    sv1 = sb("sv1", [1, 4], F32)
    tI1 = sb("tI1", [128, 2048], I32)
    tI2 = sb("tI2", [128, 1024], I32)
    tF1 = sb("tF1", [128, 2048], F32)
    tH1 = sb("tH1", [128, 1024], F16)
    idn = sb("idn", [128, 128], F16)

    pA = stk.enter_context(nc.psum_tensor("pA", [128, 1024], F32))
    pB = stk.enter_context(nc.psum_tensor("pB", [128, 1024], F32))
    pC = stk.enter_context(nc.psum_tensor("pC", [128, 512], F32))
    pD = stk.enter_context(nc.psum_tensor("pD", [128, 512], F32))
    pE = stk.enter_context(nc.psum_tensor("pE", [128, 512], F32))
    pF = stk.enter_context(nc.psum_tensor("pF", [128, 512], F32))

    P = Prog(nc, stk, paranoid=paranoid)
    PM = set(parmask.split(",")) if parmask else set()

    def par(tag):
        if not paranoid:
            P.paranoid = (tag in PM) or ("all" in PM)
    P.sem("gseq","qW", "qLN", "qTR", "qSP", "qSL", "qR", "qO", "qB", "qX",
          "qXG", "qRG", "qW2", "qN", "qBk",
          "mS", "mV", "mT", "cc")

    rpbF32 = rpb.ap().bitcast(F32)
    tI2i8 = tI2.ap().bitcast(I8)
    tI2f = tI2.ap().bitcast(F32)

    par("prep1")
    P.opw("vector", lambda e: e.memset(ones16[:], 1.0))
    P.opw("vector", lambda e: e.memset(ones32[:], 1.0))
    P.opw("vector", lambda e: e.memset(scl[:], 0.0))
    P.opw("vector", lambda e: e.memset(tmp1[:], 0.0))
    P.opw("vector", lambda e: e.memset(gmax[:], 0.0))
    P.opw("vector", lambda e: e.memset(grow[:], 0.0))
    P.opw("vector", lambda e: e.memset(brow[:], 0.0))

    for wi, wsrc in enumerate((w_q, w_k, w_v, w_o)):
        P.dma("gpsimd", wat16[wi], wsrc, "qW2")
    at16_ld = P.snap("qW2")
    for j in range(4):
        ld = P.dma("sync", rpbF32[0:1, 0:D], b_all[j:j + 1, :], "qW")
        P.wait("scalar", "qW", ld)
        bev = P.op("scalar", lambda e, j=j: e.copy(
            out=brows[:, j * D:(j + 1) * D], in_=rpbF32[0:1, 0:D]), inc=("mS", 1))
        P.wait("sync", "mS", bev)
    P.dma("sync", ctok[:], costok, "qW")
    P.dma("sync", stok[:], sintok, "qW")
    P.dma("sync", idn[:], idn_h, "qW")
    consts_ld = P.snap("qW")

    NBL = 32
    pp = {"s": 0, "mult": {}, "cv": {}, "dq": {}, "ev": {}, "pet": {},
          "blk": {}, "pend": None}

    def _prep_flush_one(item):
        ps, blk_fn, done_cb = item
        sl = ps % 2
        bk = (pA, pB)[sl]
        stg = wbf[:, 6144 + sl * 1024:6144 + sl * 1024 + 1024]
        wbo = wbf[:, 4096 + sl * 1024:4096 + sl * 1024 + 1024]
        P.wait("tensor", "mV", pp["dq"][ps])
        P.wait("tensor", "mS", pp["ev"].get(ps - 2))
        for c8 in range(8):
            P.op("tensor", lambda e, wbo=wbo, c8=c8, bk=bk: e.matmul(
                bk[:, c8 * 128:(c8 + 1) * 128],
                wbo[:, c8 * 128:(c8 + 1) * 128], idn[:],
                start=True, stop=True),
                inc=("mT", 1) if c8 == 7 else None)
        pp["pet"][ps] = P.cnt["mT"]
        P.wait("scalar", "mT", pp["pet"][ps])
        P.wait("scalar", "qBk", pp["blk"].get(ps - 2))
        ev = P.op("scalar", lambda e, stg=stg, bk=bk: e.copy(
            out=stg, in_=bk[:]), inc=("mS", 1))
        pp["ev"][ps] = ev
        P.wait("gpsimd", "mS", ev)
        dst, src = blk_fn(stg)
        pp["blk"][ps] = P.dma("gpsimd", dst, src, "qBk")
        done_cb(pp["blk"][ps])

    def prep_emit(src_ap, blk_fn, done_cb):
        s = pp["s"]
        pp["s"] += 1
        sl = s % 2
        wdt = rpbF32[:, sl * 1024:sl * 1024 + 1024]
        wtq = tF1[:, sl * 1024:sl * 1024 + 1024]
        wqd = tI2i8[:, sl * 1024:sl * 1024 + 1024]
        wsc = tI2f[:, 512 + sl * 32:544 + sl * 32]
        wrec = tI2f[:, 576 + sl * 32:608 + sl * 32]
        wbo = wbf[:, 4096 + sl * 1024:4096 + sl * 1024 + 1024]
        P.wait("sync", "mV", pp["mult"].get(s - 2))
        ld = P.dma("sync", wdt, src_ap, "qW")
        P.wait("vector", "qW", ld)
        v3 = wdt.rearrange("p (b i) -> p b i", i=32)
        P.op("vector", lambda e, v3=v3, wsc=wsc: e.tensor_reduce(
            out=wsc, in_=v3, axis=AX.X, op=ALU.max, apply_absolute_value=True))
        P.op("vector", lambda e, wsc=wsc: e.tensor_scalar(
            out=wsc, in0=wsc, scalar1=1.0 / 7.0, scalar2=1e-12,
            op0=ALU.mult, op1=ALU.max))
        P.op("vector", lambda e, wsc=wsc, wrec=wrec: e.reciprocal(
            out=wrec, in_=wsc))
        rb = wrec[:, :, None].broadcast_to([128, NBL, 32])
        P.wait("vector", "mS", pp["cv"].get(s - 2))
        mv = P.op("vector", lambda e, v3=v3, rb=rb, wtq=wtq: e.tensor_tensor(
            out=wtq, in0=v3, in1=rb, op=ALU.mult))
        pp["mult"][s] = mv
        P.wait("scalar", "mV", mv)
        P.wait("scalar", "mV", pp["dq"].get(s - 2))
        cv = P.op("scalar", lambda e, wqd=wqd, wtq=wtq: e.copy(
            out=wqd, in_=wtq), inc=("mS", 1))
        pp["cv"][s] = cv
        P.wait("vector", "mS", cv)
        sbb = wsc[:, :, None].broadcast_to([128, NBL, 32])
        q3 = wqd.rearrange("p (b i) -> p b i", i=32)
        P.wait("vector", "mT", pp["pet"].get(s - 2))
        dq = P.op("vector", lambda e, q3=q3, sbb=sbb, wbo=wbo: e.tensor_tensor(
            out=wbo, in0=q3, in1=sbb, op=ALU.mult))
        pp["dq"][s] = dq
        prev = pp["pend"]
        pp["pend"] = (s, blk_fn, done_cb)
        if prev is not None:
            _prep_flush_one(prev)

    def prep_flush():
        prev = pp["pend"]
        pp["pend"] = None
        if prev is not None:
            _prep_flush_one(prev)

    upT_ready = [0] * NB
    gT_ready = [0] * NB
    prepst = {"dn": 0}

    def upgate_tile(wsrc, dstT, ready, t):
        src = wsrc[t * 128:(t + 1) * 128, :]

        def blk(stg, t=t):
            return (dstT[t].rearrange("c p x -> p c x"),
                    stg.rearrange("p (c x) -> p c x", c=KC))

        prep_emit(src, blk, lambda v, t=t: ready.__setitem__(t, v))

    def down_tile(s):
        qc, t = s // 8, s % 8
        src = w_down[t * 128:(t + 1) * 128, qc * 1024:(qc + 1) * 1024]

        def blk(stg, qc=qc, t=t):
            dst = wdnT[qc * 8:(qc + 1) * 8, :, t * 128:(t + 1) * 128] \
                .rearrange("j p x -> p j x")
            return dst, stg.rearrange("p (j x) -> p j x", j=8)

        def done(v):
            prepst["dn"] = v

        prep_emit(src, blk, done)

    LNO = 2048
    QSZ = min(1024, T)
    NQ4 = T // QSZ

    def ln_pass(x_src, x_dtype, need_max):
        if need_max:
            P.opw("vector", lambda e: e.memset(acc[:, 0:1], 0.0))
        prev_free = None
        prev_ve = None
        prev_st = None
        for i in range(NTT):
            slot = i % 2
            xt = xf[:, slot * D:(slot + 1) * D]
            eng_ld = "gpsimd" if x_dtype == F32 else "sync"
            qn = "qXG" if eng_ld == "gpsimd" else "qX"
            if prev_free is not None:
                P.wait(eng_ld, prev_free[0], prev_free[1])
            ld = P.dma(eng_ld, xt, x_src[i * 128:(i + 1) * 128, :], qn)
            P.wait("vector", qn, ld)
            P.wait("scalar", qn, ld)
            P.opw("vector", lambda e, xt=xt: e.reduce_sum(
                out=tmp1[:, 0:1], in_=xt, axis=AX.X))
            P.wait("scalar", "mV", prev_ve)
            sqv = P.op("scalar", lambda e, xt=xt: e.activation(
                out=tF1[:, 0:D], in_=xt, func=AF.Square), inc=("mS", 1))
            P.wait("vector", "mS", sqv)
            P.opw("vector", lambda e: e.reduce_sum(
                out=tmp1[:, 1:2], in_=tF1[:, 0:D], axis=AX.X))
            P.op("vector", lambda e: e.tensor_scalar_mul(
                out=tmp1[:, 2:3], in0=tmp1[:, 0:1], scalar1=1.0 / D))
            P.op("vector", lambda e: e.tensor_mul(
                out=tmp1[:, 3:4], in0=tmp1[:, 2:3], in1=tmp1[:, 2:3]))
            P.op("vector", lambda e: e.scalar_tensor_tensor(
                out=tmp1[:, 4:5], in0=tmp1[:, 1:2], scalar=1.0 / D,
                in1=tmp1[:, 3:4], op0=ALU.mult, op1=ALU.subtract))
            ve = P.op("vector", lambda e: e.tensor_scalar_add(
                out=tmp1[:, 4:5], in0=tmp1[:, 4:5], scalar1=1e-5), inc=("mV", 1))
            prev_ve = ve
            P.wait("scalar", "mV", ve)
            se = P.op("scalar", lambda e: e.activation(
                out=tmp1[:, 5:6], in_=tmp1[:, 4:5], func=AF.Sqrt), inc=("mS", 1))
            P.wait("vector", "mS", se)
            P.opw("vector", lambda e: e.reciprocal(out=tmp1[:, 6:7], in_=tmp1[:, 5:6]))
            vm = P.op("vector", lambda e: e.scalar_tensor_tensor(
                out=tmp1[:, 7:8], in0=tmp1[:, 2:3], scalar=-1.0,
                in1=tmp1[:, 6:7], op0=ALU.mult, op1=ALU.mult), inc=("mV", 1))
            P.wait("scalar", "mV", vm)
            P.wait("scalar", "qLN", prev_st)
            lno = P.op("scalar", lambda e, xt=xt: e.activation(
                out=arB[:, LNO:LNO + D], in_=xt, func=AF.Identity,
                scale=tmp1[:, 6:7], bias=tmp1[:, 7:8]), inc=("mS", 1))
            P.wait("sync", "mS", lno)
            prev_st = P.dma("sync", ln_h[i * 128:(i + 1) * 128, :], arB[:, LNO:LNO + D],
                            "qLN")
            if need_max:
                P.wait("vector", "mS", lno)
                P.opw("vector", lambda e: e.tensor_reduce(
                    out=tmp1[:, 0:1], in_=arB[:, LNO:LNO + D], axis=AX.X, op=ALU.max,
                    apply_absolute_value=True))
                pv = P.op("vector", lambda e: e.tensor_max(
                    out=acc[:, 0:1], in0=acc[:, 0:1], in1=tmp1[:, 0:1]), inc=("mV", 1))
                prev_free = ("mV", pv)
            else:
                prev_free = ("mS", lno)

    bar_n = [0]
    dbg1 = [True]

    def barrier(nq, post):
        bi = bar_n[0]
        bar_n[0] += 1
        n = 128 * nq
        vp = P.op("vector", lambda e, nq=nq: e.tensor_copy(
            out=tmp1[:, 0:nq], in_=acc[:, 0:nq]), inc=("mV", 1))
        P.wait("gpsimd", "mV", vp)
        d1 = P.dma("gpsimd", bar_i[bi][0:n], tmp1[:, 0:nq], "qB")
        P.wait("gpsimd", "qB", d1)
        ccv = P.op("gpsimd", lambda e, bi=bi, n=n: e.collective_compute(
            "AllReduce", ALU.max, replica_groups=[list(range(NCORES))],
            ins=[bar_i[bi][0:n].opt()], outs=[bar_o[bi][0:n].opt()]), inc=("cc", 1))
        P.wait("gpsimd", "cc", ccv)
        d2 = P.dma("gpsimd", brow[:, 0:n], bar_o[bi][0:n], "qB")
        P.wait("vector", "qB", d2)
        for q in range(nq):
            qv = brow[:, q:n:nq]
            P.opw("vector", lambda e, q=q, qv=qv: e.tensor_reduce(
                out=gmax[:, q:q + 1], in_=qv, axis=AX.X, op=ALU.max))
        m = post()
        vb = P.op("vector", lambda e: e.tensor_copy(
            out=grow[:, 7:8], in_=grow[:, 0:1]), inc=("mV", 1))
        P.wait("tensor", "mV", vb)
        P.wait("tensor", "mS", P.snap("mS"))
        P.wait("tensor", "qW", consts_ld)
        tb = P.op("tensor", lambda e, m=m: e.matmul(
            pF[:, 0:m], ones32[:], grow[:, 0:m], start=True, stop=True),
            inc=("mT", 1))
        P.wait("scalar", "mT", tb)
        sc = P.op("scalar", lambda e, m=m: e.copy(out=scl[:, 0:m], in_=pF[:, 0:m]),
                  inc=("mS", 1))
        return sc

    def ffn(x_src, x_dtype, out_dst, out_dtype,
            pre_p2=None, up_cb=None, gate_cb=None):
        if stop_after < 1:
            return
        par("ln")
        ln_pass(x_src, x_dtype, True)
        par("bar")
        if stop_after < 2:
            return

        def post1():
            P.op("vector", lambda e: e.tensor_scalar(
                out=gmax[:, 1:2], in0=gmax[:, 0:1], scalar1=1.0 / QMAX,
                scalar2=1e-12, op0=ALU.mult, op1=ALU.max))
            P.opw("vector", lambda e: e.reciprocal(out=grow[:, 0:1], in_=gmax[:, 1:2]))
            P.op("vector", lambda e: e.tensor_copy(out=grow[:, 1:2], in_=gmax[:, 1:2]))
            return 2
        b1 = barrier(1, post1)
        if pre_p2 is not None:
            pre_p2()
        if stop_after < 3:
            return

        par("tr")
        P.wait_all("sync", "qLN")
        for c in range(KC):
            P.dma("sync", arA[:, c * T:(c + 1) * T], ln_h[:, c * 128:(c + 1) * 128],
                  "qTR", transpose=True)
        trs = P.snap("qTR")

        P.wait("scalar", "qTR", trs)
        P.wait("scalar", "mS", b1)
        for c in range(KC):
            for q in range(NQ4):
                o0 = c * T + q * QSZ
                P.op("scalar", lambda e, o0=o0: e.activation(
                    out=tI1[:, 0:QSZ], in_=arA[:, o0:o0 + QSZ], func=AF.Identity,
                    scale=scl[:, 0:1]), inc=("mS", 1))
                P.op("scalar", lambda e, o0=o0: e.activation(
                    out=arA[:, o0:o0 + QSZ], in_=tI1[:, 0:QSZ], func=AF.Identity,
                    scale=scl[:, 1:2]), inc=("mS", 1))
        xq_done = P.snap("mS")
        if stop_after < 4:
            return
        if dbg and dbg_big:
            P.wait("sync", "mS", xq_done)
            P.dma("sync", dbg_arA[:, 0:KC * T], arA[:, 0:KC * T], "qX") if dbg_big else None

        par("p2")
        P.opw("vector", lambda e: e.memset(acc[:, 0:2], 0.0))
        P.wait("tensor", "mS", xq_done)
        bank_free = {}
        slot_mm = [None, None]

        def p2_pass(ready, wT, stg, spDst, accc, prep_cb):
            st_prev = None
            sE = None
            for nb in range(NB):
                if prep_cb is not None:
                    prep_cb(nb)
                wslot = nb % 2
                w_l = wbf[:, wslot * 1024:(wslot + 1) * 1024]
                P.wait("sync", "mT", slot_mm[wslot])
                P.wait("sync", "qBk", ready[nb])
                lw_ = P.dma("sync", w_l.rearrange("p (c x) -> p c x", c=KC),
                            wT[nb].rearrange("c p x -> p c x"), "qW")
                P.wait("tensor", "qW", lw_)
                for g in range(NG):
                    bank = (pC, pD, pE, pF)[g % 4]
                    if id(bank) in bank_free:
                        sH, vH = bank_free[id(bank)]
                        P.wait("tensor", "mS", sH)
                        P.wait("tensor", "mV", vH)
                    for c in range(KC):
                        P.op("tensor", lambda e, c=c, bank=bank, w_l=w_l, g=g:
                             e.matmul(
                                 bank[:], w_l[:, c * 128:(c + 1) * 128],
                                 arA[:, c * T + g * 512:c * T + g * 512 + 512],
                                 start=(c == 0), stop=(c == KC - 1)),
                             inc=("mT", 1) if c == KC - 1 else None)
                    mTv = P.snap("mT")
                    if g == NG - 1:
                        slot_mm[wslot] = mTv
                    P.wait("scalar", "mT", mTv)
                    if g == 0:
                        P.wait("scalar", "qSP", st_prev)
                    sE = P.op("scalar", lambda e, bank=bank, stg=stg, g=g: e.copy(
                        out=stg[:, g * 512:(g + 1) * 512], in_=bank[:]),
                        inc=("mS", 1))
                    P.wait("vector", "mS", sE)
                    P.opw("vector", lambda e, stg=stg, g=g: e.tensor_reduce(
                        out=tmp1[:, 0:1], in_=stg[:, g * 512:(g + 1) * 512],
                        axis=AX.X, op=ALU.max, apply_absolute_value=True))
                    vE = P.op("vector", lambda e, accc=accc: e.tensor_max(
                        out=acc[:, accc:accc + 1], in0=acc[:, accc:accc + 1],
                        in1=tmp1[:, 0:1]), inc=("mV", 1))
                    bank_free[id(bank)] = (sE, vE)
                P.wait("gpsimd", "mS", sE)
                st_prev = P.dma("gpsimd", spDst[nb], stg[:, 0:T], "qSP")

        p2_pass(upT_ready, wupT, stA, spU, 0, up_cb)
        p2_pass(gT_ready, wgT, stB, spG, 1, gate_cb)
        p2_sp = P.snap("qSP")
        if stop_after < 5:
            return

        par("bar")

        def post2():
            P.op("vector", lambda e: e.tensor_scalar(
                out=gmax[:, 2:4], in0=gmax[:, 0:2], scalar1=1.0 / QMAX,
                scalar2=1e-12, op0=ALU.mult, op1=ALU.max))
            P.opw("vector", lambda e: e.reciprocal(out=grow[:, 0:1], in_=gmax[:, 2:3]))
            P.opw("vector", lambda e: e.reciprocal(out=grow[:, 1:2], in_=gmax[:, 3:4]))
            P.op("vector", lambda e: e.tensor_copy(out=grow[:, 2:4], in_=gmax[:, 2:4]))
            P.op("vector", lambda e: e.tensor_scalar_mul(
                out=grow[:, 4:5], in0=gmax[:, 3:4], scalar1=1.0 / 65536.0))
            return 5
        b2 = barrier(2, post2)
        import os as _os
        if dbg and dbg1[0] and _os.environ.get('DBG2','1')=='1':
            dbg1[0] = False
            vd = P.op("vector", lambda e: e.tensor_copy(out=tmp1[:, 0:2],
                                                        in_=acc[:, 0:2]), inc=("mV", 1))
            P.wait("sync", "mV", vd)
            P.dma("sync", dbg_acc[:, 0:2], tmp1[:, 0:2], "qX")
            P.dma("sync", dbg_acc[0:1, 2:10], gmax[:, 0:8], "qX")
            P.dma("sync", dbg_acc[0:1, 10:18], grow[:, 0:8], "qX")
            P.wait("scalar", "mS", b2)
            P.dma("sync", dbg_stA, stA[:, 0:T], "qX")
        if stop_after < 6:
            return
        P.op("vector", lambda e: e.tensor_copy(out=sv1[:, 0:1], in_=grow[:, 4:5]))
        P.op("vector", lambda e: e.tensor_copy(out=sv1[:, 1:2], in_=grow[:, 0:1]))
        P.op("vector", lambda e: e.tensor_copy(out=sv1[:, 2:3], in_=grow[:, 2:3]))

        par("p3")
        P.opw("vector", lambda e: e.memset(acc[:, 0:1], 0.0))
        P.wait("sync", "qSP", p2_sp)
        P.wait("scalar", "mS", b2)
        st_hist = [None, None]
        ld_p = [None, None]
        vfree = {}
        i2f = None
        pend = [None]
        s4l = [None]

        def flush_s4():
            fn, pend[0] = pend[0], None
            return fn() if fn is not None else None

        qg = 0
        for nb in range(NB):
            slot = nb % 2
            gbuf = arA[:, slot * T:(slot + 1) * T]
            stg = stA[:, 0:T] if slot == 0 else rpb[:, 0:T]
            P.wait("sync", "mS", ld_p[slot])
            ldg = P.dma("sync", gbuf, spG[nb], "qSL")
            P.wait("scalar", "qSL", ldg)
            P.wait("scalar", "qSP", st_hist[slot])
            for q in range(NQ4):
                q0 = q * QSZ
                ts = qg % 2
                i1 = tI1[:, ts * 1024:ts * 1024 + QSZ]
                i2 = tI2[:, 0:QSZ]
                f1 = tF1[:, ts * 1024:ts * 1024 + QSZ]
                P.wait("scalar", "mV", vfree.get(ts))
                P.op("scalar", lambda e, gbuf=gbuf, q0=q0, i1=i1: e.activation(
                    out=i1, in_=gbuf[:, q0:q0 + QSZ], func=AF.Identity,
                    scale=scl[:, 1:2]), inc=("mS", 1))
                P.op("scalar", lambda e, i1=i1, f1=f1: e.activation(
                    out=f1, in_=i1, func=AF.Sigmoid, scale=scl[:, 3:4]), inc=("mS", 1))
                P.wait("scalar", "mV", i2f)
                c3 = P.op("scalar", lambda e, f1=f1, i2=i2: e.activation(
                    out=i2, in_=f1, func=AF.Identity, scale=65536.0), inc=("mS", 1))
                flush_s4()
                P.wait("vector", "mS", c3)
                i2f = P.op("vector", lambda e, i1=i1, i2=i2, f1=f1: e.tensor_tensor(
                    out=f1, in0=i1, in1=i2, op=ALU.mult))
                P.op("vector", lambda e, f1=f1: e.tensor_reduce(
                    out=tmp1[:, 0:1], in_=f1, axis=AX.X, op=ALU.max,
                    apply_absolute_value=True))
                vmark = P.op("vector", lambda e: e.tensor_max(
                    out=acc[:, 0:1], in0=acc[:, 0:1], in1=tmp1[:, 0:1]))

                def mk_s4(vmark=vmark, q0=q0, f1=f1, stg=stg):
                    def go():
                        P.wait("scalar", "mV", vmark)
                        return P.op("scalar", lambda e: e.activation(
                            out=stg[:, q0:q0 + QSZ], in_=f1, func=AF.Identity,
                            scale=scl[:, 4:5]), inc=("mS", 1))
                    return go
                pend[0] = mk_s4()
                vfree[ts] = vmark
                qg += 1
            s4 = flush_s4() or s4l[0]
            ld_p[slot] = P.cnt["mS"]
            P.wait("gpsimd", "mS", s4)
            st_hist[slot] = P.dma("gpsimd", spT[nb], stg, "qSP")
        p3_sp = P.snap("qSP")
        if stop_after < 7:
            return

        par("bar")

        def post3():
            P.op("vector", lambda e: e.tensor_mul(
                out=gmax[:, 1:2], in0=gmax[:, 0:1], in1=sv1[:, 0:1]))
            P.op("vector", lambda e: e.tensor_scalar(
                out=gmax[:, 2:3], in0=gmax[:, 1:2], scalar1=1.0 / QMAX,
                scalar2=1e-12, op0=ALU.mult, op1=ALU.max))
            P.opw("vector", lambda e: e.reciprocal(out=grow[:, 0:1], in_=gmax[:, 2:3]))
            P.op("vector", lambda e: e.tensor_copy(out=grow[:, 1:2], in_=gmax[:, 2:3]))
            P.op("vector", lambda e: e.tensor_copy(out=grow[:, 2:3], in_=sv1[:, 1:2]))
            P.op("vector", lambda e: e.tensor_copy(out=grow[:, 3:4], in_=sv1[:, 2:3]))
            return 4
        b3 = barrier(1, post3)
        P.op("vector", lambda e: e.tensor_copy(out=sv1[:, 3:4], in_=grow[:, 1:2]))

        par("p4")
        P.opw("vector", lambda e: e.memset(acc[:, 0:1], 0.0))
        P.wait("sync", "qSP", p3_sp)
        P.wait("scalar", "mS", b3)
        st_hist = [None, None]
        ld_p4 = [None, None]
        vfree = {}
        h1f = None
        pend = [None]
        s4l = [None]

        def flush_s4b():
            fn, pend[0] = pend[0], None
            return fn() if fn is not None else None

        qg = 0
        for nb in range(NB):
            slot = nb % 2
            tbuf = arA[:, slot * T:(slot + 1) * T]
            ubuf = arA[:, (2 + slot) * T:(3 + slot) * T]
            stg = stB[:, 0:T] if slot == 0 else arB[:, 0:T]
            P.wait("sync", "mS", ld_p4[slot])
            P.dma("sync", tbuf, spT[nb], "qSL")
            ldu = P.dma("sync", ubuf, spU[nb], "qSL")
            P.wait("scalar", "qSL", ldu)
            P.wait("scalar", "qSP", st_hist[slot])
            for q in range(NQ4):
                q0 = q * QSZ
                ts = qg % 2
                i1 = tI1[:, ts * 1024:ts * 1024 + QSZ]
                i2 = tI2[:, 0:QSZ]
                f1 = tF1[:, ts * 1024:ts * 1024 + QSZ]
                h1 = tH1[:, 0:QSZ]
                P.wait("scalar", "mV", vfree.get(ts))
                P.op("scalar", lambda e, tbuf=tbuf, q0=q0, i1=i1: e.activation(
                    out=i1, in_=tbuf[:, q0:q0 + QSZ], func=AF.Identity,
                    scale=scl[:, 0:1]), inc=("mS", 1))
                P.op("scalar", lambda e, ubuf=ubuf, q0=q0, i2=i2: e.activation(
                    out=i2, in_=ubuf[:, q0:q0 + QSZ], func=AF.Identity,
                    scale=scl[:, 2:3]), inc=("mS", 1))
                P.wait("scalar", "mV", h1f)
                c3 = P.op("scalar", lambda e, i2=i2, h1=h1: e.activation(
                    out=h1, in_=i2, func=AF.Identity, scale=scl[:, 3:4]), inc=("mS", 1))
                flush_s4b()
                P.wait("vector", "mS", c3)
                h1f = P.op("vector", lambda e, i1=i1, h1=h1, f1=f1: e.tensor_tensor(
                    out=f1, in0=i1, in1=h1, op=ALU.mult))
                P.op("vector", lambda e, f1=f1: e.tensor_reduce(
                    out=tmp1[:, 0:1], in_=f1, axis=AX.X, op=ALU.max,
                    apply_absolute_value=True))
                vmark = P.op("vector", lambda e: e.tensor_max(
                    out=acc[:, 0:1], in0=acc[:, 0:1], in1=tmp1[:, 0:1]))

                def mk_s4b(vmark=vmark, q0=q0, f1=f1, stg=stg):
                    def go():
                        P.wait("scalar", "mV", vmark)
                        return P.op("scalar", lambda e: e.activation(
                            out=stg[:, q0:q0 + QSZ], in_=f1, func=AF.Identity,
                            scale=scl[:, 1:2]), inc=("mS", 1))
                    return go
                pend[0] = mk_s4b()
                vfree[ts] = vmark
                qg += 1
            s4 = flush_s4b() or s4l[0]
            ld_p4[slot] = P.cnt["mS"]
            P.wait("gpsimd", "mS", s4)
            st_hist[slot] = P.dma("gpsimd", spZ[nb], stg, "qSP")
        p4_sp = P.snap("qSP")
        if stop_after < 8:
            return

        par("bar")

        def post4():
            P.op("vector", lambda e: e.tensor_mul(
                out=gmax[:, 1:2], in0=gmax[:, 0:1], in1=sv1[:, 3:4]))
            P.op("vector", lambda e: e.tensor_scalar(
                out=gmax[:, 2:3], in0=gmax[:, 1:2], scalar1=1.0 / QMAX,
                scalar2=1e-12, op0=ALU.mult, op1=ALU.max))
            P.opw("vector", lambda e: e.reciprocal(out=grow[:, 0:1], in_=gmax[:, 2:3]))
            P.op("vector", lambda e: e.tensor_scalar_mul(
                out=grow[:, 1:2], in0=gmax[:, 2:3], scalar1=0.5))
            return 2
        b4 = barrier(1, post4)

        par("p5")
        P.wait("sync", "qSP", p4_sp)
        P.wait("sync", "qBk", prepst["dn"])
        for nb2 in range(NB):
            P.dma("sync", arA[:, nb2 * D:nb2 * D + D], wdnT[nb2], "qW")
        wdn_ld = P.snap("qW")
        P.wait("tensor", "qW", wdn_ld)
        P.wait("scalar", "mS", b4)
        OBUF = 2048
        o32_v = tI1[:, 0:512]
        zs0 = AR - 1024
        down_free = {}
        out_prev = None
        for g in range(NG):
            ocs = []
            for nb in range(NB):
                zslot = nb % 2
                zb = arB[:, zs0 + zslot * 512:zs0 + (zslot + 1) * 512]
                P.wait("sync", "mS", ocs[nb - 2] if nb >= 2 else None)
                ldz = P.dma("sync", zb, spZ[nb][:, g * 512:(g + 1) * 512], "qSL")
                P.wait("scalar", "qSL", ldz)
                P.op("scalar", lambda e, zb=zb: e.activation(
                    out=o32_v, in_=zb, func=AF.Identity, scale=scl[:, 0:1]),
                    inc=("mS", 1))
                ocs.append(P.op("scalar", lambda e, nb=nb: e.activation(
                    out=arB[:, OBUF + nb * 512:OBUF + (nb + 1) * 512], in_=o32_v,
                    func=AF.Copy), inc=("mS", 1)))
            for tt in range(TPG):
                bank = (pA, pB)[tt % 2]
                if id(bank) in down_free:
                    P.wait("tensor", "mV", down_free[id(bank)])
                mmv = None
                for nb in range(NB):
                    P.wait("tensor", "mS", ocs[nb])
                    lhs = arB[:, OBUF + nb * 512 + tt * 128:OBUF + nb * 512 + (tt + 1) * 128]
                    for hh in range(2):
                        mmv = P.op("tensor", lambda e, bank=bank, lhs=lhs, nb=nb, hh=hh:
                                   e.matmul(bank[:, hh * 512:(hh + 1) * 512], lhs,
                                            arA[:, nb * D + hh * 512:nb * D + (hh + 1) * 512],
                                            start=(nb == 0), stop=(nb == NB - 1)),
                                   inc=("mT", 1) if (nb == NB - 1 and hh == 1) else None)
                ti = g * TPG + tt
                xslot = ti % 2
                xin_ap = xf[:, xslot * D:(xslot + 1) * D]
                eng_ld = "gpsimd" if x_dtype == F32 else "sync"
                qn = "qRG" if eng_ld == "gpsimd" else "qR"
                P.wait(eng_ld, "mV", down_free.get(id(bank)))
                ldx = P.dma(eng_ld, xin_ap, x_src[ti * 128:(ti + 1) * 128, :], qn)
                P.wait("vector", qn, ldx)
                P.wait("vector", "mT", mmv)
                P.wait("vector", "qO", out_prev)
                out_ap = tF1[:, 1024:1024 + D] if out_dtype == F32 else tH1[:, 0:D]
                ov = P.op("vector", lambda e, bank=bank, xin_ap=xin_ap, out_ap=out_ap:
                          e.scalar_tensor_tensor(
                              out=out_ap, in0=bank[:], scalar=scl[:, 1:2],
                              in1=xin_ap, op0=ALU.mult, op1=ALU.add), inc=("mV", 1))
                down_free[id(bank)] = ov
                P.wait("sync", "mV", ov)
                out_prev = P.dma("sync", out_dst[ti * 128:(ti + 1) * 128, :], out_ap,
                                 "qO")
        P.wait_all("sync", "qO")

    def attention():
        par("ln")
        ln_pass(res_a, F16, False)
        par("attq")
        P.wait_all("sync", "qLN")
        for c in range(KC):
            P.dma("sync", arA[:, c * T:(c + 1) * T], ln_h[:, c * 128:(c + 1) * 128],
                  "qTR", transpose=True)
        trs = P.snap("qTR")
        P.wait("tensor", "qTR", trs)

        def rope_fast(buf16, i, tb):
            v4 = buf16.rearrange("p (h f two) -> p h f two", two=2, f=32)
            ke = v4[:, :, 0:5, 0:1]
            ko = v4[:, :, 0:5, 1:2]

            def tmp(o):
                return rpb[:, tb + o:tb + o + 80].rearrange(
                    "p (h f) -> p h f", h=H).unsqueeze(3)
            t_o, t_e, tu = tmp(0), tmp(80), tmp(160)
            cp = ctok[:, i * 5:i * 5 + 5].unsqueeze(1).unsqueeze(3) \
                .broadcast_to([128, H, 5, 1])
            sp = stok[:, i * 5:i * 5 + 5].unsqueeze(1).unsqueeze(3) \
                .broadcast_to([128, H, 5, 1])
            P.op("vector", lambda e: e.tensor_tensor(
                out=t_o, in0=ko, in1=sp, op=ALU.mult))
            P.op("vector", lambda e: e.tensor_tensor(
                out=t_e, in0=ke, in1=sp, op=ALU.mult))
            P.op("vector", lambda e: e.tensor_tensor(
                out=tu, in0=ke, in1=cp, op=ALU.mult))
            P.op("vector", lambda e: e.tensor_tensor(
                out=ke, in0=tu, in1=t_o, op=ALU.subtract))
            P.op("vector", lambda e: e.tensor_tensor(
                out=tu, in0=ko, in1=cp, op=ALU.mult))
            P.op("vector", lambda e: e.tensor_tensor(
                out=ko, in0=tu, in1=t_e, op=ALU.add))

        def rope_tile(buf16, i):
            rope_fast(buf16, i, 3 * D)

        wq_l = wbf[:, 0:KC * 1024]
        P.wait("sync", "qW2", at16_ld)
        lw = P.dma("sync", wq_l.rearrange("p (c x) -> p c x", c=KC),
                   wat16[0].rearrange("(c p) x -> p c x", p=128), "qW")
        P.wait("tensor", "qW", lw)
        q16 = rpb[:, 2 * D:3 * D]
        qb_free = {}
        vq = None
        st_q = None
        for i in range(NTT):
            bank = (pA, pB)[i % 2]
            if id(bank) in qb_free:
                P.wait("tensor", "mS", qb_free[id(bank)])
            for c in range(KC):
                for hh in range(2):
                    P.op("tensor", lambda e, c=c, hh=hh, i=i, bank=bank: e.matmul(
                        bank[:, hh * 512:(hh + 1) * 512],
                        arA[:, c * T + i * 128:c * T + (i + 1) * 128],
                        wq_l[:, c * 1024 + hh * 512:c * 1024 + (hh + 1) * 512],
                        start=(c == 0), stop=False))
            P.op("tensor", lambda e, bank=bank: e.matmul(
                bank[:, 0:512], ones16[:], brows[:, 0:512], start=False, stop=True))
            qmm = P.op("tensor", lambda e, bank=bank: e.matmul(
                bank[:, 512:1024], ones16[:], brows[:, 512:1024], start=False,
                stop=True), inc=("mT", 1))
            P.wait("scalar", "mT", qmm)
            P.wait("scalar", "mV", vq)
            P.wait("scalar", "qSP", st_q)
            sQ = P.op("scalar", lambda e, bank=bank: e.copy(out=q16, in_=bank[:]),
                      inc=("mS", 1))
            qb_free[id(bank)] = sQ
            P.wait("vector", "mS", sQ)
            rope_tile(q16, i)
            vq = P.op("vector", lambda e: e.tensor_copy(
                out=tmp1[:, 1:2], in_=tmp1[:, 1:2]), inc=("mV", 1))
            P.wait("gpsimd", "mV", vq)
            st_q = P.dma("gpsimd", q_h[i * 128:(i + 1) * 128, :], q16, "qSP")
        P.wait("sync", "qSP", P.snap("qSP"))
        for c in range(KC):
            P.dma("sync", arB[:, c * T:(c + 1) * T], q_h[:, c * 128:(c + 1) * 128],
                  "qTR", transpose=True)
        trsQ = P.snap("qTR")

        par("attkv")
        P.wait("sync", "mT", P.snap("mT"))
        lwk = P.dma("sync", wbf[:, 0:KC * 1024].rearrange("p (c x) -> p c x", c=KC),
                    wat16[1].rearrange("(c p) x -> p c x", p=128), "qW")
        lv1 = P.dma("sync", stA[:, 0:4096].rearrange("p (c x) -> p c x", c=4),
                    wat16[2].rearrange("(c p) x -> c p x", p=128)[0:4]
                    .rearrange("c p x -> p c x"), "qW")
        lv2 = P.dma("sync", stB[:, 0:4096].rearrange("p (c x) -> p c x", c=4),
                    wat16[2].rearrange("(c p) x -> c p x", p=128)[4:8]
                    .rearrange("c p x -> p c x"), "qW")
        P.wait("tensor", "qW", lv2)
        k16 = rpb[:, 0:D]
        v16 = rpb[:, D:2 * D]
        kv_free = None
        m1_tile = None
        for i in range(NTT):
            if kv_free is not None:
                P.wait("tensor", "mV", kv_free)
            for c in range(KC):
                for hh in range(2):
                    P.op("tensor", lambda e, c=c, hh=hh, i=i: e.matmul(
                        pA[:, hh * 512:(hh + 1) * 512],
                        arA[:, c * T + i * 128:c * T + (i + 1) * 128],
                        wbf[:, c * 1024 + hh * 512:c * 1024 + (hh + 1) * 512],
                        start=(c == 0), stop=False))
            P.op("tensor", lambda e: e.matmul(
                pA[:, 0:512], ones16[:], brows[:, 1024:1536], start=False, stop=True))
            mk = P.op("tensor", lambda e: e.matmul(
                pA[:, 512:1024], ones16[:], brows[:, 1536:2048], start=False,
                stop=True), inc=("mT", 1))
            for c in range(KC):
                wsrc = stA if c < 4 else stB
                cc = c % 4
                for hh in range(2):
                    P.op("tensor", lambda e, c=c, cc=cc, hh=hh, i=i, wsrc=wsrc:
                         e.matmul(
                             pB[:, hh * 512:(hh + 1) * 512],
                             arA[:, c * T + i * 128:c * T + (i + 1) * 128],
                             wsrc[:, cc * 1024 + hh * 512:cc * 1024 + (hh + 1) * 512],
                             start=(c == 0), stop=False))
            P.op("tensor", lambda e: e.matmul(
                pB[:, 0:512], ones16[:], brows[:, 2048:2560], start=False, stop=True))
            mv_ = P.op("tensor", lambda e: e.matmul(
                pB[:, 512:1024], ones16[:], brows[:, 2560:3072], start=False,
                stop=True), inc=("mT", 1))
            P.wait("scalar", "mT", mv_)
            if kv_free is not None:
                P.wait("scalar", "mV", kv_free)
            if i > 0:
                P.wait("scalar", "mT", m1_tile)
            P.op("scalar", lambda e: e.copy(out=k16, in_=pA[:]), inc=("mS", 1))
            sV = P.op("scalar", lambda e: e.copy(out=v16, in_=pB[:]), inc=("mS", 1))
            P.wait("vector", "mS", sV)
            rope_fast(k16, i, 2 * D)
            kv_free = P.op("vector", lambda e: e.tensor_copy(
                out=tmp1[:, 0:1], in_=tmp1[:, 0:1]), inc=("mV", 1))
            P.wait("tensor", "mV", kv_free)
            for h in range(H):
                P.op("tensor", lambda e, h=h, i=i: e.matmul(
                    pE[64 * (h % 2):64 * (h % 2) + 64,
                       (h // 2) * 64:(h // 2) * 64 + 64],
                    k16[:, h * 64:(h + 1) * 64], v16[:, h * 64:(h + 1) * 64],
                    start=(i == 0 and h <= 1), stop=(i == NTT - 1),
                    skip_group_check=True),
                    inc=("mT", 1) if h == H - 1 else None)
            m1_tile = P.snap("mT")
        m1_mm = P.snap("mT")
        P.wait("scalar", "mT", m1_mm)
        sM = P.op("scalar", lambda e: e.activation(
            out=m1sb[:], in_=pE[:], func=AF.Copy, scale=0.125), inc=("mS", 1))
        if dbg and dbg_big:
            P.wait("sync", "mS", sM)
            P.dma("sync", dbg_m1, m1sb[:], "qX")

        par("attc")
        P.wait("sync", "mT", m1_mm)
        lwo = P.dma("sync", wbf[:, 0:KC * 1024].rearrange("p (c x) -> p c x", c=KC),
                    wat16[3].rearrange("(c p) x -> p c x", p=128), "qW")
        P.wait("tensor", "qW", lwo)
        P.wait("tensor", "qTR", trsQ)
        P.wait("tensor", "mS", sM)
        ctx_free = {}
        wo_free = {}
        out_prev = None
        wo_last = None
        for g in range(NG):
            for jc in range(KC):
                bank = (pC, pD)[jc % 2]
                if id(bank) in ctx_free:
                    P.wait("tensor", "mS", ctx_free[id(bank)])
                if g > 0 and jc == 0:
                    P.wait("scalar", "mT", wo_last)
                mm = None
                for sub in range(2):
                    h = jc * 2 + sub
                    mm = P.op("tensor", lambda e, h=h, g=g, bank=bank: e.matmul(
                        bank[64 * (h % 2):64 * (h % 2) + 64, :],
                        m1sb[64 * (h % 2):64 * (h % 2) + 64,
                             (h // 2) * 64:(h // 2) * 64 + 64],
                        arB[64 * (h % 2):64 * (h % 2) + 64,
                            (h // 2) * T + g * 512:(h // 2) * T + (g + 1) * 512],
                        start=True, stop=True, skip_group_check=True,
                        tile_position=(64 * (h % 2), 64 * (h % 2))),
                        inc=("mT", 1) if sub == 1 else None)
                P.wait("scalar", "mT", mm)
                sE = P.op("scalar", lambda e, jc=jc, bank=bank: e.copy(
                    out=stA[:, jc * 512:(jc + 1) * 512], in_=bank[:]), inc=("mS", 1))
                ctx_free[id(bank)] = sE
            ctx_done = P.snap("mS")
            P.wait("tensor", "mS", ctx_done)
            for tt in range(TPG):
                bank = (pA, pB)[tt % 2]
                if id(bank) in wo_free:
                    P.wait("tensor", "mV", wo_free[id(bank)])
                mm = None
                for jc in range(KC):
                    for hh in range(2):
                        P.op("tensor", lambda e, jc=jc, hh=hh, tt=tt, bank=bank:
                             e.matmul(
                                 bank[:, hh * 512:(hh + 1) * 512],
                                 stA[:, jc * 512 + tt * 128:jc * 512 + (tt + 1) * 128],
                                 wbf[:, jc * 1024 + hh * 512:jc * 1024 + (hh + 1) * 512],
                                 start=(jc == 0), stop=False))
                P.op("tensor", lambda e, bank=bank: e.matmul(
                    bank[:, 0:512], ones16[:], brows[:, 3072:3584], start=False,
                    stop=True))
                mm = P.op("tensor", lambda e, bank=bank: e.matmul(
                    bank[:, 512:1024], ones16[:], brows[:, 3584:4096], start=False,
                    stop=True), inc=("mT", 1))
                wo_last = mm
                ti = g * TPG + tt
                xslot = ti % 2
                xt16 = xf[:, xslot * D:(xslot + 1) * D]
                P.wait("sync", "mV", wo_free.get(id(bank)))
                ldx = P.dma("sync", xt16, res_a[ti * 128:(ti + 1) * 128, :], "qR")
                P.wait("vector", "qR", ldx)
                P.wait("vector", "mT", mm)
                P.wait("vector", "qO", out_prev)
                ov = P.op("vector", lambda e, bank=bank, xt16=xt16: e.tensor_tensor(
                    out=tH1[:, 0:D], in0=bank[:], in1=xt16, op=ALU.add),
                    inc=("mV", 1))
                wo_free[id(bank)] = ov
                P.wait("sync", "mV", ov)
                out_prev = P.dma("sync", res_b[ti * 128:(ti + 1) * 128, :],
                                 tH1[:, 0:D], "qO")
        P.wait_all("sync", "qO")

    def pre_p2_ffn1():
        for t in range(4):
            upgate_tile(w_up, wupT, upT_ready, t)

    def up_cb_ffn1(nb):
        if nb + 4 < NB:
            upgate_tile(w_up, wupT, upT_ready, nb + 4)
        upgate_tile(w_gate, wgT, gT_ready, nb)

    def gate_cb_ffn1(nb):
        down_tile(nb)
        if nb == NB - 1:
            prep_flush()

    ffn(x_in, F32, res_a, F16,
        pre_p2=pre_p2_ffn1,
        up_cb=up_cb_ffn1,
        gate_cb=gate_cb_ffn1)
    if phases >= 2:
        attention()
    if phases >= 3:
        ffn(res_b, F16, x_out, F32)

    if dbg:
        vdb = P.op("vector", lambda e: e.tensor_copy(out=tmp1[:, 0:1],
                                                     in_=tmp1[:, 0:1]), inc=("mV", 1))
        P.wait("sync", "mV", vdb)
        P.dma("sync", dbg_scl[:, 0:8], scl[:, 0:8], "qO")
        P.wait_all("sync", "qSP")
        P.wait_all("sync", "qB")
        if dbg_big:
            P.dma("sync", dbg_ra, res_a, "qO")
            P.dma("sync", dbg_rb, res_b, "qO")
            P.dma("sync", dbg_wu.rearrange("(a c p x) -> a c p x", a=NB, c=KC,
                                           p=128), wupT, "qO")

    if dump_src is not None:
        for s in ("qO", "qSP", "qB", "qSL", "qX", "qLN", "qTR", "qW", "qR",
                  "qXG", "qRG", "qW2", "qN", "qBk"):
            P.wait_all("sync", s)
        P.wait("sync", "mS", P.snap("mS"))
        P.wait("sync", "mV", P.snap("mV"))
        P.wait("sync", "mT", P.snap("mT"))
        srcs = {"res_a": res_a, "res_b": res_b, "ln_h": ln_h, "q_h": q_h}
        xo16 = x_out.bitcast(F16)
        if dump_src in srcs:
            P.dma("sync", xo16[0:T, 0:D], srcs[dump_src], "qO")
        elif dump_src.startswith("sp"):
            t_ = {"spU0": spU, "spG0": spG, "spT0": spT, "spZ0": spZ}[dump_src]
            P.dma("sync", xo16[0:128, 0:T], t_[0], "qO")
        elif dump_src == "scl":
            P.dma("sync", x_out[0:128, 0:8], scl[:, 0:8], "qO")

    for e in ENGS:
        P.wait_all(e, "qO")
        P.wait_all(e, "qSP")
        P.wait_all(e, "qB")
        P.wait_all(e, "qW2")
        P.wait_all(e, "qN")
        P.wait_all(e, "qBk")

    P.build()
    stk.close()
    return nc


def _rope_tables(T, hd=64):
    scale = np.arange(0, hd, 2, dtype=np.float32)[: hd // 2]
    with np.errstate(over="ignore"):
        denom = np.float32(10000.0) ** scale
    freqs = (np.float32(1.0) / denom).astype(np.float32)
    ang = (np.arange(T, dtype=np.float32)[:, None] * freqs[None, :]).astype(np.float32)
    return np.cos(ang).astype(np.float32), np.sin(ang).astype(np.float32)


def make_inputs(x_core, Wq, bq, Wk, bk, Wv, bv, Wo, bo, w_gate, w_up, w_down, T):
    NTT = T // 128
    cos, sin = _rope_tables(T)
    cos5, sin5 = cos[:, :5], sin[:, :5]
    costok = cos5.reshape(NTT, 128, 5).transpose(1, 0, 2).reshape(128, NTT * 5)
    sintok = sin5.reshape(NTT, 128, 5).transpose(1, 0, 2).reshape(128, NTT * 5)
    return {
        "x_in": np.ascontiguousarray(x_core, dtype=np.float32),
        "w_q": np.ascontiguousarray(Wq, dtype=np.float32),
        "w_k": np.ascontiguousarray(Wk, dtype=np.float32),
        "w_v": np.ascontiguousarray(Wv, dtype=np.float32),
        "w_o": np.ascontiguousarray(Wo, dtype=np.float32),
        "b_all": np.ascontiguousarray(
            np.stack([np.asarray(b, dtype=np.float32) for b in (bq, bk, bv, bo)])),
        "w_up": np.ascontiguousarray(w_up, dtype=np.float32),
        "w_gate": np.ascontiguousarray(w_gate, dtype=np.float32),
        "w_down": np.ascontiguousarray(w_down, dtype=np.float32),
        "costok": np.ascontiguousarray(costok.astype(np.float32)),
        "sintok": np.ascontiguousarray(sintok.astype(np.float32)),
        "idn_h": np.eye(128, dtype=np.float16),
    }


def kernel(L0_attn_x, L0_attn_Wq, L0_attn_bq, L0_attn_Wk, L0_attn_bk,
           L0_attn_Wv, L0_attn_bv, L0_attn_Wo, L0_attn_bo,
           L0_ffn_w_gate, L0_ffn_w_up, L0_ffn_w_down):
    x = np.ascontiguousarray(np.asarray(L0_attn_x, dtype=np.float32))
    B, T, d = x.shape
    assert B == NCORES and d == D
    nc = build_nc(T)
    in_maps = [
        make_inputs(x[i], L0_attn_Wq, L0_attn_bq, L0_attn_Wk, L0_attn_bk,
                    L0_attn_Wv, L0_attn_bv, L0_attn_Wo, L0_attn_bo,
                    L0_ffn_w_gate, L0_ffn_w_up, L0_ffn_w_down, T)
        for i in range(NCORES)
    ]
    res = run_bass_kernel_spmd(nc, in_maps, core_ids=list(range(NCORES)))
    out = np.stack([np.asarray(res.results[i]["x_out"]) for i in range(NCORES)])
    return out.astype(np.float32)
